# version 05
# baseline (speedup 1.0000x reference)
import numpy as np
from contextlib import ExitStack
import concourse.bass as bass
import concourse.mybir as mybir
from concourse.bass_utils import run_bass_kernel_spmd

F32 = mybir.dt.float32
BF16 = mybir.dt.bfloat16
I32 = mybir.dt.int32
AF = mybir.ActivationFunctionType
ALU = mybir.AluOpType
AX = mybir.AxisListType

D = 1024
NFC = D // 128
TCH = 512

ENGINES = ("sync", "tensor", "vector", "scalar", "gpsimd")


class SemC:
    def __init__(self, h, name):
        self.h = h
        self.name = name
        self.v = 0

    def bump(self, n):
        self.v += n
        return self.v


def _ap_span(a):
    try:
        t = a.tensor
        name = t.name
        off = int(a.offset)
        ext = 1
        for d in a.ap:
            ext += int(d[0]) * (int(d[1]) - 1)
        return (name, off, off + ext)
    except Exception:
        return None


def _spans(args):
    out = []
    for a in args:
        s = _ap_span(a)
        if s:
            out.append(s)
    return out


def _overlap(s1, s2):
    return s1[0] == s2[0] and s1[1] < s2[2] and s2[1] < s1[2]


class Qe:

    GUARDED = ("vector", "scalar", "gpsimd")
    WINDOW = 48

    def __init__(self, b, eng):
        self.b = b
        self.eng = eng
        self.window = []
        self.cells = []
        self.next_unassigned = 0
        self.last_cell = None

    def _app(self, fn):
        self.b.q[self.eng].append(fn)

    def _ensure_inc(self, idx):
        g = self.b.sems["g" + self.eng]
        while self.next_unassigned <= idx:
            c = self.cells[self.next_unassigned]
            if c["sem"] is None:
                g.bump(1)
                c["sem"] = g
                c["val"] = g.v
            self.next_unassigned += 1
        c = self.cells[idx]
        return (c["sem"], c["val"])

    def op(self, opname, *args, inc=None, incn=1, **kw):
        guarded = self.eng in self.GUARDED
        outs = ins = None
        if guarded:
            outs = _spans(args[:1]) + _spans(
                [kw[k] for k in ("accum_out",) if kw.get(k) is not None])
            ins = _spans(args[1:]) + _spans(
                [kw[k] for k in ("scale", "bias") if kw.get(k) is not None])
            for (pidx, pcell, pouts, pins) in self.window:
                dep = any(_overlap(s, po) for po in pouts for s in (outs + ins)) or \
                    any(_overlap(o, pi) for pi in pins for o in outs)
                if dep:
                    self.wait(*self._ensure_inc(pidx))
        if inc is not None:
            inc.bump(incn)
        cell = {"sem": None, "val": 0}

        def run(e, opname=opname, args=args, kw=kw, inc=inc, incn=incn, cell=cell):
            inst = getattr(e, opname)(*args, **kw)
            if inc is not None:
                inst.then_inc(inc.h, incn)
            elif cell["sem"] is not None:
                inst.then_inc(cell["sem"].h, 1)
        self._app(run)
        if inc is not None:
            cell["sem"] = inc
            cell["val"] = inc.v
        self.last_cell = cell
        self.cells.append(cell)
        if guarded:
            self.window.append((len(self.cells) - 1, cell, outs, ins))
            if len(self.window) > self.WINDOW:
                self.window.pop(0)
        return inc.v if inc is not None else None

    def wait(self, sem, val):
        if val <= 0:
            return
        hw = self.b.waited[self.eng]
        if hw.get(sem.name, -1) >= val:
            return
        hw[sem.name] = val

        def run(e, sem=sem, val=val):
            e.wait_ge(sem.h, val)
        self._app(run)

    def milestone(self, sem):
        if not self.cells:
            return (sem, 0)
        return self._ensure_inc(len(self.cells) - 1)



class Builder:
    def __init__(self, nc, es, T):
        self.nc = nc
        self.es = es
        self.T = T
        self.NTC = T // TCH
        self.q = {e: [] for e in ENGINES}
        self.waited = {e: {} for e in ENGINES}
        self.sems = {}
        self.dma_sems = []
        for name in ["ts", "vs", "ss",
                     "gvector", "gscalar", "gtensor", "ggpsimd", "gsync"]:
            h = es.enter_context(nc.semaphore(name))
            self.sems[name] = SemC(h, name)
        self.sync = Qe(self, "sync")
        self.tensor = Qe(self, "tensor")
        self.vector = Qe(self, "vector")
        self.scalar = Qe(self, "scalar")
        self.gpsimd = Qe(self, "gpsimd")
        self.ts = self.sems["ts"]
        self.vs = self.sems["vs"]
        self.ss = self.sems["ss"]

    def dma_sem(self, name):
        if name not in self.sems:
            h = self.es.enter_context(self.nc.semaphore(name))
            self.sems[name] = SemC(h, name)
            self.dma_sems.append(self.sems[name])
        return self.sems[name]

    def sb(self, name, shp, dt=F32):
        if name in getattr(self, "_sb_names", set()):
            i = 2
            while f"{name}_{i}" in self._sb_names:
                i += 1
            name = f"{name}_{i}"
        if not hasattr(self, "_sb_names"):
            self._sb_names = set()
        self._sb_names.add(name)
        return self.es.enter_context(self.nc.sbuf_tensor(name, shp, dt))

    def ps(self, name, shp, dt=F32):
        return self.es.enter_context(self.nc.psum_tensor(name, shp, dt))

    def dram(self, name, shp, dt=F32, kind="ExternalInput"):
        return self.nc.dram_tensor(name, shp, dt, kind=kind).ap()

    def load(self, dst, src, sem):
        return self.sync.op("dma_start", dst, src, inc=sem, incn=16)

    def store(self, dst, src, sem):
        return self.sync.op("dma_start", dst, src, inc=sem, incn=16)

    def dma_fence(self):
        for s in self.dma_sems:
            self.sync.wait(s, s.v)

    def finish(self):
        for s in self.dma_sems:
            self.sync.wait(s, s.v)
        block = self.es.enter_context(self.nc.Block())
        for eng in ENGINES:
            ops = self.q[eng]
            if not ops:
                continue
            getattr(block, eng)(lambda e, ops=ops: [fn(e) for fn in ops])


class TW:

    def __init__(self, b, name, K, N, nmat=1):
        self.K = K
        self.NB = N // 128
        self.h = b.nc.dram_tensor(name, [nmat * self.NB * K, 128], BF16,
                                  kind="Internal")

    def __getitem__(self, mat):
        return TWView(self, mat)

    def view(self):
        return TWView(self, 0)

    def src(self, mat, col0, nk, k0=0):
        blk = mat * self.NB + col0 // 128
        off = blk * 128 * self.K + k0 * 128
        return bass.AP(self.h, off, [[self.K, 128], [1, nk * 128]])

    def dst(self, mat, k, c0, ncols):
        ncb = ncols // 128
        blk0 = mat * self.NB + c0 // 128
        off = blk0 * 128 * self.K + k * 128
        return bass.AP(self.h, off,
                       [[self.K, 128], [128 * self.K, ncb], [1, 128]])


class TWView:
    def __init__(self, tw, mat):
        self.tw = tw
        self.mat = mat


class Arena:

    def __init__(self, b, name, cols_bf):
        self.h = b.sb(name, [128, cols_bf], BF16)
        self.cols = cols_bf
        self.off = 0

    def reset(self):
        self.off = 0

    def take_bf(self, cols):
        assert self.off + cols <= self.cols, f"arena overflow {self.off}+{cols}"
        ap = self.h[:, self.off:self.off + cols]
        self.off += cols
        return ap

    def take_f32(self, cols):
        self.off = (self.off + 1) & ~1
        ap = self.h[:, self.off:self.off + 2 * cols].bitcast(F32)
        self.off += 2 * cols
        return ap

    def take_i32(self, cols):
        self.off = (self.off + 1) & ~1
        ap = self.h[:, self.off:self.off + 2 * cols].bitcast(I32)
        self.off += 2 * cols
        return ap


def emit_barrier(b):
    b.ws_next = 0
    b.xp_next = 0
    ms = [b.tensor.milestone(b.ts), b.vector.milestone(b.vs),
          b.scalar.milestone(b.ss)]
    b.dma_fence()
    for q in (b.tensor, b.vector, b.scalar, b.sync):
        for (sem, val) in ms:
            q.wait(sem, val)
        for s in b.dma_sems:
            q.wait(s, s.v)


class WStream:

    def __init__(self, b, name, nk, ocw=128, arena=None, nbuf=2):
        self.b = b
        self.nk = nk
        self.ocw = ocw
        self.nbuf = nbuf
        if arena is None:
            self.stash = b.sb(name, [128, nbuf * nk * ocw], BF16)
        else:
            self.stash = arena.take_bf(nbuf * nk * ocw)
        self.parity = 0
        self.free_ts = [0] * nbuf
        n = getattr(b, "ws_next", 0)
        b.ws_next = n + 1
        self.sems = [b.dma_sem(f"ws{n}_{i}") for i in range(nbuf)]

    def load_block(self, wsrc, col0, row0=0, ncols=None, k0=0):
        b = self.b
        nco = ncols or self.ocw
        p = self.parity
        off = p * self.nk * self.ocw
        b.sync.wait(b.ts, self.free_ts[p])
        sem = self.sems[p]
        tiles = [self.stash[:, off + k * nco: off + (k + 1) * nco]
                 for k in range(self.nk)]
        if isinstance(wsrc, TWView):
            assert row0 == 0
            ncb = nco // 128
            if ncb == 1:
                b.load(self.stash[:, off: off + self.nk * nco],
                       wsrc.tw.src(wsrc.mat, col0, self.nk, k0=k0), sem)
            else:
                dst3 = self.stash[:, off: off + self.nk * nco].rearrange(
                    "p (k cb c) -> p k (cb c)", k=self.nk, c=128)
                for cb in range(ncb):
                    b.load(dst3[:, :, cb * 128:(cb + 1) * 128],
                           wsrc.tw.src(wsrc.mat, col0 + cb * 128, self.nk),
                           sem)
        else:
            for k in range(self.nk):
                b.load(tiles[k], wsrc[row0 + k * 128: row0 + (k + 1) * 128,
                                      col0:col0 + nco], sem)
        self._last_parity = p
        self.parity = (p + 1) % self.nbuf
        return tiles, (sem, sem.v)

    def consumed(self, ts_val):
        self.free_ts[self._last_parity] = ts_val


class XPipe:

    def __init__(self, b, CON, tag):
        self.b = b
        self.x = CON["xtile"]
        n = getattr(b, "xp_next", 0)
        b.xp_next = n + 1
        self.lsem = [b.dma_sem(f"xp{n}l0"), b.dma_sem(f"xp{n}l1")]
        self.ssem = [b.dma_sem(f"xp{n}s0"), b.dma_sem(f"xp{n}s1")]
        self.cons = [0, 0]
        self.it = 0

    def begin(self, src_ap):
        p = self.it % 2
        b = self.b
        b.sync.wait(b.vs, self.cons[p])
        b.load(self.x[:, p * TCH:(p + 1) * TCH], src_ap, self.lsem[p])
        return p

    def ld_ap(self, p):
        return self.x[:, p * TCH:(p + 1) * TCH]

    def out_ap(self, p):
        return self.x[:, (2 + p) * TCH:(3 + p) * TCH]

    def wait_ld(self, p):
        self.b.vector.wait(self.lsem[p], self.lsem[p].v)

    def wait_out_free(self, p):
        self.b.vector.wait(self.ssem[p], self.ssem[p].v)

    def end(self, p, dst_ap, vs_val):
        b = self.b
        b.sync.wait(b.vs, vs_val)
        b.store(dst_ap, self.out_ap(p), self.ssem[p])
        self.cons[p] = vs_val
        self.it += 1


def emit_precast(b, CON, bfscratch, jobs):
    NC = 1024
    p32h = [CON["lnx"][:, 0:NC], CON["lnx"][:, NC:2 * NC]]
    s32h = [CON["lnx"][:, 2 * NC:3 * NC], CON["lnx"][:, 3 * NC:4 * NC]]
    pfh = [bfscratch[:, 0:NC], bfscratch[:, NC:2 * NC]]
    lds = [b.dma_sem("prp0"), b.dma_sem("prp1")]
    sts = [b.dma_sem("prs0"), b.dma_sem("prs1")]
    cast_done = [0, 0]
    par = 0
    for (src, scl, dst, K, N) in jobs:
        for k in range(K // 128):
            for c0 in range(0, N, NC):
                nc_ = min(NC, N - c0)
                p32 = p32h[par]
                s32 = s32h[par]
                b.sync.wait(b.vs, cast_done[par])
                if scl is None:
                    b.load(p32[:, 0:nc_], src[k * 128:(k + 1) * 128, c0:c0 + nc_],
                           lds[par])
                else:
                    b.load(p32.bitcast(I32)[:, 0:nc_],
                           src[k * 128:(k + 1) * 128, c0:c0 + nc_], lds[par])
                    b.load(s32[:, 0:nc_], scl[k * 128:(k + 1) * 128, c0:c0 + nc_],
                           lds[par])
                b.vector.wait(lds[par], lds[par].v)
                b.vector.wait(sts[par], sts[par].v)
                dstb = pfh[par][:, 0:nc_]
                if scl is None:
                    vv = b.vector.op("tensor_copy", dstb, p32[:, 0:nc_], inc=b.vs)
                else:
                    b.vector.op("tensor_scalar", dstb, p32.bitcast(I32)[:, 0:nc_],
                                -128.0, None, op0=ALU.add)
                    vv = b.vector.op("tensor_tensor", dstb, dstb, s32[:, 0:nc_],
                                     op=ALU.mult, inc=b.vs)
                cast_done[par] = vv
                b.sync.wait(b.vs, vv)
                if isinstance(dst, TW):
                    kc = dst.K // 128
                    b.store(dst.dst(k // kc, k % kc, c0, nc_), dstb, sts[par])
                else:
                    b.store(dst[k * 128:(k + 1) * 128, c0:c0 + nc_], dstb,
                            sts[par])
                par ^= 1
    b.dma_fence()


def load_full_direct(b, name, dram_bf, K, N):
    KC = K // 128
    t = b.sb(name, [128, KC * N], BF16)
    lv = 0
    for k in range(KC):
        lv = b.load(t[:, k * N:(k + 1) * N], dram_bf[k * 128:(k + 1) * 128, :])

    def tiles_fn(k, n0, nw=128):
        return t[:, k * N + n0: k * N + n0 + nw]
    return tiles_fn, lv


def emit_stats(b, CON, P, src_fn, n_chunks, eps, src_f32=False):
    sumb, sqb, ab, cb = P[0], P[1], P[2], P[3]
    ones_c = CON["ones_col_bf"]
    ones_r = CON["ones_row_bf"]
    stat = CON["statv"]
    stat_bf = CON["statv_bf"]
    sq_t = CON["scratch_bf"]
    sq_t2 = CON["scratch_bf2"]
    prev_tv = [0, 0]
    for fc in range(n_chunks):
        p = fc % 2
        src = src_fn(fc)
        sq_ap = sq_t[:, p * TCH:(p + 1) * TCH]
        if src_f32:
            sq2_ap = sq_t2[:, p * TCH:(p + 1) * TCH]
            b.vector.wait(b.ts, prev_tv[p])
            vv = b.vector.op("tensor_copy", sq2_ap, src, inc=b.vs)
            b.tensor.wait(b.vs, vv)
            rhs = sq2_ap
        else:
            rhs = src
        b.scalar.wait(b.ts, prev_tv[p])
        sv = b.scalar.op("activation", sq_ap, src, AF.Square, inc=b.ss)
        b.tensor.wait(b.ss, sv)
        b.tensor.op("matmul", sumb[0:1, :], ones_c[:], rhs,
                    start=(fc == 0), stop=(fc == n_chunks - 1))
        tv = b.tensor.op("matmul", sqb[0:1, :], ones_c[:], sq_ap,
                         start=(fc == 0), stop=(fc == n_chunks - 1), inc=b.ts)
        prev_tv[p] = tv
    b.vector.wait(b.ts, tv)
    mean = stat[:, 0:TCH]
    var = stat[:, TCH:2 * TCH]
    a_v = stat[:, 2 * TCH:3 * TCH]
    c_v = stat[:, 3 * TCH:4 * TCH]
    nd = float(n_chunks * 128)
    b.vector.op("tensor_scalar", mean, sumb[0:1, :], 1.0 / nd, None, op0=ALU.mult)
    b.vector.op("tensor_scalar", var, sqb[0:1, :], 1.0 / nd, None, op0=ALU.mult)
    b.vector.op("tensor_tensor", a_v, mean, mean, op=ALU.mult)
    b.vector.op("tensor_tensor", var, var, a_v, op=ALU.subtract)
    vv = b.vector.op("tensor_scalar", var, var, eps, None, op0=ALU.add, inc=b.vs)
    b.scalar.wait(b.vs, vv)
    sv = b.scalar.op("activation", a_v, var, AF.Sqrt, inc=b.ss)
    b.vector.wait(b.ss, sv)
    b.vector.op("reciprocal", a_v, a_v)
    b.vector.op("tensor_tensor", c_v, mean, a_v, op=ALU.mult)
    b.vector.op("tensor_scalar", c_v, c_v, -1.0, None, op0=ALU.mult)
    b.vector.op("tensor_copy", stat_bf[:, 0:TCH], a_v)
    vv = b.vector.op("tensor_copy", stat_bf[:, TCH:2 * TCH], c_v, inc=b.vs)
    b.tensor.wait(b.vs, vv)
    b.tensor.op("matmul", ab[:, :], ones_r[:], stat_bf[:, 0:TCH], start=True, stop=True)
    tv = b.tensor.op("matmul", cb[:, :], ones_r[:], stat_bf[:, TCH:2 * TCH],
                     start=True, stop=True, inc=b.ts)
    return tv


def emit_rms_stats(b, CON, P, sq_src_fn, n_chunks, eps):
    sqb, ab = P[1], P[2]
    ones_c = CON["ones_col_bf"]
    ones_r = CON["ones_row_bf"]
    stat = CON["statv"]
    stat_bf = CON["statv_bf"]
    sq_t = CON["scratch_bf"]
    prev_tv = [0, 0]
    for fc in range(n_chunks):
        p = fc % 2
        sq_ap = sq_t[:, p * TCH:(p + 1) * TCH]
        b.scalar.wait(b.ts, prev_tv[p])
        sv = b.scalar.op("activation", sq_ap, sq_src_fn(fc), AF.Square, inc=b.ss)
        b.tensor.wait(b.ss, sv)
        tv = b.tensor.op("matmul", sqb[0:1, :], ones_c[:], sq_ap,
                         start=(fc == 0), stop=(fc == n_chunks - 1), inc=b.ts)
        prev_tv[p] = tv
    b.vector.wait(b.ts, tv)
    var = stat[:, TCH:2 * TCH]
    a_v = stat[:, 2 * TCH:3 * TCH]
    nd = float(n_chunks * 128)
    b.vector.op("tensor_scalar", var, sqb[0:1, :], 1.0 / nd, None, op0=ALU.mult)
    vv = b.vector.op("tensor_scalar", var, var, eps, None, op0=ALU.add, inc=b.vs)
    b.scalar.wait(b.vs, vv)
    sv = b.scalar.op("activation", a_v, var, AF.Sqrt, inc=b.ss)
    b.vector.wait(b.ss, sv)
    b.vector.op("reciprocal", a_v, a_v)
    vv = b.vector.op("tensor_copy", stat_bf[:, 0:TCH], a_v, inc=b.vs)
    b.tensor.wait(b.vs, vv)
    tv = b.tensor.op("matmul", ab[:, :], ones_r[:], stat_bf[:, 0:TCH],
                     start=True, stop=True, inc=b.ts)
    return tv


def emit_ln(b, xdram, XN, CON, P, eps=1e-5, tc_hook=None):
    T, NTC = b.T, b.NTC
    xt = CON["lnx"]
    ab, cb = P[2], P[3]
    sln = b.dma_sem("lnx")
    for tc in range(NTC):
        t0 = tc * TCH
        for fc in range(NFC):
            b.load(xt[:, fc * TCH:(fc + 1) * TCH],
                   xdram[fc * 128:(fc + 1) * 128, t0:t0 + TCH], sln)
        b.vector.wait(sln, sln.v)
        b.scalar.wait(sln, sln.v)
        tv = emit_stats(b, CON, P,
                        lambda fc: xt[:, fc * TCH:(fc + 1) * TCH],
                        NFC, eps, src_f32=True)
        b.vector.wait(b.ts, tv)
        for fc in range(NFC):
            tmp = CON["tmp_f32"]
            b.vector.op("tensor_tensor", tmp[:, 0:TCH],
                        xt[:, fc * TCH:(fc + 1) * TCH], ab[:, :], op=ALU.mult)
            b.vector.op("tensor_tensor", XN[:, fc * T + t0: fc * T + t0 + TCH],
                        tmp[:, 0:TCH], cb[:, :], op=ALU.add)
        if tc_hook is not None:
            stat = CON["statv"]
            tc_hook(tc, t0, xt, stat[:, 2 * TCH:3 * TCH], stat[:, 3 * TCH:4 * TCH])
        ms = b.vector.milestone(b.vs)
        b.tensor.wait(*ms)
        b.sync.wait(*ms)
        b.scalar.wait(*ms)


def emit_xw_to_psum(b, tiles, rhs_fn, psum_ap, nk, inc=None):
    v = None
    for k in range(nk):
        v = b.tensor.op("matmul", psum_ap, tiles[k], rhs_fn(k),
                        start=(k == 0), stop=(k == nk - 1),
                        inc=(inc if k == nk - 1 else None))
    return v



def emit_l0_ffn(b, xdram, x1dram, XN, CON, P, w, AR):
    T, NTC = b.T, b.NTC
    TH = TCH
    AR.reset()
    HBUF = AR.take_bf(NFC * TH)
    YBUF = AR.take_bf(32 * TH)
    ab, cb = P[2], P[3]
    w1s = WStream(b, "l0w1", NFC, 128, arena=AR)
    w2s = WStream(b, "l0w2", 32, 128, arena=AR)
    fb2h = b.sb("l0_fb2h", [128, NFC], F32)
    b.vector.op("tensor_scalar", fb2h[:], w["fb2c"][:], 0.5, None, op0=ALU.mult)

    pyA, pyB = P[4], P[5]
    last_evac = [None, None]
    xp = XPipe(b, CON, "f")
    for th in range(NTC):
        t0 = th * TCH
        tv = emit_stats(b, CON, P,
                        lambda fc: XN[:, fc * T + t0: fc * T + t0 + TCH],
                        NFC, 1e-5)
        b.vector.wait(b.ts, tv)
        for fc in range(NFC):
            tmp = CON["tmp_f32"]
            b.vector.op("tensor_tensor", tmp[:, 0:TCH],
                        XN[:, fc * T + t0: fc * T + t0 + TCH], ab[:, :], op=ALU.mult)
            b.vector.op("tensor_tensor", tmp[:, TCH:2 * TCH], tmp[:, 0:TCH],
                        cb[:, :], op=ALU.add)
            b.vector.op("tensor_scalar", HBUF[:, fc * TH:(fc + 1) * TH],
                        tmp[:, TCH:2 * TCH], w["g1c"][:, fc:fc + 1],
                        w["b1c"][:, fc:fc + 1], op0=ALU.mult, op1=ALU.add)
        ms = b.vector.milestone(b.vs)
        b.tensor.wait(*ms)

        for oc in range(32):
            tiles, wv = w1s.load_block(w["w1bf"], oc * 128)
            b.tensor.wait(*wv)
            pb = oc % 2
            py = pyA if pb == 0 else pyB
            if last_evac[pb]:
                b.tensor.wait(*last_evac[pb])
            tv = emit_xw_to_psum(b, tiles,
                                 lambda k: HBUF[:, k * TH:(k + 1) * TH],
                                 py[:, :], NFC, inc=b.ts)
            w1s.consumed(tv)
            b.scalar.wait(b.ts, tv)
            last_evac[pb] = (b.ss, b.scalar.op(
                "activation", YBUF[:, oc * TH:(oc + 1) * TH],
                py[:, :], AF.Silu, bias=w["fb1c"][:, oc:oc + 1], inc=b.ss))

        for le in last_evac:
            if le:
                b.tensor.wait(*le)
        for zc in range(NFC):
            tiles, wv = w2s.load_block(w["w2bf"], zc * 128)
            b.tensor.wait(*wv)
            pb = zc % 2
            pz = pyA if pb == 0 else pyB
            if last_evac[pb]:
                b.tensor.wait(*last_evac[pb])
            tv = emit_xw_to_psum(b, tiles,
                                 lambda k: YBUF[:, k * TH:(k + 1) * TH],
                                 pz[:, :], 32, inc=b.ts)
            w2s.consumed(tv)
            xpp = xp.begin(xdram[zc * 128:(zc + 1) * 128, t0:t0 + TCH])
            b.vector.wait(b.ts, tv)
            vv0 = b.vector.op("tensor_scalar", CON["tmp_f32"][:, 0:TCH], pz[:, :],
                              0.5, fb2h[:, zc:zc + 1], op0=ALU.mult, op1=ALU.add,
                              inc=b.vs)
            last_evac[pb] = (b.vs, vv0)
            xp.wait_ld(xpp)
            b.vector.op("tensor_tensor", CON["tmp_f32"][:, TCH:2 * TCH],
                        xp.ld_ap(xpp),
                        XN[:, zc * T + t0: zc * T + t0 + TCH], op=ALU.add)
            xp.wait_out_free(xpp)
            vv = b.vector.op("tensor_tensor", xp.out_ap(xpp),
                             CON["tmp_f32"][:, 0:TCH], CON["tmp_f32"][:, TCH:2 * TCH],
                             op=ALU.add, inc=b.vs)
            xp.end(xpp, x1dram[zc * 128:(zc + 1) * 128, t0:t0 + TCH], vv)


def emit_mha(b, x1dram, xodram, XN, CON, P, PT, w, wqbf, wkbf, wvbf, wobf, boc,
             abuf, AR, add_q=False, scale=0.125, bqc=None, bkc=None, bvc=None,
             q_src=None):
    T, NTC = b.T, b.NTC
    NSC = T // 128
    AR.reset()
    K2 = AR.take_bf(T)
    V2 = AR.take_bf(T)
    Q2 = AR.take_bf(T)
    VT0 = AR.take_bf(NSC * 65)
    VT1 = AR.take_bf(NSC * 65)
    EST = AR.take_bf(TCH)
    NUM = AR.take_bf(TCH)
    STG = AR.take_bf(TCH)
    RV = CON["rv"]
    wstr = WStream(b, "mha_w", NFC, 128, arena=AR)
    AB = AR.take_bf(NFC * TCH)
    WOALL = AR.take_bf(NFC * D)
    psc = [P[4], P[5]]
    pnum = P[6]
    pbc = P[2]
    pqkv = [P[0], P[1]]
    last_qkv_ev = [None, None]
    last_sc_act = [0, 0]
    last_num_ss = 0
    last_num_vs = 0
    last_pbc_vs = 0
    stg_store = None
    q2_cons = [0]

    sqn = b.dma_sem("mha_qn")
    for hp in range(NFC):
        streams = ((wkbf, K2, bkc), (wvbf, V2, bvc))
        if q_src is None:
            streams = streams + ((wqbf, Q2, bqc),)
        else:
            b.sync.wait(b.ts, q2_cons[0])
            b.load(Q2[:, :], q_src[hp * 128:(hp + 1) * 128, :], sqn)
            b.tensor.wait(sqn, sqn.v)
            b.vector.wait(sqn, sqn.v)
        for (wbf, dst, bias) in streams:
            tiles, wv_ = wstr.load_block(wbf, hp * 128)
            b.tensor.wait(*wv_)
            for tc in range(NTC):
                pb = tc % 2
                if last_qkv_ev[pb]:
                    b.tensor.wait(*last_qkv_ev[pb])
                tv = emit_xw_to_psum(
                    b, tiles,
                    lambda k, tc=tc: XN[:, k * T + tc * TCH: k * T + (tc + 1) * TCH],
                    pqkv[pb][:, :], NFC, inc=b.ts)
                if bias is not None:
                    b.vector.wait(b.ts, tv)
                    last_qkv_ev[pb] = (b.vs, b.vector.op(
                        "tensor_scalar", dst[:, tc * TCH:(tc + 1) * TCH],
                        pqkv[pb][:, :], bias[:, hp:hp + 1], None, op0=ALU.add,
                        inc=b.vs))
                else:
                    b.scalar.wait(b.ts, tv)
                    last_qkv_ev[pb] = (b.ss, b.scalar.op(
                        "copy", dst[:, tc * TCH:(tc + 1) * TCH], pqkv[pb][:, :],
                        inc=b.ss))
            wstr.consumed(tv)
        for le in last_qkv_ev:
            if le:
                b.tensor.wait(*le)
        for h2, VT in ((0, VT0), (64, VT1)):
            for c in range(NSC):
                tv = b.tensor.op("transpose", PT[:, 0:64],
                                 V2[h2:h2 + 64, c * 128:(c + 1) * 128],
                                 CON["ident_bf"][h2:h2 + 64, h2:h2 + 64], inc=b.ts)
                b.vector.wait(b.ts, tv)
                vv = b.vector.op("tensor_copy", VT[:, c * 65:c * 65 + 64], PT[:, 0:64],
                                 inc=b.vs)
                b.tensor.wait(b.vs, vv)
            vt3 = VT[:, 0:NSC * 65].rearrange("p (c e) -> p c e", e=65)
            vv = b.vector.op("memset", vt3[:, 0:NSC, 64:65], 1.0, inc=b.vs)
            b.tensor.wait(b.vs, vv)
        for tc in range(NTC):
            for h2, VT in ((0, VT0), (64, VT1)):
                for sc in range(NSC):
                    pb = sc % 2
                    b.tensor.wait(b.ss, last_sc_act[pb])
                    if sc == 0:
                        b.tensor.wait(b.ss, last_num_ss)
                        b.tensor.wait(b.vs, last_num_vs)
                    tv = b.tensor.op("matmul", psc[pb][:, :],
                                     K2[h2:h2 + 64, sc * 128:(sc + 1) * 128],
                                     Q2[h2:h2 + 64, tc * TCH:(tc + 1) * TCH],
                                     start=True, stop=True, inc=b.ts)
                    b.scalar.wait(b.ts, tv)
                    last_sc_act[pb] = b.scalar.op("activation", EST[:], psc[pb][:, :],
                                                  AF.Exp, scale=scale, inc=b.ss)
                    b.tensor.wait(b.ss, last_sc_act[pb])
                    tv = b.tensor.op("matmul", pnum[0:65, :],
                                     VT[:, sc * 65:(sc + 1) * 65],
                                     EST[:], start=(sc == 0), stop=(sc == NSC - 1),
                                     inc=(b.ts if sc == NSC - 1 else None))
                b.vector.wait(b.ts, tv)
                if stg_store:
                    b.vector.wait(*stg_store)
                b.vector.op("tensor_scalar", RV[:], pnum[64:65, :], 1e-8, None,
                            op0=ALU.add)
                vv = b.vector.op("reciprocal", RV[:], RV[:], inc=b.vs)
                b.tensor.wait(b.vs, vv)
                b.tensor.wait(b.vs, last_pbc_vs)
                tv2 = b.tensor.op("matmul", pbc[0:64, :],
                                  CON["ones_row_f32"][0:1, 0:64],
                                  RV[:], start=True, stop=True, inc=b.ts)
                b.scalar.wait(b.ts, tv)
                last_num_ss = b.scalar.op("copy", NUM[0:64, :], pnum[0:64, :], inc=b.ss)
                b.vector.wait(b.ts, tv2)
                b.vector.wait(b.ss, last_num_ss)
                vv = b.vector.op("tensor_tensor", STG[h2:h2 + 64, :], NUM[0:64, :],
                                 pbc[0:64, :], op=ALU.mult, inc=b.vs)
                if add_q:
                    vv = b.vector.op("tensor_tensor", STG[h2:h2 + 64, :],
                                     STG[h2:h2 + 64, :],
                                     Q2[h2:h2 + 64, tc * TCH:(tc + 1) * TCH],
                                     op=ALU.add, inc=b.vs)
                last_pbc_vs = vv
                last_num_vs = vv
            b.sync.wait(b.vs, vv)
            sstg = b.dma_sem("stg")
            b.store(abuf[hp * 128:(hp + 1) * 128, tc * TCH:(tc + 1) * TCH],
                    STG[:, :], sstg)
            stg_store = (sstg, sstg.v)
            q2_cons[0] = b.ts.v
    emit_barrier(b)
    swo = b.dma_sem("ini")
    for oc in range(NFC):
        b.load(WOALL[:, oc * D: (oc + 1) * D],
               wobf.tw.src(wobf.mat, oc * 128, NFC), swo)
    wo_lv = (swo, swo.v)

    def wo_tiles(k, n0, nw=128):
        oc = n0 // 128
        return WOALL[:, oc * D + k * 128: oc * D + (k + 1) * 128]
    po = [P[0], P[1]]
    last_o = [0, 0]
    ab_cons = 0
    sab = b.dma_sem("ab")
    xp = XPipe(b, CON, "m")
    for tc in range(NTC):
        b.sync.wait(b.ts, ab_cons)
        for fc in range(NFC):
            b.load(AB[:, fc * TCH:(fc + 1) * TCH],
                   abuf[fc * 128:(fc + 1) * 128, tc * TCH:(tc + 1) * TCH], sab)
        b.tensor.wait(sab, sab.v)
        b.tensor.wait(*wo_lv)
        for oc in range(NFC):
            pb = oc % 2
            b.tensor.wait(b.vs, last_o[pb])
            tv = emit_xw_to_psum(
                b, [wo_tiles(k, oc * 128) for k in range(NFC)],
                lambda k: AB[:, k * TCH:(k + 1) * TCH],
                po[pb][:, :], NFC, inc=b.ts)
            ab_cons = tv
            xpp = xp.begin(x1dram[oc * 128:(oc + 1) * 128, tc * TCH:(tc + 1) * TCH])
            b.vector.wait(b.ts, tv)
            b.vector.op("tensor_scalar", CON["tmp_f32"][:, 0:TCH], po[pb][:, :],
                        boc[:, oc:oc + 1], None, op0=ALU.add)
            xp.wait_ld(xpp)
            xp.wait_out_free(xpp)
            last_o[pb] = b.vector.op("tensor_tensor", xp.out_ap(xpp),
                                     xp.ld_ap(xpp), CON["tmp_f32"][:, 0:TCH],
                                     op=ALU.add, inc=b.vs)
            xp.end(xpp, xodram[oc * 128:(oc + 1) * 128, tc * TCH:(tc + 1) * TCH],
                   last_o[pb])




def make_router_hook(b, CON, P, w, CWT8, rwT_sb, csb, rb_b=None):
    st_state = {"prev_ev": None}
    RT = CON["rtmp"]

    def hook(tc, t0, xt, a_v, c_v):
        pr = P[4]
        pc = P[5]
        ms_v = b.vector.milestone(b.vs)
        b.tensor.wait(*ms_v)
        b.vector.wait(b.ss, b.ss.v)
        for st_ in range(TCH // 128):
            tt0 = st_ * 128
            if st_state["prev_ev"]:
                b.tensor.wait(*st_state["prev_ev"])
            for fc in range(NFC):
                b.tensor.op("matmul", pr[0:128, 0:8],
                            xt[:, fc * TCH + tt0: fc * TCH + tt0 + 128],
                            rwT_sb[:, fc * 8:(fc + 1) * 8],
                            start=(fc == 0), stop=(fc == NFC - 1))
            b.tensor.op("matmul", pc[0:128, 0:1],
                        a_v[0:1, tt0:tt0 + 128], CON["ones_row_f32"][0:1, 0:1],
                        start=True, stop=True)
            tv = b.tensor.op("matmul", pc[0:128, 1:2],
                             c_v[0:1, tt0:tt0 + 128], CON["ones_row_f32"][0:1, 0:1],
                             start=True, stop=True, inc=b.ts)
            b.vector.wait(b.ts, tv)
            acol = RT[:, 8:9]
            ccol = RT[:, 9:10]
            b.vector.op("tensor_copy", RT[:, 8:10], pc[0:128, 0:2])
            L = RT[:, 0:8]
            b.vector.op("tensor_scalar", L, pr[0:128, 0:8], acol, None,
                        op0=ALU.mult)
            b.vector.op("tensor_scalar", RT[:, 16:24], csb[:], ccol, None,
                        op0=ALU.mult)
            b.vector.op("tensor_tensor", L, L, RT[:, 16:24], op=ALU.add)
            if rb_b is not None:
                b.vector.op("tensor_tensor", L, L, rb_b[:], op=ALU.add)
            m1 = RT[:, 10:11]
            b.vector.op("tensor_reduce", m1, L, axis=AX.X, op=ALU.max)
            nm1 = RT[:, 11:12]
            b.vector.op("tensor_scalar", nm1, m1, -1.0, None, op0=ALU.mult)
            msx = b.vector.milestone(b.vs)
            b.scalar.wait(*msx)
            sv = b.scalar.op("activation", RT[:, 16:24], L, AF.Exp, bias=nm1,
                             inc=b.ss)
            b.vector.wait(b.ss, sv)
            E = RT[:, 16:24]
            se = RT[:, 12:13]
            b.vector.op("tensor_reduce", se, E, axis=AX.X, op=ALU.add)
            b.vector.op("reciprocal", RT[:, 13:14], se)
            S = RT[:, 24:32]
            b.vector.op("tensor_scalar", S, E, RT[:, 13:14], None, op0=ALU.mult)
            mx = RT[:, 10:11]
            b.vector.op("tensor_reduce", mx, S, axis=AX.X, op=ALU.max)
            msk = RT[:, 32:40]
            b.vector.op("tensor_scalar", msk, S, mx, None, op0=ALU.is_ge)
            b.vector.op("scalar_tensor_tensor", RT[:, 16:24], msk, -1e30, S,
                        op0=ALU.mult, op1=ALU.add)
            m2 = RT[:, 11:12]
            b.vector.op("tensor_reduce", m2, RT[:, 16:24], axis=AX.X, op=ALU.max)
            den = RT[:, 12:13]
            b.vector.op("tensor_tensor", den, mx, m2, op=ALU.add)
            b.vector.op("reciprocal", RT[:, 13:14], den)
            b.vector.op("tensor_scalar", msk, S, m2, None, op0=ALU.is_ge)
            b.vector.op("tensor_tensor", S, S, msk, op=ALU.mult)
            b.vector.op("tensor_scalar", S, S, RT[:, 13:14], None, op0=ALU.mult)
            cwb = CON["scratch_bf"][:, 0:8]
            vv2 = b.vector.op("tensor_copy", cwb, S, inc=b.vs)
            b.tensor.wait(b.vs, vv2)
            tv2 = b.tensor.op("transpose", b.PT_BF[0:8, 0:128], cwb,
                              CON["ident_bf"][:, :], inc=b.ts)
            b.scalar.wait(b.ts, tv2)
            ev = b.scalar.op("copy", CWT8[0:8, t0 + tt0: t0 + tt0 + 128],
                             b.PT_BF[0:8, 0:128], inc=b.ss)
            st_state["prev_ev"] = (b.ss, ev)
            b.vector.wait(b.ss, ev)
    return hook


def emit_l1_attn(b, x1dram, xodram, XN, CON, P, w, AR):
    T, NTC = b.T, b.NTC
    AR.reset()
    PL = [AR.take_bf(NFC * TCH) for _ in range(6)]
    FC_ = AR.take_bf(NFC * D)
    FS_ = AR.take_bf(NFC * D)
    wstr = WStream(b, "l1a_w", NFC, 128, arena=AR)
    OT = AR.take_bf(2 * TCH)
    CARRY = b.sb("l1_carry", [128, 16], F32)
    SBF = CON["statv_bf"]
    ONE_PL = CON["scratch_bf2"]
    xp = XPipe(b, CON, "a1")

    sfc = b.dma_sem("l1_fc")
    for k in range(NFC):
        b.load(FC_[:, k * D:(k + 1) * D], w["fftc"][k * 128:(k + 1) * 128, :], sfc)
        b.load(FS_[:, k * D:(k + 1) * D], w["ffts"][k * 128:(k + 1) * 128, :], sfc)
    b.tensor.wait(sfc, sfc.v)
    vv = b.vector.op("memset", ONE_PL[:, 0:TCH], 1.0, inc=b.vs)
    vv = b.vector.op("memset", CARRY[:], 0.0, inc=b.vs)

    pA, pB = P[0], P[1]
    pnrm = P[6]
    pbc = P[2]
    evA = [None, None]
    ot_cons = [None, None]

    def mm_block(tiles_fn, nk, rhs_fn, dst, bias_col, pbi, extra_wait=None,
                 rhs_ready=None):
        pb_ = pbi % 2
        pp = pA if pb_ == 0 else pB
        if evA[pb_]:
            b.tensor.wait(*evA[pb_])
        if extra_wait:
            b.tensor.wait(*extra_wait)
        if rhs_ready:
            b.tensor.wait(*rhs_ready)
        tv = emit_xw_to_psum(b, [tiles_fn(k) for k in range(nk)], rhs_fn,
                             pp[:, :], nk, inc=b.ts)
        b.vector.wait(b.ts, tv)
        if bias_col is not None:
            ev = b.vector.op("tensor_scalar", dst, pp[:, :], bias_col, None,
                             op0=ALU.add, inc=b.vs)
        else:
            ev = b.vector.op("tensor_copy", dst, pp[:, :], inc=b.vs)
        evA[pb_] = (b.vs, ev)
        return tv, (b.vs, ev)

    pbi = [0]

    for tc in range(NTC):
        t0 = tc * TCH
        UK, UV, KR, KI, VR, VI = PL[0], PL[1], PL[2], PL[3], PL[4], PL[5]
        if tc > 0:
            ms_s = b.scalar.milestone(b.ss)
            ms_t = b.tensor.milestone(b.ts)
            b.vector.wait(*ms_s)
            b.vector.wait(*ms_t)
        u_ready = {}
        for (wbf, bias, dst) in ((w["wk_bf"], w["bkc"], UK),
                                 (w["wv_bf"], w["bvc"], UV)):
            for oc in range(NFC):
                tiles, wv_ = wstr.load_block(wbf, oc * 128)
                _, ev = mm_block(lambda k: tiles[k], NFC,
                                 lambda k: XN[:, k * T + t0: k * T + t0 + TCH],
                                 dst[:, oc * TCH:(oc + 1) * TCH],
                                 bias[:, oc:oc + 1], pbi[0], extra_wait=wv_)
                wstr.consumed(b.ts.v)
                pbi[0] += 1
            u_ready[id(dst)] = ev
        for (u_, re_, im_) in ((UK, KR, KI), (UV, VR, VI)):
            for (mat, dst) in ((FC_, re_), (FS_, im_)):
                for oc in range(NFC):
                    mm_block(lambda k, mat=mat: mat[:, k * D + oc * 128: k * D + oc * 128 + 128],
                             NFC,
                             lambda k, u_=u_: u_[:, k * TCH:(k + 1) * TCH],
                             dst[:, oc * TCH:(oc + 1) * TCH], None, pbi[0],
                             rhs_ready=u_ready[id(u_)])
                    pbi[0] += 1
        ms_t = b.tensor.milestone(b.ts)
        b.vector.wait(*ms_t)
        for fcx in range(NFC):
            sl = slice(fcx * TCH, (fcx + 1) * TCH)
            t1_ = CON["tmp_f32"][:, 0:TCH]
            t2_ = CON["tmp_f32"][:, TCH:2 * TCH]
            b.vector.op("tensor_tensor", t1_, KR[:, sl], VR[:, sl], op=ALU.mult)
            b.vector.op("tensor_tensor", t2_, KI[:, sl], VI[:, sl], op=ALU.mult)
            b.vector.op("tensor_tensor", t1_, t1_, t2_, op=ALU.subtract)
            b.vector.op("tensor_tensor", t2_, KR[:, sl], VI[:, sl], op=ALU.mult)
            b.vector.op("tensor_copy", UK[:, sl], t1_)
            b.vector.op("tensor_tensor", t1_, KI[:, sl], VR[:, sl], op=ALU.mult)
            b.vector.op("tensor_tensor", UV[:, sl], t1_, t2_, op=ALU.add)
        for (srcp, dstp, ci) in ((UK, VR, 0), (UV, VI, 1)):
            for fcx in range(NFC):
                sl = slice(fcx * TCH, (fcx + 1) * TCH)
                cc = CARRY[:, ci * NFC + fcx: ci * NFC + fcx + 1]
                b.vector.op("tensor_tensor_scan", dstp[:, sl], ONE_PL[:, 0:TCH],
                            srcp[:, sl], cc, op0=ALU.mult, op1=ALU.add)
                b.vector.op("tensor_copy", cc, dstp[:, sl].tensor[0:128, 0:1]
                            if False else dstp[:, fcx * TCH + TCH - 1: fcx * TCH + TCH])
        ms_t = b.tensor.milestone(b.ts)
        b.vector.wait(*ms_t)
        for oc in range(NFC):
            tiles, wv_ = wstr.load_block(w["wq_bf"], oc * 128)
            _, ev = mm_block(lambda k: tiles[k], NFC,
                             lambda k: XN[:, k * T + t0: k * T + t0 + TCH],
                             UK[:, oc * TCH:(oc + 1) * TCH],
                             w["bqc"][:, oc:oc + 1], pbi[0], extra_wait=wv_)
            wstr.consumed(b.ts.v)
            pbi[0] += 1
        uq_ready = ev
        for (mat, dst) in ((FC_, KR), (FS_, KI)):
            for oc in range(NFC):
                mm_block(lambda k, mat=mat: mat[:, k * D + oc * 128: k * D + oc * 128 + 128],
                         NFC,
                         lambda k: UK[:, k * TCH:(k + 1) * TCH],
                         dst[:, oc * TCH:(oc + 1) * TCH], None, pbi[0],
                         rhs_ready=uq_ready)
                pbi[0] += 1
        ms_t = b.tensor.milestone(b.ts)
        b.vector.wait(*ms_t)
        for fcx in range(NFC):
            sl = slice(fcx * TCH, (fcx + 1) * TCH)
            t1_ = CON["tmp_f32"][:, 0:TCH]
            t2_ = CON["tmp_f32"][:, TCH:2 * TCH]
            b.vector.op("tensor_tensor", t1_, VR[:, sl], KR[:, sl], op=ALU.mult)
            b.vector.op("tensor_tensor", t2_, VI[:, sl], KI[:, sl], op=ALU.mult)
            b.vector.op("tensor_tensor", t1_, t1_, t2_, op=ALU.add)
            b.vector.op("tensor_tensor", t2_, VI[:, sl], KR[:, sl], op=ALU.mult)
            b.vector.op("tensor_copy", UK[:, sl], t1_)
            b.vector.op("tensor_tensor", t1_, VR[:, sl], KI[:, sl], op=ALU.mult)
            b.vector.op("tensor_tensor", UV[:, sl], t2_, t1_, op=ALU.subtract)
        ms_v = b.vector.milestone(b.vs)
        b.scalar.wait(*ms_v)
        b.tensor.wait(*ms_v)
        prev_sq = [None, None]
        idx = 0
        for pl_ in (UK, UV):
            for fcx in range(NFC):
                p_ = idx % 2
                sq_ap = CON["scratch_bf"][:, p_ * TCH:(p_ + 1) * TCH]
                if prev_sq[p_]:
                    b.scalar.wait(*prev_sq[p_])
                sv = b.scalar.op("activation", sq_ap,
                                 pl_[:, fcx * TCH:(fcx + 1) * TCH], AF.Square,
                                 inc=b.ss)
                b.tensor.wait(b.ss, sv)
                tv = b.tensor.op("matmul", pnrm[0:1, :], CON["ones_col_bf"][:],
                                 sq_ap, start=(idx == 0),
                                 stop=(idx == 2 * NFC - 1), inc=b.ts)
                prev_sq[p_] = (b.ts, tv)
                idx += 1
        b.vector.wait(b.ts, tv)
        sv_ = CON["statv"][:, 0:TCH]
        b.vector.op("tensor_copy", sv_, pnrm[0:1, :])
        msv2 = b.vector.milestone(b.vs)
        b.scalar.wait(*msv2)
        sv2 = b.scalar.op("activation", sv_, sv_, AF.Sqrt, inc=b.ss)
        b.vector.wait(b.ss, sv2)
        b.vector.op("tensor_scalar", sv_, sv_, 1e-9, None, op0=ALU.add)
        b.vector.op("reciprocal", sv_, sv_)
        b.vector.op("tensor_scalar", sv_, sv_, 1.0 / 32768.0, None, op0=ALU.mult)
        vv = b.vector.op("tensor_copy", SBF[:, 0:TCH], sv_, inc=b.vs)
        b.tensor.wait(b.vs, vv)
        tvb = b.tensor.op("matmul", pbc[:, :], CON["ones_row_bf"][:],
                          SBF[:, 0:TCH], start=True, stop=True, inc=b.ts)
        for oc in range(NFC):
            pb_ = pbi[0] % 2
            pp = pA if pb_ == 0 else pB
            if evA[pb_]:
                b.tensor.wait(*evA[pb_])
            for k in range(NFC):
                b.tensor.op("matmul", pp[:, :],
                            FC_[:, k * D + oc * 128: k * D + oc * 128 + 128],
                            UK[:, k * TCH:(k + 1) * TCH],
                            start=(k == 0), stop=False)
            for k in range(NFC):
                tv = b.tensor.op("matmul", pp[:, :],
                                 FS_[:, k * D + oc * 128: k * D + oc * 128 + 128],
                                 UV[:, k * TCH:(k + 1) * TCH],
                                 start=False, stop=(k == NFC - 1),
                                 inc=(b.ts if k == NFC - 1 else None))
            pbi[0] += 1
            xpp = xp.begin(x1dram[oc * 128:(oc + 1) * 128, t0:t0 + TCH])
            b.scalar.wait(b.ts, tv)
            osl = oc % 2
            if ot_cons[osl]:
                b.scalar.wait(*ot_cons[osl])
            ot_ap = OT[:, osl * TCH:(osl + 1) * TCH]
            ev = b.scalar.op("copy", ot_ap, pp[:, :], inc=b.ss)
            evA[pb_] = (b.ss, ev)
            b.vector.wait(b.ss, ev)
            b.vector.wait(b.ts, tvb)
            vvo = b.vector.op("tensor_tensor", CON["tmp_f32"][:, 0:TCH], ot_ap,
                              pbc[:, :], op=ALU.mult, inc=b.vs)
            ot_cons[osl] = (b.vs, vvo)
            xp.wait_ld(xpp)
            xp.wait_out_free(xpp)
            vv = b.vector.op("tensor_tensor", xp.out_ap(xpp),
                             xp.ld_ap(xpp), CON["tmp_f32"][:, 0:TCH],
                             op=ALU.add, inc=b.vs)
            xp.end(xpp, xodram[oc * 128:(oc + 1) * 128, t0:t0 + TCH], vv)


def emit_l1_ffn(b, xdram, x1dram, XN, CON, P, w, AR, CWT8, has_shared=False,
                gelu=False):
    T, NTC = b.T, b.NTC
    TH = 1024 if T >= 1024 else T
    NSUB = TH // TCH
    NTH = T // TH
    AR.reset()
    HB = AR.take_bf(16 * TH)
    GT = AR.take_bf(NSUB * TCH)
    BCW = AR.take_bf(NSUB * TCH)
    ACC = AR.take_f32(NFC * TH)
    wgS = WStream(b, "l1f_g", NFC, 128, arena=AR)
    wuS = WStream(b, "l1f_u", NFC, 128, arena=AR) if not gelu else None
    wdS = WStream(b, "l1f_d", 16, 128, arena=AR)
    xp = XPipe(b, CON, "f1")
    pY = [P[4], P[5]]
    pbc = P[6]
    evH = [None] * 4
    evG = [None, None]
    evU = [None, None]
    evY = [None, None]
    evB = None
    bcw_cons = [None]
    n_exp = w["n_experts"]

    for tc in range(NTH):
        t0 = tc * TH
        acc_ev = [None] * NFC
        exps = (["sh"] if has_shared else []) + list(range(n_exp))
        for e in exps:
            shared = (e == "sh")
            if not shared:
                for ts_ in range(NSUB):
                    if evB:
                        b.tensor.wait(*evB)
                    tvb = b.tensor.op("matmul", pbc[:, :],
                                      w["oh8"][:, e * 128:(e + 1) * 128],
                                      CWT8[0:8, t0 + ts_ * TCH: t0 + (ts_ + 1) * TCH],
                                      start=True, stop=True, inc=b.ts)
                    b.scalar.wait(b.ts, tvb)
                    if bcw_cons[0]:
                        b.scalar.wait(*bcw_cons[0])
                    evb2 = b.scalar.op("copy", BCW[:, ts_ * TCH:(ts_ + 1) * TCH],
                                       pbc[:, :], inc=b.ss)
                    evB = (b.ss, evb2)
            for half in range(2):
                hbase = half * 16
                for oc in range(16):
                    oca = hbase + oc
                    slot = oc % 2
                    if gelu:
                        wmat = w["sh_w1"] if shared else w["sp_w1"][e]
                        bias = w["sh_b1c"] if shared else w["sp_b1c"][e]
                        tiles, wv_ = wgS.load_block(wmat, oca * 128)
                        b.tensor.wait(*wv_)
                        for ts_ in range(NSUB):
                            bk = P[2 * slot + ts_]
                            if evH[2 * slot + ts_]:
                                b.tensor.wait(*evH[2 * slot + ts_])
                            tv = emit_xw_to_psum(
                                b, tiles,
                                lambda k, ts_=ts_: XN[:, k * T + t0 + ts_ * TCH:
                                                      k * T + t0 + (ts_ + 1) * TCH],
                                bk[:, :], NFC, inc=b.ts)
                        wgS.consumed(tv)
                        b.scalar.wait(b.ts, tv)
                        for ts_ in range(NSUB):
                            bk = P[2 * slot + ts_]
                            ev = b.scalar.op(
                                "activation",
                                HB[:, oc * TH + ts_ * TCH: oc * TH + (ts_ + 1) * TCH],
                                bk[:, :], AF.Gelu_apprx_tanh,
                                bias=bias[:, oca:oca + 1], inc=b.ss)
                            evH[2 * slot + ts_] = (b.ss, ev)
                    else:
                        tiles, wv_ = wgS.load_block(w["wg"][e], oca * 128)
                        b.tensor.wait(*wv_)
                        for ts_ in range(NSUB):
                            if evG[ts_]:
                                b.tensor.wait(*evG[ts_])
                            tv = emit_xw_to_psum(
                                b, tiles,
                                lambda k, ts_=ts_: XN[:, k * T + t0 + ts_ * TCH:
                                                      k * T + t0 + (ts_ + 1) * TCH],
                                P[ts_][:, :], NFC, inc=b.ts)
                        wgS.consumed(tv)
                        tvg = tv
                        tiles2, wv2 = wuS.load_block(w["wu"][e], oca * 128)
                        b.tensor.wait(*wv2)
                        for ts_ in range(NSUB):
                            if evU[ts_]:
                                b.tensor.wait(*evU[ts_])
                            tv2 = emit_xw_to_psum(
                                b, tiles2,
                                lambda k, ts_=ts_: XN[:, k * T + t0 + ts_ * TCH:
                                                      k * T + t0 + (ts_ + 1) * TCH],
                                P[2 + ts_][:, :], NFC, inc=b.ts)
                        wuS.consumed(tv2)
                        b.scalar.wait(b.ts, tvg)
                        for ts_ in range(NSUB):
                            sv = b.scalar.op("activation",
                                             GT[:, ts_ * TCH:(ts_ + 1) * TCH],
                                             P[ts_][:, :], AF.Silu, inc=b.ss)
                            evG[ts_] = (b.ss, sv)
                        b.vector.wait(b.ss, sv)
                        b.vector.wait(b.ts, tv2)
                        for ts_ in range(NSUB):
                            ev = b.vector.op(
                                "tensor_tensor",
                                HB[:, oc * TH + ts_ * TCH: oc * TH + (ts_ + 1) * TCH],
                                GT[:, ts_ * TCH:(ts_ + 1) * TCH],
                                P[2 + ts_][:, :], op=ALU.mult, inc=b.vs)
                            evU[ts_] = (b.vs, ev)
                for le in (evH + evG + evU):
                    if le:
                        b.tensor.wait(*le)
                for zc in range(NFC):
                    wmat = (w["sh_w2"] if shared else w["sp_w2"][e]) if gelu \
                        else w["wd"][e]
                    tiles, wv_ = wdS.load_block(wmat, zc * 128, k0=hbase)
                    b.tensor.wait(*wv_)
                    for ts_ in range(NSUB):
                        pb_ = (zc * NSUB + ts_) % 2
                        if evY[pb_]:
                            b.tensor.wait(*evY[pb_])
                        tv = emit_xw_to_psum(
                            b, tiles,
                            lambda k, ts_=ts_: HB[:, k * TH + ts_ * TCH:
                                                  k * TH + (ts_ + 1) * TCH],
                            pY[pb_][:, :], 16, inc=b.ts)
                        b.vector.wait(b.ts, tv)
                        acc_ap = ACC[:, zc * TH + ts_ * TCH: zc * TH + (ts_ + 1) * TCH]
                        first = (e == exps[0]) and half == 0
                        tmp = CON["tmp_f32"][:, 0:TCH]
                        py_ap = pY[pb_][:, :]
                        add_bias = (half == 1)
                        if shared:
                            if add_bias:
                                b.vector.op("tensor_scalar", tmp, py_ap,
                                            w["sh_b2c"][:, zc:zc + 1], None,
                                            op0=ALU.add)
                                src_ap = tmp
                            else:
                                src_ap = py_ap
                            if first:
                                ev = b.vector.op("tensor_copy", acc_ap, src_ap,
                                                 inc=b.vs)
                            else:
                                ev = b.vector.op("tensor_tensor", acc_ap, acc_ap,
                                                 src_ap, op=ALU.add, inc=b.vs)
                        else:
                            b.vector.wait(*evB)
                            if gelu and add_bias:
                                b.vector.op("tensor_scalar", tmp, py_ap,
                                            w["sp_b2c"][e][:, zc:zc + 1], None,
                                            op0=ALU.add)
                                ev0 = b.vector.op(
                                    "tensor_tensor", tmp, tmp,
                                    BCW[:, ts_ * TCH:(ts_ + 1) * TCH],
                                    op=ALU.mult, inc=b.vs)
                            else:
                                ev0 = b.vector.op(
                                    "tensor_tensor", tmp, py_ap,
                                    BCW[:, ts_ * TCH:(ts_ + 1) * TCH],
                                    op=ALU.mult, inc=b.vs)
                            bcw_cons[0] = (b.vs, ev0)
                            if first:
                                ev = b.vector.op("tensor_copy", acc_ap, tmp,
                                                 inc=b.vs)
                            else:
                                ev = b.vector.op("tensor_tensor", acc_ap, acc_ap,
                                                 tmp, op=ALU.add, inc=b.vs)
                        evY[pb_] = (b.vs, ev)
                        acc_ev[zc] = (b.vs, ev)
                    wdS.consumed(tv)
        for zc in range(NFC):
            for ts_ in range(NSUB):
                tt0 = t0 + ts_ * TCH
                xpp = xp.begin(xdram[zc * 128:(zc + 1) * 128, tt0:tt0 + TCH])
                xp.wait_ld(xpp)
                xp.wait_out_free(xpp)
                if acc_ev[zc]:
                    b.vector.wait(*acc_ev[zc])
                vv = b.vector.op("tensor_tensor", xp.out_ap(xpp), xp.ld_ap(xpp),
                                 ACC[:, zc * TH + ts_ * TCH:
                                     zc * TH + (ts_ + 1) * TCH],
                                 op=ALU.add, inc=b.vs)
                xp.end(xpp, x1dram[zc * 128:(zc + 1) * 128, tt0:tt0 + TCH], vv)


def emit_block1(b, CON, P, PT, xin, xout):
    T = b.T
    x1 = b.dram("x1scr1", [D, T], F32, kind="Internal")
    w = {
        "wqT": b.dram("L1_wqT", [D, D], F32),
        "wkT": b.dram("L1_wkT", [D, D], F32),
        "wvT": b.dram("L1_wvT", [D, D], F32),
        "fftc": b.dram("L1_fftc", [D, D], BF16),
        "ffts": b.dram("L1_ffts", [D, D], BF16),
        "n_experts": 8,
    }
    w["wq_bf"] = TW(b, "L1_wq_bf", D, D).view()
    w["wk_bf"] = TW(b, "L1_wk_bf", D, D).view()
    w["wv_bf"] = TW(b, "L1_wv_bf", D, D).view()
    w["bkc"] = load_colvec(b, "l1bkc", NFC, b.dram("L1_bkc", [128, NFC], F32))
    w["bvc"] = load_colvec(b, "l1bvc", NFC, b.dram("L1_bvc", [128, NFC], F32))
    w["bqc"] = load_colvec(b, "l1bqc", NFC, b.dram("L1_bqc", [128, NFC], F32))
    wg_f = b.dram("L1_wg", [8 * D, 4096], F32)
    wu_f = b.dram("L1_wu", [8 * D, 4096], F32)
    wd_f = b.dram("L1_wd", [8 * 4096, D], F32)
    wg_bf = TW(b, "L1_wg_bf", D, 4096, nmat=8)
    wu_bf = TW(b, "L1_wu_bf", D, 4096, nmat=8)
    wd_bf = TW(b, "L1_wd_bf", 4096, D, nmat=8)
    w["wg"] = [wg_bf[e] for e in range(8)]
    w["wu"] = [wu_bf[e] for e in range(8)]
    w["wd"] = [wd_bf[e] for e in range(8)]
    rwT = b.dram("L1_rwT", [D, 8], F32)
    w["oh8"] = load_oh8(b, CON)
    jobs = [
        (w["wqT"], None, w["wq_bf"].tw, D, D),
        (w["wkT"], None, w["wk_bf"].tw, D, D),
        (w["wvT"], None, w["wv_bf"].tw, D, D),
        (wg_f, None, wg_bf, 8 * D, 4096),
        (wu_f, None, wu_bf, 8 * D, 4096),
        (wd_f, None, wd_bf, 8 * 4096, D),
    ]
    if getattr(b, "XN", None) is None:
        b.XN = b.sb("XN", [128, NFC * b.T], BF16)
    XN = b.XN
    emit_precast(b, CON, XN[:, 0:2048], jobs)
    ARB = get_arb(b)
    rwT_sb = ARB.take_f32(NFC * 8)
    srw = b.dma_sem("ini_l1rw")
    for fc in range(NFC):
        b.load(rwT_sb[:, fc * 8:(fc + 1) * 8], rwT[fc * 128:(fc + 1) * 128, :], srw)
    csb = ARB.take_f32(8)
    b.tensor.wait(srw, srw.v)
    tvc = None
    for fc in range(NFC):
        tvc = b.tensor.op("matmul", P[6][0:1, 0:8], CON["ones_col_f32"][:],
                          rwT_sb[:, fc * 8:(fc + 1) * 8],
                          start=(fc == 0), stop=(fc == NFC - 1), inc=b.ts)
    b.vector.wait(b.ts, tvc)
    vvc = b.vector.op("tensor_copy", CON["statv"][0:1, 0:8], P[6][0:1, 0:8],
                      inc=b.vs)
    b.tensor.wait(b.vs, vvc)
    tvc = b.tensor.op("matmul", P[5][0:128, 0:8], CON["ones_row_f32"][0:1, :],
                      CON["statv"][0:1, 0:8], start=True, stop=True, inc=b.ts)
    b.vector.wait(b.ts, tvc)
    b.vector.op("tensor_copy", csb[:], P[5][0:128, 0:8], inc=b.vs)
    CWT8 = ARB.take_bf(T)[0:8, :]
    hook = make_router_hook(b, CON, P, w, CWT8, rwT_sb, csb)
    AR = b.AR if getattr(b, "AR", None) is not None else Arena(b, "AR", 44032)
    b.AR = AR
    emit_ln(b, xin, XN, CON, P, tc_hook=hook)
    emit_barrier(b)
    emit_l1_ffn(b, xin, x1, XN, CON, P, w, AR, CWT8)
    emit_barrier(b)
    emit_l1_attn(b, x1, xout, XN, CON, P, w, AR)


def load_oh8(b, CON):
    if getattr(b, "_oh8", None) is not None:
        return b._oh8
    t = b.sb("oh8", [8, 8 * 128], BF16)
    s = b.dma_sem("ini_oh8")
    b.load(t[:], b.dram("oh8_in", [8, 8 * 128], BF16), s)
    b.tensor.wait(s, s.v)
    b._oh8 = t
    return t


def alloc_common(b):
    CON = {}
    CON["ones_col_bf"] = b.sb("ones_col", [128, 1], BF16)
    CON["ones_col_f32"] = b.sb("ones_col32", [128, 1], F32)
    CON["rtmp"] = b.sb("rtmp", [128, 40], F32)
    CON["ones_row_bf"] = b.sb("ones_row", [1, 128], BF16)
    CON["ones_row_f32"] = b.sb("ones_row32", [1, 128], F32)
    CON["ident_bf"] = b.sb("ident_bf", [128, 128], BF16)
    CON["scratch_bf"] = b.sb("scratch_bf", [128, 2 * TCH], BF16)
    CON["scratch_bf2"] = b.sb("scratch_bf2", [128, 2 * TCH], BF16)
    CON["statv"] = b.sb("statv", [1, 4 * TCH], F32)
    CON["statv_bf"] = b.sb("statv_bf", [1, 2 * TCH], BF16)
    CON["tmp_f32"] = b.sb("tmp_f32", [128, 2 * TCH], F32)
    CON["xtile"] = b.sb("xtile", [128, 4 * TCH], F32)
    CON["lnx"] = b.sb("lnx", [128, NFC * TCH], F32)
    CON["rv"] = b.sb("con_rv", [1, TCH], F32)
    ident = b.dram("ident_in", [128, 128], F32)
    sini = b.dma_sem("ini_id")
    b.load(CON["xtile"][:, 0:128], ident, sini)
    b.vector.wait(sini, sini.v)
    b.vector.op("tensor_copy", CON["ident_bf"][:], CON["xtile"][:, 0:128])
    b.vector.op("memset", CON["ones_col_bf"][:], 1.0)
    b.vector.op("memset", CON["ones_col_f32"][:], 1.0)
    b.vector.op("memset", CON["ones_row_bf"][:], 1.0)
    vv = b.vector.op("memset", CON["ones_row_f32"][:], 1.0, inc=b.vs)
    b.tensor.wait(b.vs, vv)
    b.scalar.wait(b.vs, vv)
    return CON


def get_arb(b):
    if getattr(b, "ARB", None) is None:
        b.ARB = Arena(b, "ARB", 4992)
    b.ARB.reset()
    return b.ARB


def load_colvec_arb(b, arb, n_chunks, dram_ap, sem_name):
    t = arb.take_f32(n_chunks)
    s = b.dma_sem(sem_name)
    b.load(t, dram_ap, s)
    b.scalar.wait(s, s.v)
    b.vector.wait(s, s.v)
    return t


def load_colvec(b, name, n_chunks, dram_ap):
    t = b.sb(name, [128, n_chunks], F32)
    sini = b.dma_sem("ini_" + name)
    b.load(t[:], dram_ap, sini)
    b.scalar.wait(sini, sini.v)
    b.vector.wait(sini, sini.v)
    return t


def emit_block0(b, CON, P, PT, xin, xout):
    T = b.T
    x1 = b.dram("x1scr", [D, T], F32, kind="Internal")
    abuf = b.dram("abuf0", [D, T], BF16, kind="Internal")
    w = {
        "wqT": b.dram("L0_wqT", [D, D], F32),
        "wkT": b.dram("L0_wkT", [D, D], F32),
        "wvT": b.dram("L0_wvT", [D, D], F32),
        "woT": b.dram("L0_woT", [D, D], F32),
        "fc1_q": b.dram("L0_fc1_q", [D, 4096], I32),
        "fc1_se": b.dram("L0_fc1_se", [D, 4096], F32),
        "fc2_q": b.dram("L0_fc2_q", [4096, D], I32),
        "fc2_se": b.dram("L0_fc2_se", [4096, D], F32),
    }
    wq_bf = TW(b, "L0_wq_bf", D, D)
    wk_bf = TW(b, "L0_wk_bf", D, D)
    wv_bf = TW(b, "L0_wv_bf", D, D)
    wo_bf = TW(b, "L0_wo_bf", D, D)
    w1_bf = TW(b, "L0_w1_bf", D, 4096)
    w2_bf = TW(b, "L0_w2_bf", 4096, D)
    w["w1bf"] = w1_bf.view()
    w["w2bf"] = w2_bf.view()
    w["g1c"] = load_colvec(b, "g1c", NFC, b.dram("L0_g1c", [128, NFC], F32))
    w["b1c"] = load_colvec(b, "b1c", NFC, b.dram("L0_b1c", [128, NFC], F32))
    w["fb1c"] = load_colvec(b, "fb1c", 32, b.dram("L0_fb1c", [128, 32], F32))
    w["fb2c"] = load_colvec(b, "fb2c", NFC, b.dram("L0_fb2c", [128, NFC], F32))
    w["boc"] = load_colvec(b, "boc", NFC, b.dram("L0_boc", [128, NFC], F32))
    if getattr(b, "XN", None) is None:
        b.XN = b.sb("XN", [128, NFC * b.T], BF16)
    XN = b.XN
    AR = b.AR if getattr(b, "AR", None) is not None else Arena(b, "AR", 44032)
    b.AR = AR
    emit_precast(b, CON, XN[:, 0:2048], [
        (w["fc1_q"], w["fc1_se"], w1_bf, D, 4096),
        (w["fc2_q"], w["fc2_se"], w2_bf, 4096, D),
        (w["wqT"], None, wq_bf, D, D),
        (w["wkT"], None, wk_bf, D, D),
        (w["wvT"], None, wv_bf, D, D),
        (w["woT"], None, wo_bf, D, D),
    ])
    emit_ln(b, xin, XN, CON, P)
    emit_barrier(b)
    emit_l0_ffn(b, xin, x1, XN, CON, P, w, AR)
    emit_barrier(b)
    emit_mha(b, x1, xout, XN, CON, P, PT, w, wq_bf.view(), wk_bf.view(),
             wv_bf.view(), wo_bf.view(), w["boc"], abuf, AR)


def host_prep_l0(inputs):
    C = np.ascontiguousarray
    def colvec(v):
        return C(np.asarray(v).reshape(-1, 128).T)
    return {
        "L0_wqT": C(np.asarray(inputs["L0_attn_Wq"]).T),
        "L0_wkT": C(np.asarray(inputs["L0_attn_Wk"]).T),
        "L0_wvT": C(np.asarray(inputs["L0_attn_Wv"]).T),
        "L0_woT": C(np.asarray(inputs["L0_attn_Wo"]).T),
        "L0_boc": colvec(inputs["L0_attn_bo"]),
        "L0_g1c": colvec(inputs["L0_ffn_ln_gamma"]),
        "L0_b1c": colvec(inputs["L0_ffn_ln_beta"]),
        "L0_fc1_q": C(np.asarray(inputs["L0_ffn_fc1_q"])),
        "L0_fc1_se": C(np.repeat(np.asarray(inputs["L0_ffn_fc1_scales"]), 128, axis=0)),
        "L0_fb1c": colvec(inputs["L0_ffn_fc1_bias"]),
        "L0_fc2_q": C(np.asarray(inputs["L0_ffn_fc2_q"])),
        "L0_fc2_se": C(np.repeat(np.asarray(inputs["L0_ffn_fc2_scales"]), 128, axis=0)),
        "L0_fb2c": colvec(inputs["L0_ffn_fc2_bias"]),
        "ident_in": np.eye(128, dtype=np.float32),
    }


def build_block0(T=4096):
    nc = bass.Bass("TRN2", target_bir_lowering=False, debug=False)
    es = ExitStack()
    es.__enter__()
    b = Builder(nc, es, T)
    xin = b.dram("xin", [D, T], F32)
    xout = b.dram("xout", [D, T], F32, kind="ExternalOutput")
    P = [b.ps(f"P{i}", [128, TCH], F32) for i in range(7)]
    PT = b.ps("PT", [128, 1024], BF16)
    b.PT_BF = PT
    CON = alloc_common(b)
    emit_block0(b, CON, P, PT, xin, xout)
    b.finish()
    return nc, es



def emit_l2_qprep(b, XN, CON, P, w, qn_dram, AR):
    T, NTC = b.T, b.NTC
    AR.reset()
    QTC = AR.take_bf(NFC * TCH)
    wstrq = WStream(b, "l2q_w", NFC, 128, arena=AR)
    pq = [P[4], P[5]]
    ab, cb = P[2], P[3]
    evq = [None, None]
    sqs = b.dma_sem("l2_qn_st")
    for tc in range(NTC):
        t0 = tc * TCH
        b.vector.wait(sqs, sqs.v)
        for oc in range(NFC):
            tiles, wv_ = wstrq.load_block(w["wq_bf"], oc * 128)
            b.tensor.wait(*wv_)
            pb = oc % 2
            if evq[pb]:
                b.tensor.wait(*evq[pb])
            tv = emit_xw_to_psum(
                b, tiles, lambda k: XN[:, k * T + t0: k * T + t0 + TCH],
                pq[pb][:, :], NFC, inc=b.ts)
            wstrq.consumed(tv)
            b.vector.wait(b.ts, tv)
            ev = b.vector.op("tensor_scalar", QTC[:, oc * TCH:(oc + 1) * TCH],
                             pq[pb][:, :], w["bqc"][:, oc:oc + 1], None,
                             op0=ALU.add, inc=b.vs)
            evq[pb] = (b.vs, ev)
        ms_v = b.vector.milestone(b.vs)
        b.tensor.wait(*ms_v)
        b.scalar.wait(*ms_v)
        tv = emit_stats(b, CON, P,
                        lambda fc: QTC[:, fc * TCH:(fc + 1) * TCH],
                        NFC, 1e-5)
        b.vector.wait(b.ts, tv)
        b.sync.wait(sqs, sqs.v)
        for fc in range(NFC):
            tmp = CON["tmp_f32"]
            b.vector.op("tensor_tensor", tmp[:, 0:TCH],
                        QTC[:, fc * TCH:(fc + 1) * TCH], ab[:, :], op=ALU.mult)
            b.vector.op("tensor_tensor", tmp[:, TCH:2 * TCH], tmp[:, 0:TCH],
                        cb[:, :], op=ALU.add)
            vv = b.vector.op("tensor_scalar", QTC[:, fc * TCH:(fc + 1) * TCH],
                             tmp[:, TCH:2 * TCH], w["lngc"][:, fc:fc + 1],
                             w["lnbc"][:, fc:fc + 1], op0=ALU.mult, op1=ALU.add,
                             inc=b.vs)
            b.sync.wait(b.vs, vv)
            b.store(qn_dram[fc * 128:(fc + 1) * 128, t0:t0 + TCH],
                    QTC[:, fc * TCH:(fc + 1) * TCH], sqs)
        ms2 = b.vector.milestone(b.vs)
        b.tensor.wait(*ms2)


def emit_block2(b, CON, P, PT, xin, xout):
    T = b.T
    x1 = b.dram("x1scr2", [D, T], F32, kind="Internal")
    abuf = b.dram("abuf2", [D, T], BF16, kind="Internal")
    qn_dram = b.dram("l2_qn", [D, T], BF16, kind="Internal")
    w = {}
    names = ["Wq", "Wk", "Wv", "Wo"]
    f32_in = {}
    for n in names:
        f32_in[n] = b.dram("L2_" + n, [D, D], F32)
        w[n.lower() + "_bf"] = TW(b, "L2_" + n + "_bf", D, D).view()
    w["bqc"] = load_colvec(b, "l2bqc", NFC, b.dram("L2_bqc", [128, NFC], F32))
    w["bkc"] = load_colvec(b, "l2bkc", NFC, b.dram("L2_bkc", [128, NFC], F32))
    w["bvc"] = load_colvec(b, "l2bvc", NFC, b.dram("L2_bvc", [128, NFC], F32))
    w["boc"] = load_colvec(b, "l2boc", NFC, b.dram("L2_boc", [128, NFC], F32))
    w["lngc"] = load_colvec(b, "l2lngc", NFC, b.dram("L2_lngc", [128, NFC], F32))
    w["lnbc"] = load_colvec(b, "l2lnbc", NFC, b.dram("L2_lnbc", [128, NFC], F32))
    shw1_f = b.dram("L2_shw1T", [D, 4096], F32)
    shw2_f = b.dram("L2_shw2T", [4096, D], F32)
    spw1_f = b.dram("L2_spw1T", [8 * D, 4096], F32)
    spw2_f = b.dram("L2_spw2T", [8 * 4096, D], F32)
    shw1_bf = TW(b, "L2_shw1_bf", D, 4096)
    shw2_bf = TW(b, "L2_shw2_bf", 4096, D)
    spw1_bf = TW(b, "L2_spw1_bf", D, 4096, nmat=8)
    spw2_bf = TW(b, "L2_spw2_bf", 4096, D, nmat=8)
    w["sh_w1"] = shw1_bf.view()
    w["sh_w2"] = shw2_bf.view()
    w["sp_w1"] = [spw1_bf[e] for e in range(8)]
    w["sp_w2"] = [spw2_bf[e] for e in range(8)]
    w["sh_b1c"] = load_colvec(b, "l2shb1", 32, b.dram("L2_shb1c", [128, 32], F32))
    w["sh_b2c"] = load_colvec(b, "l2shb2", NFC, b.dram("L2_shb2c", [128, NFC], F32))
    ARB = get_arb(b)
    spb1 = ARB.take_f32(8 * 32)
    spb2 = ARB.take_f32(8 * NFC)
    scv = b.dma_sem("ini_l2spb")
    b.load(spb1, b.dram("L2_spb1c", [128, 8 * 32], F32), scv)
    b.load(spb2, b.dram("L2_spb2c", [128, 8 * NFC], F32), scv)
    b.scalar.wait(scv, scv.v)
    b.vector.wait(scv, scv.v)
    w["sp_b1c"] = [spb1[:, e * 32:(e + 1) * 32] for e in range(8)]
    w["sp_b2c"] = [spb2[:, e * NFC:(e + 1) * NFC] for e in range(8)]
    rwT = b.dram("L2_rwT", [D, 8], F32)
    rbb = ARB.take_f32(8)
    srb = b.dma_sem("ini_l2rb")
    b.load(rbb[:], b.dram("L2_rbb", [128, 8], F32), srb)
    b.vector.wait(srb, srb.v)
    w["oh8"] = load_oh8(b, CON)
    w["n_experts"] = 8
    jobs = [(f32_in[n], None, w[n.lower() + "_bf"].tw, D, D) for n in names]
    jobs += [
        (shw1_f, None, shw1_bf, D, 4096),
        (shw2_f, None, shw2_bf, 4096, D),
        (spw1_f, None, spw1_bf, 8 * D, 4096),
        (spw2_f, None, spw2_bf, 8 * 4096, D),
    ]
    if getattr(b, "XN", None) is None:
        b.XN = b.sb("XN", [128, NFC * b.T], BF16)
    XN = b.XN
    AR = b.AR if getattr(b, "AR", None) is not None else Arena(b, "AR", 44032)
    b.AR = AR
    emit_precast(b, CON, XN[:, 0:2048], jobs)
    rwT_sb = ARB.take_f32(NFC * 8)
    srw = b.dma_sem("ini_l2rw")
    for fc in range(NFC):
        b.load(rwT_sb[:, fc * 8:(fc + 1) * 8], rwT[fc * 128:(fc + 1) * 128, :], srw)
    csb = ARB.take_f32(8)
    b.tensor.wait(srw, srw.v)
    tvc = None
    for fc in range(NFC):
        tvc = b.tensor.op("matmul", P[6][0:1, 0:8], CON["ones_col_f32"][:],
                          rwT_sb[:, fc * 8:(fc + 1) * 8],
                          start=(fc == 0), stop=(fc == NFC - 1), inc=b.ts)
    b.vector.wait(b.ts, tvc)
    vvc = b.vector.op("tensor_copy", CON["statv"][0:1, 0:8], P[6][0:1, 0:8],
                      inc=b.vs)
    b.tensor.wait(b.vs, vvc)
    tvc = b.tensor.op("matmul", P[5][0:128, 0:8], CON["ones_row_f32"][0:1, :],
                      CON["statv"][0:1, 0:8], start=True, stop=True, inc=b.ts)
    b.vector.wait(b.ts, tvc)
    b.vector.op("tensor_copy", csb[:], P[5][0:128, 0:8], inc=b.vs)
    CWT8 = ARB.take_bf(T)[0:8, :]
    hook = make_router_hook(b, CON, P, w, CWT8, rwT_sb, csb, rb_b=rbb)
    emit_ln(b, xin, XN, CON, P, tc_hook=hook)
    emit_barrier(b)
    emit_l2_qprep(b, XN, CON, P, w, qn_dram, AR)
    emit_barrier(b)
    emit_l1_ffn(b, xin, x1, XN, CON, P, w, AR, CWT8, has_shared=True, gelu=True)
    emit_barrier(b)
    emit_mha(b, x1, xout, XN, CON, P, PT, w, w["wq_bf"], w["wk_bf"], w["wv_bf"],
             w["wo_bf"], w["boc"], abuf, AR, add_q=True, bkc=w["bkc"],
             bvc=w["bvc"], q_src=qn_dram)


def build_block2(T=4096):
    nc = bass.Bass("TRN2", target_bir_lowering=False, debug=False)
    es = ExitStack()
    es.__enter__()
    b = Builder(nc, es, T)
    xin = b.dram("xin", [D, T], F32)
    xout = b.dram("xout", [D, T], F32, kind="ExternalOutput")
    P = [b.ps(f"P{i}", [128, TCH], F32) for i in range(7)]
    PT = b.ps("PT", [128, 1024], BF16)
    b.PT_BF = PT
    CON = alloc_common(b)
    emit_block2(b, CON, P, PT, xin, xout)
    b.finish()
    return nc, es


def host_prep_l2(inputs):
    C = np.ascontiguousarray
    import ml_dtypes

    def colvec(v):
        return C(np.asarray(v).reshape(-1, 128).T)
    oh8 = np.zeros((8, 8 * 128), dtype=ml_dtypes.bfloat16)
    for e in range(8):
        oh8[e, e * 128:(e + 1) * 128] = 1.0
    sw1 = np.asarray(inputs["L2_ffn_spec_w1"])
    sw2 = np.asarray(inputs["L2_ffn_spec_w2"])
    return {
        "L2_Wq": C(np.asarray(inputs["L2_attn_Wq"])),
        "L2_Wk": C(np.asarray(inputs["L2_attn_Wk"])),
        "L2_Wv": C(np.asarray(inputs["L2_attn_Wv"])),
        "L2_Wo": C(np.asarray(inputs["L2_attn_Wo"])),
        "L2_bqc": colvec(inputs["L2_attn_bq"]),
        "L2_bkc": colvec(inputs["L2_attn_bk"]),
        "L2_bvc": colvec(inputs["L2_attn_bv"]),
        "L2_boc": colvec(inputs["L2_attn_bo"]),
        "L2_lngc": colvec(inputs["L2_attn_ln_g"]),
        "L2_lnbc": colvec(inputs["L2_attn_ln_b"]),
        "L2_shw1T": C(np.asarray(inputs["L2_ffn_shared_w1"]).T),
        "L2_shw2T": C(np.asarray(inputs["L2_ffn_shared_w2"]).T),
        "L2_spw1T": C(sw1.transpose(0, 2, 1).reshape(8 * D, 4096)),
        "L2_spw2T": C(sw2.transpose(0, 2, 1).reshape(8 * 4096, D)),
        "L2_shb1c": colvec(inputs["L2_ffn_shared_b1"]),
        "L2_shb2c": colvec(inputs["L2_ffn_shared_b2"]),
        "L2_spb1c": C(np.concatenate(
            [np.asarray(inputs["L2_ffn_spec_b1"])[e].reshape(32, 128).T
             for e in range(8)], axis=1)),
        "L2_spb2c": C(np.concatenate(
            [np.asarray(inputs["L2_ffn_spec_b2"])[e].reshape(NFC, 128).T
             for e in range(8)], axis=1)),
        "L2_rwT": C(np.asarray(inputs["L2_ffn_router_w"]).T),
        "L2_rbb": C(np.broadcast_to(np.asarray(inputs["L2_ffn_router_b"]),
                                    (128, 8)).copy()),
        "oh8_in": oh8,
        "ident_in": np.eye(128, dtype=np.float32),
    }



def emit_l3_ffn(b, xdram, x1dram, XN, CON, P, w, AR):
    T, NTC = b.T, b.NTC
    AR.reset()
    PB = AR.take_bf(2 * TCH)
    HBUF = AR.take_bf(32 * TCH)
    QB = AR.take_bf(2 * TCH)
    u1S = WStream(b, "l3_u1", NFC, 128, arena=AR)
    v1S = WStream(b, "l3_v1", 2, 128, arena=AR)
    u2S = WStream(b, "l3_u2", 32, 128, arena=AR)
    v2S = WStream(b, "l3_v2", 2, 128, arena=AR)
    xp = XPipe(b, CON, "f3")
    pA, pB_ = P[4], P[5]
    ev = [None, None]
    for tc in range(NTC):
        t0 = tc * TCH
        for oc in range(2):
            tiles, wv_ = u1S.load_block(w["u1_bf"], oc * 128)
            b.tensor.wait(*wv_)
            pb = oc % 2
            if ev[pb]:
                b.tensor.wait(*ev[pb])
            tv = emit_xw_to_psum(b, tiles,
                                 lambda k: XN[:, k * T + t0: k * T + t0 + TCH],
                                 (pA if pb == 0 else pB_)[:, :], NFC, inc=b.ts)
            u1S.consumed(tv)
            b.vector.wait(b.ts, tv)
            e_ = b.vector.op("tensor_copy", PB[:, oc * TCH:(oc + 1) * TCH],
                             (pA if pb == 0 else pB_)[:, :], inc=b.vs)
            ev[pb] = (b.vs, e_)
        msv = b.vector.milestone(b.vs)
        b.tensor.wait(*msv)
        for oc in range(32):
            tiles, wv_ = v1S.load_block(w["v1_bf"], oc * 128)
            b.tensor.wait(*wv_)
            pb = oc % 2
            if ev[pb]:
                b.tensor.wait(*ev[pb])
            tv = emit_xw_to_psum(b, tiles,
                                 lambda k: PB[:, k * TCH:(k + 1) * TCH],
                                 (pA if pb == 0 else pB_)[:, :], 2, inc=b.ts)
            v1S.consumed(tv)
            b.scalar.wait(b.ts, tv)
            e_ = b.scalar.op("activation", HBUF[:, oc * TCH:(oc + 1) * TCH],
                             (pA if pb == 0 else pB_)[:, :], AF.Gelu_apprx_tanh,
                             bias=w["b1c"][:, oc:oc + 1], inc=b.ss)
            ev[pb] = (b.ss, e_)
        for le in ev:
            if le:
                b.tensor.wait(*le)
        for oc in range(2):
            tiles, wv_ = u2S.load_block(w["u2_bf"], oc * 128)
            b.tensor.wait(*wv_)
            pb = oc % 2
            if ev[pb]:
                b.tensor.wait(*ev[pb])
            tv = emit_xw_to_psum(b, tiles,
                                 lambda k: HBUF[:, k * TCH:(k + 1) * TCH],
                                 (pA if pb == 0 else pB_)[:, :], 32, inc=b.ts)
            u2S.consumed(tv)
            b.vector.wait(b.ts, tv)
            e_ = b.vector.op("tensor_copy", QB[:, oc * TCH:(oc + 1) * TCH],
                             (pA if pb == 0 else pB_)[:, :], inc=b.vs)
            ev[pb] = (b.vs, e_)
        msv = b.vector.milestone(b.vs)
        b.tensor.wait(*msv)
        for oc in range(NFC):
            tiles, wv_ = v2S.load_block(w["v2_bf"], oc * 128)
            b.tensor.wait(*wv_)
            pb = oc % 2
            if ev[pb]:
                b.tensor.wait(*ev[pb])
            pp = pA if pb == 0 else pB_
            tv = emit_xw_to_psum(b, tiles,
                                 lambda k: QB[:, k * TCH:(k + 1) * TCH],
                                 pp[:, :], 2, inc=b.ts)
            v2S.consumed(tv)
            xpp = xp.begin(xdram[oc * 128:(oc + 1) * 128, t0:t0 + TCH])
            b.vector.wait(b.ts, tv)
            ev1 = b.vector.op("tensor_scalar", CON["tmp_f32"][:, 0:TCH], pp[:, :],
                              w["b2c"][:, oc:oc + 1], None, op0=ALU.add, inc=b.vs)
            ev[pb] = (b.vs, ev1)
            xp.wait_ld(xpp)
            xp.wait_out_free(xpp)
            vv = b.vector.op("tensor_tensor", xp.out_ap(xpp), xp.ld_ap(xpp),
                             CON["tmp_f32"][:, 0:TCH], op=ALU.add, inc=b.vs)
            xp.end(xpp, x1dram[oc * 128:(oc + 1) * 128, t0:t0 + TCH], vv)


def emit_l3_attn(b, x1dram, xodram, XN, CON, P, w, AR):
    T, NTC = b.T, b.NTC
    DI = 2048
    NDC = DI // 128
    xs_dram = w["xs_scr"]
    xsc_dram = w["xsc_scr"]
    res_dram = w["res_scr"]
    dt_dram = w["dt_scr"]
    u_dram = w["u_scr"]
    y_dram = w["y_scr"]
    weff_tw = w["weff_tw"]

    AR.reset()
    RNB = AR.take_bf(NFC * TCH)
    winS = WStream(b, "l3_win", NFC, 128, arena=AR)
    ab = P[2]
    pA, pB_ = P[4], P[5]
    ev = [None, None]
    sxr = b.dma_sem("l3_xr")
    for tc in range(NTC):
        t0 = tc * TCH
        b.scalar.wait(sxr, sxr.v)
        b.vector.wait(sxr, sxr.v)
        tv = emit_rms_stats(b, CON, P,
                            lambda fc: XN[:, fc * T + t0: fc * T + t0 + TCH],
                            NFC, 1e-5)
        b.vector.wait(b.ts, tv)
        for fc in range(NFC):
            tmp = CON["tmp_f32"]
            b.vector.op("tensor_tensor", tmp[:, 0:TCH],
                        XN[:, fc * T + t0: fc * T + t0 + TCH], ab[:, :],
                        op=ALU.mult)
            b.vector.op("tensor_scalar", RNB[:, fc * TCH:(fc + 1) * TCH],
                        tmp[:, 0:TCH], w["rmsc"][:, fc:fc + 1], None,
                        op0=ALU.mult)
        msv = b.vector.milestone(b.vs)
        b.tensor.wait(*msv)
        for oc in range(32):
            tiles, wv_ = winS.load_block(w["win_bf"], oc * 128)
            b.tensor.wait(*wv_)
            pb = oc % 2
            if ev[pb]:
                b.tensor.wait(*ev[pb])
            pp = pA if pb == 0 else pB_
            tv = emit_xw_to_psum(b, tiles,
                                 lambda k: RNB[:, k * TCH:(k + 1) * TCH],
                                 pp[:, :], NFC, inc=b.ts)
            winS.consumed(tv)
            b.vector.wait(b.ts, tv)
            b.vector.wait(sxr, sxr.v)
            e_ = b.vector.op("tensor_copy", CON["scratch_bf"][:, (pb) * TCH:(pb + 1) * TCH],
                             pp[:, :], inc=b.vs)
            ev[pb] = (b.vs, e_)
            b.sync.wait(b.vs, e_)
            dstd = xs_dram if oc < 16 else res_dram
            row0 = (oc % 16) * 128
            b.store(dstd[row0:row0 + 128, t0:t0 + TCH],
                    CON["scratch_bf"][:, pb * TCH:(pb + 1) * TCH], sxr)

    emit_barrier(b)

    AR.reset()
    CT = AR.take_bf(2 * (TCH + 4))
    CTB = AR.take_bf(2 * TCH)
    sconv = b.dma_sem("l3_cv")
    sconv_st = b.dma_sem("l3_cvs")
    conv_cons = [None, None]
    for dc in range(NDC):
        for tc in range(NTC):
            t0 = tc * TCH
            sl = (dc * NTC + tc) % 2
            ctile = CT[:, sl * (TCH + 4): sl * (TCH + 4) + TCH + 3]
            cbt = CTB[:, sl * TCH:(sl + 1) * TCH]
            if conv_cons[sl]:
                b.sync.wait(*conv_cons[sl])
            ld = b.dma_sem("l3_cvl%d" % sl)
            if tc == 0:
                b.load(ctile[:, 0:3], w["zeros3"][0:128, 0:3], ld)
                b.load(ctile[:, 3:3 + TCH],
                       xs_dram[dc * 128:(dc + 1) * 128, t0:t0 + TCH], ld)
            else:
                b.load(ctile[:, 0:3 + TCH],
                       xs_dram[dc * 128:(dc + 1) * 128, t0 - 3:t0 + TCH], ld)
            b.vector.wait(ld, ld.v)
            cv = ctile
            t1_ = CON["tmp_f32"][:, 0:TCH]
            b.vector.op("tensor_scalar", t1_, cv[:, 0:TCH],
                        w["cwc"][:, dc * 4:dc * 4 + 1], None, op0=ALU.mult)
            b.vector.op("scalar_tensor_tensor", t1_, cv[:, 1:1 + TCH],
                        w["cwc"][:, dc * 4 + 1:dc * 4 + 2], t1_,
                        op0=ALU.mult, op1=ALU.add)
            b.vector.op("scalar_tensor_tensor", t1_, cv[:, 2:2 + TCH],
                        w["cwc"][:, dc * 4 + 2:dc * 4 + 3], t1_,
                        op0=ALU.mult, op1=ALU.add)
            vv = b.vector.op("scalar_tensor_tensor", t1_, cv[:, 3:3 + TCH],
                            w["cwc"][:, dc * 4 + 3:dc * 4 + 4], t1_,
                            op0=ALU.mult, op1=ALU.add, inc=b.vs)
            conv_cons[sl] = (b.vs, vv)
            b.scalar.wait(b.vs, vv)
            b.scalar.wait(sconv_st, sconv_st.v)
            sv = b.scalar.op("activation", cbt, t1_, AF.Silu,
                             bias=w["cbc"][:, dc:dc + 1], inc=b.ss)
            b.sync.wait(b.ss, sv)
            b.store(xsc_dram[dc * 128:(dc + 1) * 128, t0:t0 + TCH], cbt, sconv_st)

    emit_barrier(b)

    AR.reset()
    wxS = WStream(b, "l3_wx", 16, 128, arena=AR)
    wdtS = WStream(b, "l3_wdt", 16, TCH, arena=AR)
    WEC = AR.take_bf(TCH)
    evw = [None, None]
    for ic in range(16):
        tiles, wv_ = wxS.load_block(w["wxT_bf"], ic * 128)
        b.tensor.wait(*wv_)
        for jc in range(4):
            rtiles, wv2 = wdtS.load_block(w["wdt_bf"], jc * TCH, ncols=TCH)
            b.tensor.wait(*wv2)
            pb = jc % 2
            if evw[pb]:
                b.tensor.wait(*evw[pb])
            pp = pA if pb == 0 else pB_
            tv = emit_xw_to_psum(b, tiles,
                                 lambda k: rtiles[k],
                                 pp[:, :], 16, inc=b.ts)
            wdtS.consumed(tv)
            b.vector.wait(b.ts, tv)
            e_ = b.vector.op("tensor_copy", WEC[:, 0:TCH], pp[:, :], inc=b.vs)
            evw[pb] = (b.vs, e_)
            b.sync.wait(b.vs, e_)
            b.store(weff_tw.dst(0, ic, jc * TCH, TCH), WEC[:, 0:TCH],
                    b.dma_sem("l3_wes"))
            b.vector.wait(b.dma_sem("l3_wes"), b.dma_sem("l3_wes").v)
        wxS.consumed(b.ts.v)

    emit_barrier(b)

    AR.reset()
    XSS = AR.take_bf(16 * TCH)
    weffS = WStream(b, "l3_we", 16, 128, arena=AR)
    wx2S = WStream(b, "l3_wx2", 16, 64, arena=AR)
    DTT = AR.take_bf(2 * TCH)
    BCB = b.ARB.take_bf(b.T)[0:64, :]
    sxss = b.dma_sem("l3_xss")
    sdt = b.dma_sem("l3_dts")
    evd = [None, None]
    tiles2, wv2 = wx2S.load_block(w["wx2_bf"], 0, ncols=64)
    for tc in range(NTC):
        t0 = tc * TCH
        b.sync.wait(b.ts, b.ts.v)
        for k in range(16):
            b.load(XSS[:, k * TCH:(k + 1) * TCH],
                   xsc_dram[k * 128:(k + 1) * 128, t0:t0 + TCH], sxss)
        b.tensor.wait(sxss, sxss.v)
        for oc in range(16):
            tiles, wv_ = weffS.load_block(weff_tw.view(), oc * 128)
            b.tensor.wait(*wv_)
            pb = oc % 2
            if evd[pb]:
                b.tensor.wait(*evd[pb])
            pp = pA if pb == 0 else pB_
            tv = emit_xw_to_psum(b, tiles,
                                 lambda k: XSS[:, k * TCH:(k + 1) * TCH],
                                 pp[:, :], 16, inc=b.ts)
            weffS.consumed(tv)
            b.scalar.wait(b.ts, tv)
            t1_ = CON["tmp_f32"][:, 0:TCH]
            t2_ = CON["tmp_f32"][:, TCH:2 * TCH]
            b.scalar.op("activation", t1_, pp[:, :], AF.Abs)
            b.scalar.op("activation", t1_, t1_, AF.Exp, scale=-1.0)
            b.scalar.op("activation", t1_, t1_, AF.Ln, bias=1.0)
            sv = b.scalar.op("activation", t2_, pp[:, :], AF.Relu, inc=b.ss)
            b.vector.wait(b.ss, sv)
            b.vector.op("tensor_tensor", t2_, t2_, t1_, op=ALU.add)
            b.vector.wait(sdt, sdt.v)
            e_ = b.vector.op("tensor_copy", DTT[:, 0:TCH], t2_, inc=b.vs)
            evd[pb] = (b.vs, e_)
            b.sync.wait(b.vs, e_)
            b.store(dt_dram[oc * 128:(oc + 1) * 128, t0:t0 + TCH], DTT[:, 0:TCH],
                    sdt)
            e2 = b.vector.op("tensor_tensor", DTT[:, TCH:2 * TCH], DTT[:, 0:TCH],
                             XSS[:, oc * TCH:(oc + 1) * TCH], op=ALU.mult,
                             inc=b.vs)
            evd[pb] = (b.vs, e2)
            b.sync.wait(b.vs, e2)
            b.store(u_dram[oc * 128:(oc + 1) * 128, t0:t0 + TCH],
                    DTT[:, TCH:2 * TCH], sdt)
        b.tensor.wait(*wv2)
        tvbc = emit_xw_to_psum(b, tiles2,
                               lambda k: XSS[:, k * TCH:(k + 1) * TCH],
                               P[6][0:64, :], 16, inc=b.ts)
        b.vector.wait(b.ts, tvbc)
        b.vector.op("tensor_copy", BCB[0:64, t0:t0 + TCH], P[6][0:64, :],
                    inc=b.vs)

    emit_barrier(b)

    AR.reset()
    _TH2 = b.T // max(1, b.T // 2048)
    BREP = AR.take_bf(b.T)
    CREP = AR.take_bf(b.T)
    APL = AR.take_f32(_TH2)
    DTRP = AR.take_bf(_TH2)
    URP = AR.take_bf(_TH2)
    BPL = AR.take_f32(_TH2)
    HPL = AR.take_f32(_TH2)
    PPL = AR.take_bf(_TH2)
    RESS = AR.take_bf(b.T)
    RSIL = AR.take_bf(b.T)
    SMAT_sb = AR.take_bf(16 * 128)
    QMAT_sb = AR.take_bf(16 * 128)
    RBS_sb = b.ARB.take_bf(128)[0:64, :]
    SCAR = b.sb("l3_scar", [128, 16], F32)
    YT = AR.take_bf(TCH)
    scm = b.dma_sem("ini_l3sc")
    b.load(RBS_sb[:], w["rbs_in"], scm)
    for i in range(16):
        b.load(SMAT_sb[:, i * 128:(i + 1) * 128],
               w["smat_in"][i * 128:(i + 1) * 128, :], scm)
        b.load(QMAT_sb[:, i * 128:(i + 1) * 128],
               w["qmat_in"][i * 128:(i + 1) * 128, :], scm)
    b.tensor.wait(scm, scm.v)
    ASTATE = b.ARB.take_f32(256)
    sas = b.dma_sem("ini_l3as")
    b.load(ASTATE[:], w["alog_in"], sas)
    b.scalar.wait(sas, sas.v)
    sv = b.scalar.op("activation", ASTATE[:], ASTATE[:], AF.Exp, inc=b.ss)
    b.vector.wait(b.ss, sv)
    vv = b.vector.op("tensor_scalar", ASTATE[:], ASTATE[:], -1.0, None,
                     op0=ALU.mult, inc=b.vs)
    b.scalar.wait(b.vs, vv)
    vrep = 0
    for (base, dst) in ((0, BREP), (32, CREP)):
        for tch in range(NTC):
            b.tensor.wait(b.vs, vrep)
            tv = b.tensor.op("matmul", P[6][0:128, :],
                             RBS_sb[base:base + 16, :],
                             BCB[base:base + 16, tch * TCH:(tch + 1) * TCH],
                             start=True, stop=True, inc=b.ts)
            b.vector.wait(b.ts, tv)
            vrep = b.vector.op("tensor_copy", dst[:, tch * TCH:(tch + 1) * TCH],
                               P[6][0:128, :], inc=b.vs)
    msv = b.vector.milestone(b.vs)
    b.tensor.wait(*msv)
    b.scalar.wait(*msv)

    NHALVES = max(1, b.T // 2048)
    TH2 = b.T // NHALVES
    NTH2 = TH2 // TCH
    sdtr = b.dma_sem("l3_dtr")
    sur = b.dma_sem("l3_ur")
    sres = b.dma_sem("l3_res")
    syst = b.dma_sem("l3_yst")
    vv = b.vector.op("memset", SCAR[:], 0.0, inc=b.vs)
    dt_h = w["dt_scr_h"]
    u_h = w["u_scr_h"]
    rep_cons_v = 0
    rep_cons_s = 0
    res_cons_s = 0
    rsil_cons_v = 0
    scan_prev = 0
    for dc in range(NDC):
        b.sync.wait(b.ss, res_cons_s)
        b.sync.wait(b.vs, rsil_cons_v)
        b.load(RESS[:, :], res_dram[dc * 128:(dc + 1) * 128, :], sres)
        b.scalar.wait(sres, sres.v)
        svr = b.scalar.op("activation", RSIL[:, :], RESS[:, :], AF.Silu, inc=b.ss)
        res_cons_s = svr
        for th in range(NHALVES):
            tb = th * TH2
            for scj in range(16):
                sc = dc * 16 + scj
                r0 = dc * 128 + scj * 8
                b.sync.wait(b.vs, rep_cons_v)
                b.sync.wait(b.ss, rep_cons_s)
                b.load(DTRP[:, :], bass.AP(dt_h, (r0 * b.T + tb),
                                           [[b.T, 8], [0, 16], [1, TH2]]), sdtr)
                b.load(URP[:, :], bass.AP(u_h, (r0 * b.T + tb),
                                          [[b.T, 8], [0, 16], [1, TH2]]), sur)
                b.scalar.wait(sdtr, sdtr.v)
                b.scalar.wait(b.vs, scan_prev)
                acol = ASTATE[:, sc:sc + 1]
                sva = b.scalar.op("activation", APL[:, 0:TH2], DTRP[:, :], AF.Exp,
                                  scale=acol, inc=b.ss)
                rep_cons_s = sva
                b.vector.wait(sur, sur.v)
                vvb = b.vector.op("tensor_tensor", BPL[:, :], URP[:, :],
                                  BREP[:, tb:tb + TH2], op=ALU.mult, inc=b.vs)
                rep_cons_v = vvb
                b.vector.wait(b.ss, sva)
                init = 0.0 if th == 0 else SCAR[:, scj:scj + 1]
                scan_prev = b.vector.op("tensor_tensor_scan", HPL[:, :],
                                        APL[:, 0:TH2], BPL[:, :], init,
                                        op0=ALU.mult, op1=ALU.add, inc=b.vs)
                if th < NHALVES - 1:
                    b.vector.op("tensor_copy", SCAR[:, scj:scj + 1],
                                HPL[:, TH2 - 1:TH2])
                msq = b.tensor.milestone(b.ts)
                b.vector.wait(*msq)
                vvp = b.vector.op("tensor_tensor", PPL[:, :], HPL[:, :],
                                  CREP[:, tb:tb + TH2], op=ALU.mult, inc=b.vs)
                b.tensor.wait(b.vs, vvp)
                for tch in range(NTH2):
                    b.tensor.op("matmul", P[tch][:, :],
                                QMAT_sb[:, scj * 128:(scj + 1) * 128],
                                PPL[:, tch * TCH:(tch + 1) * TCH],
                                start=(scj == 0), stop=(scj == 15),
                                inc=(b.ts if scj == 15 else None))
            tvy = b.ts.v
            b.vector.wait(b.ts, tvy)
            b.vector.wait(b.ss, svr)
            for tch in range(NTH2):
                b.vector.wait(syst, syst.v)
                vvy = b.vector.op("tensor_tensor", YT[:, 0:TCH], P[tch][:, :],
                                  RSIL[:, tb + tch * TCH: tb + (tch + 1) * TCH],
                                  op=ALU.mult, inc=b.vs)
                rsil_cons_v = vvy
                b.sync.wait(b.vs, vvy)
                b.store(y_dram[dc * 128:(dc + 1) * 128,
                               tb + tch * TCH: tb + (tch + 1) * TCH],
                        YT[:, 0:TCH], syst)
            b.vector.wait(syst, syst.v)
            ms2 = b.vector.milestone(b.vs)
            b.tensor.wait(*ms2)

    emit_barrier(b)

    AR.reset()
    YS = AR.take_bf(16 * TCH)
    woS = WStream(b, "l3_wo", 16, 128, arena=AR)
    xp = XPipe(b, CON, "a3")
    sys_ = b.dma_sem("l3_ys")
    evo = [None, None]
    for tc in range(NTC):
        t0 = tc * TCH
        b.sync.wait(b.ts, b.ts.v)
        for k in range(16):
            b.load(YS[:, k * TCH:(k + 1) * TCH],
                   y_dram[k * 128:(k + 1) * 128, t0:t0 + TCH], sys_)
        b.tensor.wait(sys_, sys_.v)
        for oc in range(NFC):
            tiles, wv_ = woS.load_block(w["wout_bf"], oc * 128)
            b.tensor.wait(*wv_)
            pb = oc % 2
            if evo[pb]:
                b.tensor.wait(*evo[pb])
            pp = pA if pb == 0 else pB_
            tv = emit_xw_to_psum(b, tiles,
                                 lambda k: YS[:, k * TCH:(k + 1) * TCH],
                                 pp[:, :], 16, inc=b.ts)
            woS.consumed(tv)
            xpp = xp.begin(x1dram[oc * 128:(oc + 1) * 128, t0:t0 + TCH])
            b.vector.wait(b.ts, tv)
            ev1 = b.vector.op("tensor_tensor", CON["tmp_f32"][:, 0:TCH],
                              XN[:, oc * T + t0: oc * T + t0 + TCH], pp[:, :],
                              op=ALU.add, inc=b.vs)
            evo[pb] = (b.vs, ev1)
            xp.wait_ld(xpp)
            xp.wait_out_free(xpp)
            vv = b.vector.op("tensor_tensor", xp.out_ap(xpp), xp.ld_ap(xpp),
                             CON["tmp_f32"][:, 0:TCH], op=ALU.add, inc=b.vs)
            xp.end(xpp, xodram[oc * 128:(oc + 1) * 128, t0:t0 + TCH], vv)


def emit_block3(b, CON, P, PT, xin, xout):
    T = b.T
    x1 = b.dram("x1scr3", [D, T], F32, kind="Internal")
    DI = 2048
    w = {
        "win_f": b.dram("L3_Win", [D, 4096], F32),
        "wxT_f": b.dram("L3_WxT", [2080, DI], F32),
        "wx2_f": b.dram("L3_Wx2p", [DI, 64], F32),
        "wdt_f": b.dram("L3_Wdt", [DI, DI], F32),
        "wout_f": b.dram("L3_Wout", [DI, D], F32),
        "u1_f": b.dram("L3_U1", [D, 256], F32),
        "v1_f": b.dram("L3_V1", [256, 4096], F32),
        "u2_f": b.dram("L3_U2", [4096, 256], F32),
        "v2_f": b.dram("L3_V2", [256, D], F32),
        "smat_in": b.dram("L3_smat", [16 * 128, 128], BF16),
        "qmat_in": b.dram("L3_qmat", [16 * 128, 128], BF16),
        "rbs_in": b.dram("L3_rbs", [64, 128], BF16),
        "alog_in": b.dram("L3_alog", [128, 256], F32),
        "zeros3": b.dram("L3_zeros3", [128, 4], BF16),
    }
    for (nm, K_, N_) in [("win_bf", D, 4096), ("wxT_bf", 2048, DI),
                         ("wdt_bf", DI, DI), ("wout_bf", DI, D),
                         ("u1_bf", D, 256), ("v1_bf", 256, 4096),
                         ("u2_bf", 4096, 256), ("v2_bf", 256, D)]:
        w[nm] = TW(b, "L3_" + nm, K_, N_).view()
    w["wx2_bf"] = b.dram("L3_wx2_bf", [DI, 64], BF16, kind="Internal")
    for (nm, shp, dt_) in [("xs_scr", [DI, T], BF16), ("xsc_scr", [DI, T], BF16),
                           ("res_scr", [DI, T], BF16), ("dt_scr", [DI, T], BF16),
                           ("u_scr", [DI, T], BF16), ("y_scr", [DI, T], BF16)]:
        h = b.nc.dram_tensor("L3_" + nm, shp, dt_, kind="Internal")
        w[nm] = h.ap()
        if nm in ("dt_scr", "u_scr"):
            w[nm + "_h"] = h
    w["weff_tw"] = TW(b, "L3_weff", DI, DI)
    ARB = get_arb(b)
    w["rmsc"] = load_colvec_arb(b, ARB, NFC, b.dram("L3_rmsc", [128, NFC], F32), "ini_l3cv")
    w["cwc"] = load_colvec_arb(b, ARB, 64, b.dram("L3_cwc", [128, 64], F32), "ini_l3cv2")
    w["cbc"] = load_colvec_arb(b, ARB, 16, b.dram("L3_cbc", [128, 16], F32), "ini_l3cv3")
    w["b1c"] = load_colvec_arb(b, ARB, 32, b.dram("L3_b1c", [128, 32], F32), "ini_l3cv4")
    w["b2c"] = load_colvec_arb(b, ARB, NFC, b.dram("L3_b2c", [128, NFC], F32), "ini_l3cv5")
    if getattr(b, "XN", None) is None:
        b.XN = b.sb("XN", [128, NFC * b.T], BF16)
    XN = b.XN
    AR = b.AR if getattr(b, "AR", None) is not None else Arena(b, "AR", 44032)
    b.AR = AR
    jobs = [
        (w["win_f"], None, w["win_bf"].tw, D, 4096),
        (w["wxT_f"], None, w["wxT_bf"].tw, 2048, DI),
        (w["wx2_f"], None, w["wx2_bf"], DI, 64),
        (w["wdt_f"], None, w["wdt_bf"].tw, DI, DI),
        (w["wout_f"], None, w["wout_bf"].tw, DI, D),
        (w["u1_f"], None, w["u1_bf"].tw, D, 256),
        (w["v1_f"], None, w["v1_bf"].tw, 256, 4096),
        (w["u2_f"], None, w["u2_bf"].tw, 4096, 256),
        (w["v2_f"], None, w["v2_bf"].tw, 256, D),
    ]
    emit_precast(b, CON, XN[:, 0:2048], jobs)
    emit_ln(b, xin, XN, CON, P)
    emit_barrier(b)
    emit_l3_ffn(b, xin, x1, XN, CON, P, w, AR)
    emit_barrier(b)
    emit_l3_attn(b, x1, xout, XN, CON, P, w, AR)


def build_block3(T=4096):
    nc = bass.Bass("TRN2", target_bir_lowering=False, debug=False)
    es = ExitStack()
    es.__enter__()
    b = Builder(nc, es, T)
    xin = b.dram("xin", [D, T], F32)
    xout = b.dram("xout", [D, T], F32, kind="ExternalOutput")
    P = [b.ps(f"P{i}", [128, TCH], F32) for i in range(7)]
    PT = b.ps("PT", [128, 1024], BF16)
    b.PT_BF = PT
    CON = alloc_common(b)
    emit_block3(b, CON, P, PT, xin, xout)
    b.finish()
    return nc, es


def host_prep_l3(inputs):
    C = np.ascontiguousarray
    import ml_dtypes
    DI = 2048

    def colvec(v):
        return C(np.asarray(v).reshape(-1, 128).T)
    wx = np.asarray(inputs["L3_attn_W_x"])
    wx2p = np.zeros((DI, 64), np.float32)
    wx2p[:, 0:16] = wx[:, 2048:2064]
    wx2p[:, 32:48] = wx[:, 2064:2080]
    smat = np.zeros((16 * 128, 128), dtype=ml_dtypes.bfloat16)
    qmat = np.zeros((16 * 128, 128), dtype=ml_dtypes.bfloat16)
    for j in range(16):
        r0 = 8 * j
        for i in range(128):
            smat[j * 128 + r0 + i // 16, i] = 1.0
            qmat[j * 128 + i, r0 + i // 16] = 1.0
    rbs = np.zeros((64, 128), dtype=ml_dtypes.bfloat16)
    for n in range(16):
        for i in range(128):
            if i % 16 == n:
                rbs[n, i] = 1.0
                rbs[32 + n, i] = 1.0
    alog = np.asarray(inputs["L3_attn_A_log"]).astype(np.float32)
    alog_state = C(alog.reshape(-1)[None, :].reshape(256, 128).T)
    return {
        "L3_Win": C(np.asarray(inputs["L3_attn_W_in"])),
        "L3_WxT": C(wx.T),
        "L3_Wx2p": wx2p,
        "L3_Wdt": C(np.asarray(inputs["L3_attn_W_dt"])),
        "L3_Wout": C(np.asarray(inputs["L3_attn_W_out"])),
        "L3_U1": C(np.asarray(inputs["L3_ffn_U1"])),
        "L3_V1": C(np.asarray(inputs["L3_ffn_V1"])),
        "L3_U2": C(np.asarray(inputs["L3_ffn_U2"])),
        "L3_V2": C(np.asarray(inputs["L3_ffn_V2"])),
        "L3_smat": smat,
        "L3_qmat": qmat,
        "L3_rbs": rbs,
        "L3_alog": alog_state,
        "L3_zeros3": np.zeros((128, 4), dtype=ml_dtypes.bfloat16),
        "L3_rmsc": colvec(inputs["L3_attn_rms_w"]),
        "L3_cwc": colvec(np.asarray(inputs["L3_attn_conv_w"]).reshape(DI, 4)
                         .reshape(16, 128, 4).transpose(1, 0, 2).reshape(128, 64)
                         ) if False else C(
            np.asarray(inputs["L3_attn_conv_w"]).reshape(DI, 4)
            .reshape(16, 128, 4).transpose(1, 0, 2).reshape(128, 64)),
        "L3_cbc": colvec(inputs["L3_attn_conv_b"]),
        "L3_b1c": colvec(inputs["L3_ffn_b1"]),
        "L3_b2c": colvec(inputs["L3_ffn_b2"]),
        "ident_in": np.eye(128, dtype=np.float32),
    }


def build_block1(T=4096):
    nc = bass.Bass("TRN2", target_bir_lowering=False, debug=False)
    es = ExitStack()
    es.__enter__()
    b = Builder(nc, es, T)
    xin = b.dram("xin", [D, T], F32)
    xout = b.dram("xout", [D, T], F32, kind="ExternalOutput")
    P = [b.ps(f"P{i}", [128, TCH], F32) for i in range(7)]
    PT = b.ps("PT", [128, 1024], BF16)
    b.PT_BF = PT
    CON = alloc_common(b)
    emit_block1(b, CON, P, PT, xin, xout)
    b.finish()
    return nc, es


def host_prep_l1(inputs):
    C = np.ascontiguousarray
    import ml_dtypes

    def colvec(v):
        return C(np.asarray(v).reshape(-1, 128).T)
    d = np.arange(D)
    th = 2.0 * np.pi * np.outer(d, d) / D
    fftc = np.cos(th).astype(ml_dtypes.bfloat16)
    ffts = (-np.sin(th)).astype(ml_dtypes.bfloat16)
    oh8 = np.zeros((8, 8 * 128), dtype=ml_dtypes.bfloat16)
    for e in range(8):
        oh8[e, e * 128:(e + 1) * 128] = 1.0
    return {
        "L1_wqT": C(np.asarray(inputs["L1_attn_Wq"]).T),
        "L1_wkT": C(np.asarray(inputs["L1_attn_Wk"]).T),
        "L1_wvT": C(np.asarray(inputs["L1_attn_Wv"]).T),
        "L1_bqc": colvec(inputs["L1_attn_bq"]),
        "L1_bkc": colvec(inputs["L1_attn_bk"]),
        "L1_bvc": colvec(inputs["L1_attn_bv"]),
        "L1_fftc": fftc,
        "L1_ffts": ffts,
        "L1_wg": C(np.asarray(inputs["L1_ffn_w_gate"]).reshape(8 * D, 4096)),
        "L1_wu": C(np.asarray(inputs["L1_ffn_w_up"]).reshape(8 * D, 4096)),
        "L1_wd": C(np.asarray(inputs["L1_ffn_w_down"]).reshape(8 * 4096, D)),
        "L1_rwT": C(np.asarray(inputs["L1_ffn_router_w"]).T),
        "oh8_in": oh8,
        "ident_in": np.eye(128, dtype=np.float32),
    }


def build_full(T=4096):
    nc = bass.Bass("TRN2", target_bir_lowering=False, debug=False)
    es = ExitStack()
    es.__enter__()
    b = Builder(nc, es, T)
    xin = b.dram("xin", [D, T], F32)
    xout = b.dram("xout", [D, T], F32, kind="ExternalOutput")
    xc1 = b.dram("xc1", [D, T], F32, kind="Internal")
    xc2 = b.dram("xc2", [D, T], F32, kind="Internal")
    xc3 = b.dram("xc3", [D, T], F32, kind="Internal")
    P = [b.ps(f"P{i}", [128, TCH], F32) for i in range(7)]
    PT = b.ps("PT", [128, 1024], BF16)
    b.PT_BF = PT
    CON = alloc_common(b)
    b.AR = Arena(b, "AR", 44032)
    emit_block0(b, CON, P, PT, xin, xc1)
    emit_barrier(b)
    emit_block1(b, CON, P, PT, xc1, xc2)
    emit_barrier(b)
    emit_block2(b, CON, P, PT, xc2, xc3)
    emit_barrier(b)
    emit_block3(b, CON, P, PT, xc3, xout)
    b.finish()
    n_sems = len(b.sems)
    assert n_sems <= 100, f"too many semaphores: {n_sems}"
    return nc, es


def host_prep_all(inputs):
    m = {}
    m.update(host_prep_l0(inputs))
    m.update(host_prep_l1(inputs))
    m.update(host_prep_l2(inputs))
    m.update(host_prep_l3(inputs))
    return m


FUSED = True


def kernel(**inputs):
    T = 4096
    x = np.asarray(inputs["L0_attn_x"])
    if FUSED:
        nc, es = build_full(T)
        wmap = host_prep_all(inputs)
        in_maps = [dict(wmap, xin=np.ascontiguousarray(x[c].T)) for c in range(8)]
        res = run_bass_kernel_spmd(nc, in_maps, core_ids=list(range(8)))
        es.__exit__(None, None, None)
        return np.stack([np.ascontiguousarray(r["xout"].T) for r in res.results])
    cur = [np.ascontiguousarray(x[c].T) for c in range(8)]
    blocks = [(build_block0, host_prep_l0), (build_block1, host_prep_l1),
              (build_block2, host_prep_l2), (build_block3, host_prep_l3)]
    for build, prep in blocks:
        nc, es = build(T)
        wmap = prep(inputs)
        in_maps = [dict(wmap, xin=cur[c]) for c in range(8)]
        res = run_bass_kernel_spmd(nc, in_maps, core_ids=list(range(8)))
        es.__exit__(None, None, None)
        cur = [np.ascontiguousarray(r["xout"]) for r in res.results]
    return np.stack([np.ascontiguousarray(c.T) for c in cur])


# version 06 (final)
# speedup vs baseline: 1.0502x; 1.0502x over previous
import numpy as np
from contextlib import ExitStack
import concourse.bass as bass
import concourse.mybir as mybir
from concourse.bass_utils import run_bass_kernel_spmd

F32 = mybir.dt.float32
BF16 = mybir.dt.bfloat16
I32 = mybir.dt.int32
AF = mybir.ActivationFunctionType
ALU = mybir.AluOpType
AX = mybir.AxisListType

D = 1024
NFC = D // 128
TCH = 512

ENGINES = ("sync", "tensor", "vector", "scalar", "gpsimd")


class SemC:
    def __init__(self, h, name):
        self.h = h
        self.name = name
        self.v = 0

    def bump(self, n):
        self.v += n
        return self.v


def _ap_span(a):
    try:
        t = a.tensor
        name = t.name
        off = int(a.offset)
        ext = 1
        for d in a.ap:
            ext += int(d[0]) * (int(d[1]) - 1)
        return (name, off, off + ext)
    except Exception:
        return None


def _spans(args):
    out = []
    for a in args:
        s = _ap_span(a)
        if s:
            out.append(s)
    return out


def _overlap(s1, s2):
    return s1[0] == s2[0] and s1[1] < s2[2] and s2[1] < s1[2]


class Qe:

    GUARDED = ("vector", "scalar", "gpsimd")
    WINDOW = 48

    def __init__(self, b, eng):
        self.b = b
        self.eng = eng
        self.window = []
        self.cells = []
        self.next_unassigned = 0
        self.last_cell = None

    def _app(self, fn):
        self.b.q[self.eng].append(fn)

    def _ensure_inc(self, idx):
        g = self.b.sems["g" + self.eng]
        while self.next_unassigned <= idx:
            c = self.cells[self.next_unassigned]
            if c["sem"] is None:
                g.bump(1)
                c["sem"] = g
                c["val"] = g.v
            self.next_unassigned += 1
        c = self.cells[idx]
        return (c["sem"], c["val"])

    def op(self, opname, *args, inc=None, incn=1, **kw):
        guarded = self.eng in self.GUARDED
        outs = ins = None
        if guarded:
            outs = _spans(args[:1]) + _spans(
                [kw[k] for k in ("accum_out",) if kw.get(k) is not None])
            ins = _spans(args[1:]) + _spans(
                [kw[k] for k in ("scale", "bias") if kw.get(k) is not None])
            for (pidx, pcell, pouts, pins) in self.window:
                dep = any(_overlap(s, po) for po in pouts for s in (outs + ins)) or \
                    any(_overlap(o, pi) for pi in pins for o in outs)
                if dep:
                    self.wait(*self._ensure_inc(pidx))
        if inc is not None:
            inc.bump(incn)
        cell = {"sem": None, "val": 0}

        def run(e, opname=opname, args=args, kw=kw, inc=inc, incn=incn, cell=cell):
            inst = getattr(e, opname)(*args, **kw)
            if inc is not None:
                inst.then_inc(inc.h, incn)
            elif cell["sem"] is not None:
                inst.then_inc(cell["sem"].h, 1)
        self._app(run)
        if inc is not None:
            cell["sem"] = inc
            cell["val"] = inc.v
        self.last_cell = cell
        self.cells.append(cell)
        if guarded:
            self.window.append((len(self.cells) - 1, cell, outs, ins))
            if len(self.window) > self.WINDOW:
                self.window.pop(0)
        return inc.v if inc is not None else None

    def wait(self, sem, val):
        if val <= 0:
            return
        hw = self.b.waited[self.eng]
        if hw.get(sem.name, -1) >= val:
            return
        hw[sem.name] = val

        def run(e, sem=sem, val=val):
            e.wait_ge(sem.h, val)
        self._app(run)

    def milestone(self, sem):
        if not self.cells:
            return (sem, 0)
        return self._ensure_inc(len(self.cells) - 1)



class Builder:
    def __init__(self, nc, es, T):
        self.nc = nc
        self.es = es
        self.T = T
        self.NTC = T // TCH
        self.q = {e: [] for e in ENGINES}
        self.waited = {e: {} for e in ENGINES}
        self.sems = {}
        self.dma_sems = []
        for name in ["ts", "vs", "ss",
                     "gvector", "gscalar", "gtensor", "ggpsimd", "gsync"]:
            h = es.enter_context(nc.semaphore(name))
            self.sems[name] = SemC(h, name)
        self.sync = Qe(self, "sync")
        self.tensor = Qe(self, "tensor")
        self.vector = Qe(self, "vector")
        self.scalar = Qe(self, "scalar")
        self.gpsimd = Qe(self, "gpsimd")
        self.ts = self.sems["ts"]
        self.vs = self.sems["vs"]
        self.ss = self.sems["ss"]

    def dma_sem(self, name):
        if name not in self.sems:
            h = self.es.enter_context(self.nc.semaphore(name))
            self.sems[name] = SemC(h, name)
            self.dma_sems.append(self.sems[name])
        return self.sems[name]

    def sb(self, name, shp, dt=F32):
        if name in getattr(self, "_sb_names", set()):
            i = 2
            while f"{name}_{i}" in self._sb_names:
                i += 1
            name = f"{name}_{i}"
        if not hasattr(self, "_sb_names"):
            self._sb_names = set()
        self._sb_names.add(name)
        return self.es.enter_context(self.nc.sbuf_tensor(name, shp, dt))

    def ps(self, name, shp, dt=F32):
        return self.es.enter_context(self.nc.psum_tensor(name, shp, dt))

    def dram(self, name, shp, dt=F32, kind="ExternalInput"):
        return self.nc.dram_tensor(name, shp, dt, kind=kind).ap()

    def load(self, dst, src, sem):
        return self.sync.op("dma_start", dst, src, inc=sem, incn=16)

    def load_g(self, dst, src, sem):
        return self.gpsimd.op("dma_start", dst, src, inc=sem, incn=16)

    def store(self, dst, src, sem):
        return self.sync.op("dma_start", dst, src, inc=sem, incn=16)

    def store_g(self, dst, src, sem):
        return self.gpsimd.op("dma_start", dst, src, inc=sem, incn=16)

    def dma_fence(self):
        for s in self.dma_sems:
            self.sync.wait(s, s.v)

    def finish(self):
        for s in self.dma_sems:
            self.sync.wait(s, s.v)
        block = self.es.enter_context(self.nc.Block())
        for eng in ENGINES:
            ops = self.q[eng]
            if not ops:
                continue
            getattr(block, eng)(lambda e, ops=ops: [fn(e) for fn in ops])


class TW:

    def __init__(self, b, name, K, N, nmat=1):
        self.K = K
        self.NB = N // 128
        self.h = b.nc.dram_tensor(name, [nmat * self.NB * K, 128], BF16,
                                  kind="Internal")

    def __getitem__(self, mat):
        return TWView(self, mat)

    def view(self):
        return TWView(self, 0)

    def src(self, mat, col0, nk, k0=0):
        blk = mat * self.NB + col0 // 128
        off = blk * 128 * self.K + k0 * 128
        return bass.AP(self.h, off, [[self.K, 128], [1, nk * 128]])

    def dst(self, mat, k, c0, ncols):
        ncb = ncols // 128
        blk0 = mat * self.NB + c0 // 128
        off = blk0 * 128 * self.K + k * 128
        return bass.AP(self.h, off,
                       [[self.K, 128], [128 * self.K, ncb], [1, 128]])


class TWView:
    def __init__(self, tw, mat):
        self.tw = tw
        self.mat = mat


class Arena:

    def __init__(self, b, name, cols_bf):
        self.h = b.sb(name, [128, cols_bf], BF16)
        self.cols = cols_bf
        self.off = 0

    def reset(self):
        self.off = 0

    def take_bf(self, cols):
        assert self.off + cols <= self.cols, f"arena overflow {self.off}+{cols}"
        ap = self.h[:, self.off:self.off + cols]
        self.off += cols
        return ap

    def take_f32(self, cols):
        self.off = (self.off + 1) & ~1
        ap = self.h[:, self.off:self.off + 2 * cols].bitcast(F32)
        self.off += 2 * cols
        return ap

    def take_i32(self, cols):
        self.off = (self.off + 1) & ~1
        ap = self.h[:, self.off:self.off + 2 * cols].bitcast(I32)
        self.off += 2 * cols
        return ap


def emit_barrier(b):
    b.ws_next = 0
    b.xp_next = 0
    ms = [b.tensor.milestone(b.ts), b.vector.milestone(b.vs),
          b.scalar.milestone(b.ss)]
    b.dma_fence()
    for q in (b.tensor, b.vector, b.scalar, b.sync):
        for (sem, val) in ms:
            q.wait(sem, val)
        for s in b.dma_sems:
            q.wait(s, s.v)


class WStream:

    def __init__(self, b, name, nk, ocw=128, arena=None, nbuf=2):
        self.b = b
        self.nk = nk
        self.ocw = ocw
        self.nbuf = nbuf
        if arena is None:
            self.stash = b.sb(name, [128, nbuf * nk * ocw], BF16)
        else:
            self.stash = arena.take_bf(nbuf * nk * ocw)
        self.parity = 0
        self.free_ts = [0] * nbuf
        n = getattr(b, "ws_next", 0)
        b.ws_next = n + 1
        self.sems = [b.dma_sem(f"ws{n}_{i}") for i in range(nbuf)]

    def load_block(self, wsrc, col0, row0=0, ncols=None, k0=0):
        b = self.b
        nco = ncols or self.ocw
        p = self.parity
        off = p * self.nk * self.ocw
        b.sync.wait(b.ts, self.free_ts[p])
        sem = self.sems[p]
        tiles = [self.stash[:, off + k * nco: off + (k + 1) * nco]
                 for k in range(self.nk)]
        if isinstance(wsrc, TWView):
            assert row0 == 0
            ncb = nco // 128
            if ncb == 1:
                b.load(self.stash[:, off: off + self.nk * nco],
                       wsrc.tw.src(wsrc.mat, col0, self.nk, k0=k0), sem)
            else:
                dst3 = self.stash[:, off: off + self.nk * nco].rearrange(
                    "p (k cb c) -> p k (cb c)", k=self.nk, c=128)
                for cb in range(ncb):
                    b.load(dst3[:, :, cb * 128:(cb + 1) * 128],
                           wsrc.tw.src(wsrc.mat, col0 + cb * 128, self.nk),
                           sem)
        else:
            for k in range(self.nk):
                b.load(tiles[k], wsrc[row0 + k * 128: row0 + (k + 1) * 128,
                                      col0:col0 + nco], sem)
        self._last_parity = p
        self.parity = (p + 1) % self.nbuf
        return tiles, (sem, sem.v)

    def consumed(self, ts_val):
        self.free_ts[self._last_parity] = ts_val


class XPipe:

    def __init__(self, b, CON, tag):
        self.b = b
        self.x = CON["xtile"]
        n = getattr(b, "xp_next", 0)
        b.xp_next = n + 1
        self.lsem = [b.dma_sem(f"xp{n}l0"), b.dma_sem(f"xp{n}l1")]
        self.ssem = [b.dma_sem(f"xp{n}s0"), b.dma_sem(f"xp{n}s1")]
        self.cons = [0, 0]
        self.it = 0

    def begin(self, src_ap):
        p = self.it % 2
        b = self.b
        b.gpsimd.wait(b.vs, self.cons[p])
        b.load_g(self.x[:, p * TCH:(p + 1) * TCH], src_ap, self.lsem[p])
        return p

    def ld_ap(self, p):
        return self.x[:, p * TCH:(p + 1) * TCH]

    def out_ap(self, p):
        return self.x[:, (2 + p) * TCH:(3 + p) * TCH]

    def wait_ld(self, p):
        self.b.vector.wait(self.lsem[p], self.lsem[p].v)

    def wait_out_free(self, p):
        self.b.vector.wait(self.ssem[p], self.ssem[p].v)

    def end(self, p, dst_ap, vs_val):
        b = self.b
        b.gpsimd.wait(b.vs, vs_val)
        b.store_g(dst_ap, self.out_ap(p), self.ssem[p])
        self.cons[p] = vs_val
        self.it += 1


def emit_precast(b, CON, bfscratch, jobs):
    NC = 1024
    p32h = [CON["lnx"][:, 0:NC], CON["lnx"][:, NC:2 * NC]]
    s32h = [CON["lnx"][:, 2 * NC:3 * NC], CON["lnx"][:, 3 * NC:4 * NC]]
    pfh = [bfscratch[:, 0:NC], bfscratch[:, NC:2 * NC]]
    lds = [b.dma_sem("prp0"), b.dma_sem("prp1")]
    sts = [b.dma_sem("prs0"), b.dma_sem("prs1")]
    cast_done = [0, 0]
    par = 0
    for (src, scl, dst, K, N) in jobs:
        for k in range(K // 128):
            for c0 in range(0, N, NC):
                nc_ = min(NC, N - c0)
                p32 = p32h[par]
                s32 = s32h[par]
                b.sync.wait(b.vs, cast_done[par])
                if scl is None:
                    b.load(p32[:, 0:nc_], src[k * 128:(k + 1) * 128, c0:c0 + nc_],
                           lds[par])
                else:
                    b.load(p32.bitcast(I32)[:, 0:nc_],
                           src[k * 128:(k + 1) * 128, c0:c0 + nc_], lds[par])
                    b.load(s32[:, 0:nc_], scl[k * 128:(k + 1) * 128, c0:c0 + nc_],
                           lds[par])
                b.vector.wait(lds[par], lds[par].v)
                b.vector.wait(sts[par], sts[par].v)
                dstb = pfh[par][:, 0:nc_]
                if scl is None:
                    vv = b.vector.op("tensor_copy", dstb, p32[:, 0:nc_], inc=b.vs)
                else:
                    b.vector.op("tensor_scalar", dstb, p32.bitcast(I32)[:, 0:nc_],
                                -128.0, None, op0=ALU.add)
                    vv = b.vector.op("tensor_tensor", dstb, dstb, s32[:, 0:nc_],
                                     op=ALU.mult, inc=b.vs)
                cast_done[par] = vv
                b.gpsimd.wait(b.vs, vv)
                if isinstance(dst, TW):
                    kc = dst.K // 128
                    b.store_g(dst.dst(k // kc, k % kc, c0, nc_), dstb, sts[par])
                else:
                    b.store_g(dst[k * 128:(k + 1) * 128, c0:c0 + nc_], dstb,
                              sts[par])
                par ^= 1
    b.dma_fence()


def load_full_direct(b, name, dram_bf, K, N):
    KC = K // 128
    t = b.sb(name, [128, KC * N], BF16)
    lv = 0
    for k in range(KC):
        lv = b.load(t[:, k * N:(k + 1) * N], dram_bf[k * 128:(k + 1) * 128, :])

    def tiles_fn(k, n0, nw=128):
        return t[:, k * N + n0: k * N + n0 + nw]
    return tiles_fn, lv


def emit_stats(b, CON, P, src_fn, n_chunks, eps, src_f32=False):
    sumb, sqb, ab, cb = P[0], P[1], P[2], P[3]
    ones_c = CON["ones_col_bf"]
    ones_r = CON["ones_row_bf"]
    stat = CON["statv"]
    stat_bf = CON["statv_bf"]
    sq_t = CON["scratch_bf"]
    sq_t2 = CON["scratch_bf2"]
    prev_tv = [0, 0]
    for fc in range(n_chunks):
        p = fc % 2
        src = src_fn(fc)
        sq_ap = sq_t[:, p * TCH:(p + 1) * TCH]
        if src_f32:
            sq2_ap = sq_t2[:, p * TCH:(p + 1) * TCH]
            b.vector.wait(b.ts, prev_tv[p])
            vv = b.vector.op("tensor_copy", sq2_ap, src, inc=b.vs)
            b.tensor.wait(b.vs, vv)
            rhs = sq2_ap
        else:
            rhs = src
        b.scalar.wait(b.ts, prev_tv[p])
        sv = b.scalar.op("activation", sq_ap, src, AF.Square, inc=b.ss)
        b.tensor.wait(b.ss, sv)
        b.tensor.op("matmul", sumb[0:1, :], ones_c[:], rhs,
                    start=(fc == 0), stop=(fc == n_chunks - 1))
        tv = b.tensor.op("matmul", sqb[0:1, :], ones_c[:], sq_ap,
                         start=(fc == 0), stop=(fc == n_chunks - 1), inc=b.ts)
        prev_tv[p] = tv
    b.vector.wait(b.ts, tv)
    mean = stat[:, 0:TCH]
    var = stat[:, TCH:2 * TCH]
    a_v = stat[:, 2 * TCH:3 * TCH]
    c_v = stat[:, 3 * TCH:4 * TCH]
    nd = float(n_chunks * 128)
    b.vector.op("tensor_scalar", mean, sumb[0:1, :], 1.0 / nd, None, op0=ALU.mult)
    b.vector.op("tensor_scalar", var, sqb[0:1, :], 1.0 / nd, None, op0=ALU.mult)
    b.vector.op("tensor_tensor", a_v, mean, mean, op=ALU.mult)
    b.vector.op("tensor_tensor", var, var, a_v, op=ALU.subtract)
    vv = b.vector.op("tensor_scalar", var, var, eps, None, op0=ALU.add, inc=b.vs)
    b.scalar.wait(b.vs, vv)
    sv = b.scalar.op("activation", a_v, var, AF.Sqrt, inc=b.ss)
    b.vector.wait(b.ss, sv)
    b.vector.op("reciprocal", a_v, a_v)
    b.vector.op("tensor_tensor", c_v, mean, a_v, op=ALU.mult)
    b.vector.op("tensor_scalar", c_v, c_v, -1.0, None, op0=ALU.mult)
    b.vector.op("tensor_copy", stat_bf[:, 0:TCH], a_v)
    vv = b.vector.op("tensor_copy", stat_bf[:, TCH:2 * TCH], c_v, inc=b.vs)
    b.tensor.wait(b.vs, vv)
    b.tensor.op("matmul", ab[:, :], ones_r[:], stat_bf[:, 0:TCH], start=True, stop=True)
    tv = b.tensor.op("matmul", cb[:, :], ones_r[:], stat_bf[:, TCH:2 * TCH],
                     start=True, stop=True, inc=b.ts)
    return tv


def emit_rms_stats(b, CON, P, sq_src_fn, n_chunks, eps):
    sqb, ab = P[1], P[2]
    ones_c = CON["ones_col_bf"]
    ones_r = CON["ones_row_bf"]
    stat = CON["statv"]
    stat_bf = CON["statv_bf"]
    sq_t = CON["scratch_bf"]
    prev_tv = [0, 0]
    for fc in range(n_chunks):
        p = fc % 2
        sq_ap = sq_t[:, p * TCH:(p + 1) * TCH]
        b.scalar.wait(b.ts, prev_tv[p])
        sv = b.scalar.op("activation", sq_ap, sq_src_fn(fc), AF.Square, inc=b.ss)
        b.tensor.wait(b.ss, sv)
        tv = b.tensor.op("matmul", sqb[0:1, :], ones_c[:], sq_ap,
                         start=(fc == 0), stop=(fc == n_chunks - 1), inc=b.ts)
        prev_tv[p] = tv
    b.vector.wait(b.ts, tv)
    var = stat[:, TCH:2 * TCH]
    a_v = stat[:, 2 * TCH:3 * TCH]
    nd = float(n_chunks * 128)
    b.vector.op("tensor_scalar", var, sqb[0:1, :], 1.0 / nd, None, op0=ALU.mult)
    vv = b.vector.op("tensor_scalar", var, var, eps, None, op0=ALU.add, inc=b.vs)
    b.scalar.wait(b.vs, vv)
    sv = b.scalar.op("activation", a_v, var, AF.Sqrt, inc=b.ss)
    b.vector.wait(b.ss, sv)
    b.vector.op("reciprocal", a_v, a_v)
    vv = b.vector.op("tensor_copy", stat_bf[:, 0:TCH], a_v, inc=b.vs)
    b.tensor.wait(b.vs, vv)
    tv = b.tensor.op("matmul", ab[:, :], ones_r[:], stat_bf[:, 0:TCH],
                     start=True, stop=True, inc=b.ts)
    return tv


def emit_ln(b, xdram, XN, CON, P, eps=1e-5, tc_hook=None):
    T, NTC = b.T, b.NTC
    xt = CON["lnx"]
    ab, cb = P[2], P[3]
    sln = b.dma_sem("lnx")
    for tc in range(NTC):
        t0 = tc * TCH
        for fc in range(NFC):
            b.load(xt[:, fc * TCH:(fc + 1) * TCH],
                   xdram[fc * 128:(fc + 1) * 128, t0:t0 + TCH], sln)
        b.vector.wait(sln, sln.v)
        b.scalar.wait(sln, sln.v)
        tv = emit_stats(b, CON, P,
                        lambda fc: xt[:, fc * TCH:(fc + 1) * TCH],
                        NFC, eps, src_f32=True)
        b.vector.wait(b.ts, tv)
        for fc in range(NFC):
            tmp = CON["tmp_f32"]
            b.vector.op("tensor_tensor", tmp[:, 0:TCH],
                        xt[:, fc * TCH:(fc + 1) * TCH], ab[:, :], op=ALU.mult)
            b.vector.op("tensor_tensor", XN[:, fc * T + t0: fc * T + t0 + TCH],
                        tmp[:, 0:TCH], cb[:, :], op=ALU.add)
        if tc_hook is not None:
            stat = CON["statv"]
            tc_hook(tc, t0, xt, stat[:, 2 * TCH:3 * TCH], stat[:, 3 * TCH:4 * TCH])
        ms = b.vector.milestone(b.vs)
        b.tensor.wait(*ms)
        b.sync.wait(*ms)
        b.scalar.wait(*ms)


def emit_xw_to_psum(b, tiles, rhs_fn, psum_ap, nk, inc=None):
    v = None
    for k in range(nk):
        v = b.tensor.op("matmul", psum_ap, tiles[k], rhs_fn(k),
                        start=(k == 0), stop=(k == nk - 1),
                        inc=(inc if k == nk - 1 else None))
    return v



def emit_l0_ffn(b, xdram, x1dram, XN, CON, P, w, AR):
    T, NTC = b.T, b.NTC
    TH = TCH
    AR.reset()
    HBUF = AR.take_bf(NFC * TH)
    YBUF = AR.take_bf(32 * TH)
    ab, cb = P[2], P[3]
    w1s = WStream(b, "l0w1", NFC, 128, arena=AR)
    w2s = WStream(b, "l0w2", 32, 128, arena=AR)
    fb2h = b.sb("l0_fb2h", [128, NFC], F32)
    b.vector.op("tensor_scalar", fb2h[:], w["fb2c"][:], 0.5, None, op0=ALU.mult)

    pyA, pyB = P[4], P[5]
    last_evac = [None, None]
    xp = XPipe(b, CON, "f")
    for th in range(NTC):
        t0 = th * TCH
        tv = emit_stats(b, CON, P,
                        lambda fc: XN[:, fc * T + t0: fc * T + t0 + TCH],
                        NFC, 1e-5)
        b.vector.wait(b.ts, tv)
        for fc in range(NFC):
            tmp = CON["tmp_f32"]
            b.vector.op("tensor_tensor", tmp[:, 0:TCH],
                        XN[:, fc * T + t0: fc * T + t0 + TCH], ab[:, :], op=ALU.mult)
            b.vector.op("tensor_tensor", tmp[:, TCH:2 * TCH], tmp[:, 0:TCH],
                        cb[:, :], op=ALU.add)
            b.vector.op("tensor_scalar", HBUF[:, fc * TH:(fc + 1) * TH],
                        tmp[:, TCH:2 * TCH], w["g1c"][:, fc:fc + 1],
                        w["b1c"][:, fc:fc + 1], op0=ALU.mult, op1=ALU.add)
        ms = b.vector.milestone(b.vs)
        b.tensor.wait(*ms)

        for oc in range(32):
            tiles, wv = w1s.load_block(w["w1bf"], oc * 128)
            b.tensor.wait(*wv)
            pb = oc % 2
            py = pyA if pb == 0 else pyB
            if last_evac[pb]:
                b.tensor.wait(*last_evac[pb])
            tv = emit_xw_to_psum(b, tiles,
                                 lambda k: HBUF[:, k * TH:(k + 1) * TH],
                                 py[:, :], NFC, inc=b.ts)
            w1s.consumed(tv)
            b.scalar.wait(b.ts, tv)
            last_evac[pb] = (b.ss, b.scalar.op(
                "activation", YBUF[:, oc * TH:(oc + 1) * TH],
                py[:, :], AF.Silu, bias=w["fb1c"][:, oc:oc + 1], inc=b.ss))

        for le in last_evac:
            if le:
                b.tensor.wait(*le)
        for zc in range(NFC):
            tiles, wv = w2s.load_block(w["w2bf"], zc * 128)
            b.tensor.wait(*wv)
            pb = zc % 2
            pz = pyA if pb == 0 else pyB
            if last_evac[pb]:
                b.tensor.wait(*last_evac[pb])
            tv = emit_xw_to_psum(b, tiles,
                                 lambda k: YBUF[:, k * TH:(k + 1) * TH],
                                 pz[:, :], 32, inc=b.ts)
            w2s.consumed(tv)
            xpp = xp.begin(xdram[zc * 128:(zc + 1) * 128, t0:t0 + TCH])
            b.vector.wait(b.ts, tv)
            vv0 = b.vector.op("tensor_scalar", CON["tmp_f32"][:, 0:TCH], pz[:, :],
                              0.5, fb2h[:, zc:zc + 1], op0=ALU.mult, op1=ALU.add,
                              inc=b.vs)
            last_evac[pb] = (b.vs, vv0)
            xp.wait_ld(xpp)
            b.vector.op("tensor_tensor", CON["tmp_f32"][:, TCH:2 * TCH],
                        xp.ld_ap(xpp),
                        XN[:, zc * T + t0: zc * T + t0 + TCH], op=ALU.add)
            xp.wait_out_free(xpp)
            vv = b.vector.op("tensor_tensor", xp.out_ap(xpp),
                             CON["tmp_f32"][:, 0:TCH], CON["tmp_f32"][:, TCH:2 * TCH],
                             op=ALU.add, inc=b.vs)
            xp.end(xpp, x1dram[zc * 128:(zc + 1) * 128, t0:t0 + TCH], vv)


def emit_mha(b, x1dram, xodram, XN, CON, P, PT, w, wqbf, wkbf, wvbf, wobf, boc,
             abuf, AR, add_q=False, scale=0.125, bqc=None, bkc=None, bvc=None,
             q_src=None):
    T, NTC = b.T, b.NTC
    NSC = T // 128
    AR.reset()
    K2 = AR.take_bf(T)
    V2 = AR.take_bf(T)
    Q2 = AR.take_bf(T)
    VT0 = AR.take_bf(NSC * 65)
    VT1 = AR.take_bf(NSC * 65)
    EST = AR.take_bf(TCH)
    NUM = AR.take_bf(TCH)
    STG = AR.take_bf(TCH)
    RV = CON["rv"]
    wstr = WStream(b, "mha_w", NFC, 128, arena=AR)
    AB = AR.take_bf(NFC * TCH)
    WOALL = AR.take_bf(NFC * D)
    psc = [P[4], P[5]]
    pnum = P[6]
    pbc = P[2]
    pqkv = [P[0], P[1]]
    last_qkv_ev = [None, None]
    last_sc_act = [0, 0]
    last_num_ss = 0
    last_num_vs = 0
    last_pbc_vs = 0
    stg_store = None
    q2_cons = [0]

    sqn = b.dma_sem("mha_qn")
    for hp in range(NFC):
        streams = ((wkbf, K2, bkc), (wvbf, V2, bvc))
        if q_src is None:
            streams = streams + ((wqbf, Q2, bqc),)
        else:
            b.sync.wait(b.ts, q2_cons[0])
            b.load(Q2[:, :], q_src[hp * 128:(hp + 1) * 128, :], sqn)
            b.tensor.wait(sqn, sqn.v)
            b.vector.wait(sqn, sqn.v)
        for (wbf, dst, bias) in streams:
            tiles, wv_ = wstr.load_block(wbf, hp * 128)
            b.tensor.wait(*wv_)
            for tc in range(NTC):
                pb = tc % 2
                if last_qkv_ev[pb]:
                    b.tensor.wait(*last_qkv_ev[pb])
                tv = emit_xw_to_psum(
                    b, tiles,
                    lambda k, tc=tc: XN[:, k * T + tc * TCH: k * T + (tc + 1) * TCH],
                    pqkv[pb][:, :], NFC, inc=b.ts)
                if bias is not None:
                    b.vector.wait(b.ts, tv)
                    last_qkv_ev[pb] = (b.vs, b.vector.op(
                        "tensor_scalar", dst[:, tc * TCH:(tc + 1) * TCH],
                        pqkv[pb][:, :], bias[:, hp:hp + 1], None, op0=ALU.add,
                        inc=b.vs))
                else:
                    b.scalar.wait(b.ts, tv)
                    last_qkv_ev[pb] = (b.ss, b.scalar.op(
                        "copy", dst[:, tc * TCH:(tc + 1) * TCH], pqkv[pb][:, :],
                        inc=b.ss))
            wstr.consumed(tv)
        for le in last_qkv_ev:
            if le:
                b.tensor.wait(*le)
        for h2, VT in ((0, VT0), (64, VT1)):
            for c in range(NSC):
                tv = b.tensor.op("transpose", PT[:, 0:64],
                                 V2[h2:h2 + 64, c * 128:(c + 1) * 128],
                                 CON["ident_bf"][h2:h2 + 64, h2:h2 + 64], inc=b.ts)
                b.vector.wait(b.ts, tv)
                vv = b.vector.op("tensor_copy", VT[:, c * 65:c * 65 + 64], PT[:, 0:64],
                                 inc=b.vs)
                b.tensor.wait(b.vs, vv)
            vt3 = VT[:, 0:NSC * 65].rearrange("p (c e) -> p c e", e=65)
            vv = b.vector.op("memset", vt3[:, 0:NSC, 64:65], 1.0, inc=b.vs)
            b.tensor.wait(b.vs, vv)
        for tc in range(NTC):
            for h2, VT in ((0, VT0), (64, VT1)):
                for sc in range(NSC):
                    pb = sc % 2
                    b.tensor.wait(b.ss, last_sc_act[pb])
                    if sc == 0:
                        b.tensor.wait(b.ss, last_num_ss)
                        b.tensor.wait(b.vs, last_num_vs)
                    tv = b.tensor.op("matmul", psc[pb][:, :],
                                     K2[h2:h2 + 64, sc * 128:(sc + 1) * 128],
                                     Q2[h2:h2 + 64, tc * TCH:(tc + 1) * TCH],
                                     start=True, stop=True, inc=b.ts)
                    b.scalar.wait(b.ts, tv)
                    last_sc_act[pb] = b.scalar.op("activation", EST[:], psc[pb][:, :],
                                                  AF.Exp, scale=scale, inc=b.ss)
                    b.tensor.wait(b.ss, last_sc_act[pb])
                    tv = b.tensor.op("matmul", pnum[0:65, :],
                                     VT[:, sc * 65:(sc + 1) * 65],
                                     EST[:], start=(sc == 0), stop=(sc == NSC - 1),
                                     inc=(b.ts if sc == NSC - 1 else None))
                b.vector.wait(b.ts, tv)
                if stg_store:
                    b.vector.wait(*stg_store)
                b.vector.op("tensor_scalar", RV[:], pnum[64:65, :], 1e-8, None,
                            op0=ALU.add)
                vv = b.vector.op("reciprocal", RV[:], RV[:], inc=b.vs)
                b.tensor.wait(b.vs, vv)
                b.tensor.wait(b.vs, last_pbc_vs)
                tv2 = b.tensor.op("matmul", pbc[0:64, :],
                                  CON["ones_row_f32"][0:1, 0:64],
                                  RV[:], start=True, stop=True, inc=b.ts)
                b.scalar.wait(b.ts, tv)
                last_num_ss = b.scalar.op("copy", NUM[0:64, :], pnum[0:64, :], inc=b.ss)
                b.vector.wait(b.ts, tv2)
                b.vector.wait(b.ss, last_num_ss)
                vv = b.vector.op("tensor_tensor", STG[h2:h2 + 64, :], NUM[0:64, :],
                                 pbc[0:64, :], op=ALU.mult, inc=b.vs)
                if add_q:
                    vv = b.vector.op("tensor_tensor", STG[h2:h2 + 64, :],
                                     STG[h2:h2 + 64, :],
                                     Q2[h2:h2 + 64, tc * TCH:(tc + 1) * TCH],
                                     op=ALU.add, inc=b.vs)
                last_pbc_vs = vv
                last_num_vs = vv
            b.sync.wait(b.vs, vv)
            sstg = b.dma_sem("stg")
            b.store(abuf[hp * 128:(hp + 1) * 128, tc * TCH:(tc + 1) * TCH],
                    STG[:, :], sstg)
            stg_store = (sstg, sstg.v)
            q2_cons[0] = b.ts.v
    emit_barrier(b)
    swo = b.dma_sem("ini")
    for oc in range(NFC):
        b.load(WOALL[:, oc * D: (oc + 1) * D],
               wobf.tw.src(wobf.mat, oc * 128, NFC), swo)
    wo_lv = (swo, swo.v)

    def wo_tiles(k, n0, nw=128):
        oc = n0 // 128
        return WOALL[:, oc * D + k * 128: oc * D + (k + 1) * 128]
    po = [P[0], P[1]]
    last_o = [0, 0]
    ab_cons = 0
    sab = b.dma_sem("ab")
    xp = XPipe(b, CON, "m")
    for tc in range(NTC):
        b.sync.wait(b.ts, ab_cons)
        for fc in range(NFC):
            b.load(AB[:, fc * TCH:(fc + 1) * TCH],
                   abuf[fc * 128:(fc + 1) * 128, tc * TCH:(tc + 1) * TCH], sab)
        b.tensor.wait(sab, sab.v)
        b.tensor.wait(*wo_lv)
        for oc in range(NFC):
            pb = oc % 2
            b.tensor.wait(b.vs, last_o[pb])
            tv = emit_xw_to_psum(
                b, [wo_tiles(k, oc * 128) for k in range(NFC)],
                lambda k: AB[:, k * TCH:(k + 1) * TCH],
                po[pb][:, :], NFC, inc=b.ts)
            ab_cons = tv
            xpp = xp.begin(x1dram[oc * 128:(oc + 1) * 128, tc * TCH:(tc + 1) * TCH])
            b.vector.wait(b.ts, tv)
            b.vector.op("tensor_scalar", CON["tmp_f32"][:, 0:TCH], po[pb][:, :],
                        boc[:, oc:oc + 1], None, op0=ALU.add)
            xp.wait_ld(xpp)
            xp.wait_out_free(xpp)
            last_o[pb] = b.vector.op("tensor_tensor", xp.out_ap(xpp),
                                     xp.ld_ap(xpp), CON["tmp_f32"][:, 0:TCH],
                                     op=ALU.add, inc=b.vs)
            xp.end(xpp, xodram[oc * 128:(oc + 1) * 128, tc * TCH:(tc + 1) * TCH],
                   last_o[pb])




def make_router_hook(b, CON, P, w, CWT8, rwT_sb, csb, rb_b=None):
    st_state = {"prev_ev": None}
    RT = CON["rtmp"]

    def hook(tc, t0, xt, a_v, c_v):
        pr = P[4]
        pc = P[5]
        ms_v = b.vector.milestone(b.vs)
        b.tensor.wait(*ms_v)
        b.vector.wait(b.ss, b.ss.v)
        for st_ in range(TCH // 128):
            tt0 = st_ * 128
            if st_state["prev_ev"]:
                b.tensor.wait(*st_state["prev_ev"])
            for fc in range(NFC):
                b.tensor.op("matmul", pr[0:128, 0:8],
                            xt[:, fc * TCH + tt0: fc * TCH + tt0 + 128],
                            rwT_sb[:, fc * 8:(fc + 1) * 8],
                            start=(fc == 0), stop=(fc == NFC - 1))
            b.tensor.op("matmul", pc[0:128, 0:1],
                        a_v[0:1, tt0:tt0 + 128], CON["ones_row_f32"][0:1, 0:1],
                        start=True, stop=True)
            tv = b.tensor.op("matmul", pc[0:128, 1:2],
                             c_v[0:1, tt0:tt0 + 128], CON["ones_row_f32"][0:1, 0:1],
                             start=True, stop=True, inc=b.ts)
            b.vector.wait(b.ts, tv)
            acol = RT[:, 8:9]
            ccol = RT[:, 9:10]
            b.vector.op("tensor_copy", RT[:, 8:10], pc[0:128, 0:2])
            L = RT[:, 0:8]
            b.vector.op("tensor_scalar", L, pr[0:128, 0:8], acol, None,
                        op0=ALU.mult)
            b.vector.op("tensor_scalar", RT[:, 16:24], csb[:], ccol, None,
                        op0=ALU.mult)
            b.vector.op("tensor_tensor", L, L, RT[:, 16:24], op=ALU.add)
            if rb_b is not None:
                b.vector.op("tensor_tensor", L, L, rb_b[:], op=ALU.add)
            m1 = RT[:, 10:11]
            b.vector.op("tensor_reduce", m1, L, axis=AX.X, op=ALU.max)
            nm1 = RT[:, 11:12]
            b.vector.op("tensor_scalar", nm1, m1, -1.0, None, op0=ALU.mult)
            msx = b.vector.milestone(b.vs)
            b.scalar.wait(*msx)
            sv = b.scalar.op("activation", RT[:, 16:24], L, AF.Exp, bias=nm1,
                             inc=b.ss)
            b.vector.wait(b.ss, sv)
            E = RT[:, 16:24]
            se = RT[:, 12:13]
            b.vector.op("tensor_reduce", se, E, axis=AX.X, op=ALU.add)
            b.vector.op("reciprocal", RT[:, 13:14], se)
            S = RT[:, 24:32]
            b.vector.op("tensor_scalar", S, E, RT[:, 13:14], None, op0=ALU.mult)
            mx = RT[:, 10:11]
            b.vector.op("tensor_reduce", mx, S, axis=AX.X, op=ALU.max)
            msk = RT[:, 32:40]
            b.vector.op("tensor_scalar", msk, S, mx, None, op0=ALU.is_ge)
            b.vector.op("scalar_tensor_tensor", RT[:, 16:24], msk, -1e30, S,
                        op0=ALU.mult, op1=ALU.add)
            m2 = RT[:, 11:12]
            b.vector.op("tensor_reduce", m2, RT[:, 16:24], axis=AX.X, op=ALU.max)
            den = RT[:, 12:13]
            b.vector.op("tensor_tensor", den, mx, m2, op=ALU.add)
            b.vector.op("reciprocal", RT[:, 13:14], den)
            b.vector.op("tensor_scalar", msk, S, m2, None, op0=ALU.is_ge)
            b.vector.op("tensor_tensor", S, S, msk, op=ALU.mult)
            b.vector.op("tensor_scalar", S, S, RT[:, 13:14], None, op0=ALU.mult)
            cwb = CON["scratch_bf"][:, 0:8]
            vv2 = b.vector.op("tensor_copy", cwb, S, inc=b.vs)
            b.tensor.wait(b.vs, vv2)
            tv2 = b.tensor.op("transpose", b.PT_BF[0:8, 0:128], cwb,
                              CON["ident_bf"][:, :], inc=b.ts)
            b.scalar.wait(b.ts, tv2)
            ev = b.scalar.op("copy", CWT8[0:8, t0 + tt0: t0 + tt0 + 128],
                             b.PT_BF[0:8, 0:128], inc=b.ss)
            st_state["prev_ev"] = (b.ss, ev)
            b.vector.wait(b.ss, ev)
    return hook


def emit_l1_attn(b, x1dram, xodram, XN, CON, P, w, AR):
    T, NTC = b.T, b.NTC
    AR.reset()
    PL = [AR.take_bf(NFC * TCH) for _ in range(6)]
    FC_ = AR.take_bf(NFC * D)
    FS_ = AR.take_bf(NFC * D)
    wstr = WStream(b, "l1a_w", NFC, 128, arena=AR)
    OT = AR.take_bf(2 * TCH)
    CARRY = b.sb("l1_carry", [128, 16], F32)
    SBF = CON["statv_bf"]
    ONE_PL = CON["scratch_bf2"]
    xp = XPipe(b, CON, "a1")

    sfc = b.dma_sem("l1_fc")
    for k in range(NFC):
        b.load(FC_[:, k * D:(k + 1) * D], w["fftc"][k * 128:(k + 1) * 128, :], sfc)
        b.load(FS_[:, k * D:(k + 1) * D], w["ffts"][k * 128:(k + 1) * 128, :], sfc)
    b.tensor.wait(sfc, sfc.v)
    vv = b.vector.op("memset", ONE_PL[:, 0:TCH], 1.0, inc=b.vs)
    vv = b.vector.op("memset", CARRY[:], 0.0, inc=b.vs)

    pA, pB = P[0], P[1]
    pnrm = P[6]
    pbc = P[2]
    evA = [None, None]
    ot_cons = [None, None]

    def mm_block(tiles_fn, nk, rhs_fn, dst, bias_col, pbi, extra_wait=None,
                 rhs_ready=None):
        pb_ = pbi % 2
        pp = pA if pb_ == 0 else pB
        if evA[pb_]:
            b.tensor.wait(*evA[pb_])
        if extra_wait:
            b.tensor.wait(*extra_wait)
        if rhs_ready:
            b.tensor.wait(*rhs_ready)
        tv = emit_xw_to_psum(b, [tiles_fn(k) for k in range(nk)], rhs_fn,
                             pp[:, :], nk, inc=b.ts)
        b.vector.wait(b.ts, tv)
        if bias_col is not None:
            ev = b.vector.op("tensor_scalar", dst, pp[:, :], bias_col, None,
                             op0=ALU.add, inc=b.vs)
        else:
            ev = b.vector.op("tensor_copy", dst, pp[:, :], inc=b.vs)
        evA[pb_] = (b.vs, ev)
        return tv, (b.vs, ev)

    pbi = [0]

    for tc in range(NTC):
        t0 = tc * TCH
        UK, UV, KR, KI, VR, VI = PL[0], PL[1], PL[2], PL[3], PL[4], PL[5]
        if tc > 0:
            ms_s = b.scalar.milestone(b.ss)
            ms_t = b.tensor.milestone(b.ts)
            b.vector.wait(*ms_s)
            b.vector.wait(*ms_t)
        u_ready = {}
        for (wbf, bias, dst) in ((w["wk_bf"], w["bkc"], UK),
                                 (w["wv_bf"], w["bvc"], UV)):
            for oc in range(NFC):
                tiles, wv_ = wstr.load_block(wbf, oc * 128)
                _, ev = mm_block(lambda k: tiles[k], NFC,
                                 lambda k: XN[:, k * T + t0: k * T + t0 + TCH],
                                 dst[:, oc * TCH:(oc + 1) * TCH],
                                 bias[:, oc:oc + 1], pbi[0], extra_wait=wv_)
                wstr.consumed(b.ts.v)
                pbi[0] += 1
            u_ready[id(dst)] = ev
        for (u_, re_, im_) in ((UK, KR, KI), (UV, VR, VI)):
            for (mat, dst) in ((FC_, re_), (FS_, im_)):
                for oc in range(NFC):
                    mm_block(lambda k, mat=mat: mat[:, k * D + oc * 128: k * D + oc * 128 + 128],
                             NFC,
                             lambda k, u_=u_: u_[:, k * TCH:(k + 1) * TCH],
                             dst[:, oc * TCH:(oc + 1) * TCH], None, pbi[0],
                             rhs_ready=u_ready[id(u_)])
                    pbi[0] += 1
        ms_t = b.tensor.milestone(b.ts)
        b.vector.wait(*ms_t)
        for fcx in range(NFC):
            sl = slice(fcx * TCH, (fcx + 1) * TCH)
            t1_ = CON["tmp_f32"][:, 0:TCH]
            t2_ = CON["tmp_f32"][:, TCH:2 * TCH]
            b.vector.op("tensor_tensor", t1_, KR[:, sl], VR[:, sl], op=ALU.mult)
            b.vector.op("tensor_tensor", t2_, KI[:, sl], VI[:, sl], op=ALU.mult)
            b.vector.op("tensor_tensor", t1_, t1_, t2_, op=ALU.subtract)
            b.vector.op("tensor_tensor", t2_, KR[:, sl], VI[:, sl], op=ALU.mult)
            b.vector.op("tensor_copy", UK[:, sl], t1_)
            b.vector.op("tensor_tensor", t1_, KI[:, sl], VR[:, sl], op=ALU.mult)
            b.vector.op("tensor_tensor", UV[:, sl], t1_, t2_, op=ALU.add)
        for (srcp, dstp, ci) in ((UK, VR, 0), (UV, VI, 1)):
            for fcx in range(NFC):
                sl = slice(fcx * TCH, (fcx + 1) * TCH)
                cc = CARRY[:, ci * NFC + fcx: ci * NFC + fcx + 1]
                b.vector.op("tensor_tensor_scan", dstp[:, sl], ONE_PL[:, 0:TCH],
                            srcp[:, sl], cc, op0=ALU.mult, op1=ALU.add)
                b.vector.op("tensor_copy", cc, dstp[:, sl].tensor[0:128, 0:1]
                            if False else dstp[:, fcx * TCH + TCH - 1: fcx * TCH + TCH])
        ms_t = b.tensor.milestone(b.ts)
        b.vector.wait(*ms_t)
        for oc in range(NFC):
            tiles, wv_ = wstr.load_block(w["wq_bf"], oc * 128)
            _, ev = mm_block(lambda k: tiles[k], NFC,
                             lambda k: XN[:, k * T + t0: k * T + t0 + TCH],
                             UK[:, oc * TCH:(oc + 1) * TCH],
                             w["bqc"][:, oc:oc + 1], pbi[0], extra_wait=wv_)
            wstr.consumed(b.ts.v)
            pbi[0] += 1
        uq_ready = ev
        for (mat, dst) in ((FC_, KR), (FS_, KI)):
            for oc in range(NFC):
                mm_block(lambda k, mat=mat: mat[:, k * D + oc * 128: k * D + oc * 128 + 128],
                         NFC,
                         lambda k: UK[:, k * TCH:(k + 1) * TCH],
                         dst[:, oc * TCH:(oc + 1) * TCH], None, pbi[0],
                         rhs_ready=uq_ready)
                pbi[0] += 1
        ms_t = b.tensor.milestone(b.ts)
        b.vector.wait(*ms_t)
        for fcx in range(NFC):
            sl = slice(fcx * TCH, (fcx + 1) * TCH)
            t1_ = CON["tmp_f32"][:, 0:TCH]
            t2_ = CON["tmp_f32"][:, TCH:2 * TCH]
            b.vector.op("tensor_tensor", t1_, VR[:, sl], KR[:, sl], op=ALU.mult)
            b.vector.op("tensor_tensor", t2_, VI[:, sl], KI[:, sl], op=ALU.mult)
            b.vector.op("tensor_tensor", t1_, t1_, t2_, op=ALU.add)
            b.vector.op("tensor_tensor", t2_, VI[:, sl], KR[:, sl], op=ALU.mult)
            b.vector.op("tensor_copy", UK[:, sl], t1_)
            b.vector.op("tensor_tensor", t1_, VR[:, sl], KI[:, sl], op=ALU.mult)
            b.vector.op("tensor_tensor", UV[:, sl], t2_, t1_, op=ALU.subtract)
        ms_v = b.vector.milestone(b.vs)
        b.scalar.wait(*ms_v)
        b.tensor.wait(*ms_v)
        prev_sq = [None, None]
        idx = 0
        for pl_ in (UK, UV):
            for fcx in range(NFC):
                p_ = idx % 2
                sq_ap = CON["scratch_bf"][:, p_ * TCH:(p_ + 1) * TCH]
                if prev_sq[p_]:
                    b.scalar.wait(*prev_sq[p_])
                sv = b.scalar.op("activation", sq_ap,
                                 pl_[:, fcx * TCH:(fcx + 1) * TCH], AF.Square,
                                 inc=b.ss)
                b.tensor.wait(b.ss, sv)
                tv = b.tensor.op("matmul", pnrm[0:1, :], CON["ones_col_bf"][:],
                                 sq_ap, start=(idx == 0),
                                 stop=(idx == 2 * NFC - 1), inc=b.ts)
                prev_sq[p_] = (b.ts, tv)
                idx += 1
        b.vector.wait(b.ts, tv)
        sv_ = CON["statv"][:, 0:TCH]
        b.vector.op("tensor_copy", sv_, pnrm[0:1, :])
        msv2 = b.vector.milestone(b.vs)
        b.scalar.wait(*msv2)
        sv2 = b.scalar.op("activation", sv_, sv_, AF.Sqrt, inc=b.ss)
        b.vector.wait(b.ss, sv2)
        b.vector.op("tensor_scalar", sv_, sv_, 1e-9, None, op0=ALU.add)
        b.vector.op("reciprocal", sv_, sv_)
        b.vector.op("tensor_scalar", sv_, sv_, 1.0 / 32768.0, None, op0=ALU.mult)
        vv = b.vector.op("tensor_copy", SBF[:, 0:TCH], sv_, inc=b.vs)
        b.tensor.wait(b.vs, vv)
        tvb = b.tensor.op("matmul", pbc[:, :], CON["ones_row_bf"][:],
                          SBF[:, 0:TCH], start=True, stop=True, inc=b.ts)
        for oc in range(NFC):
            pb_ = pbi[0] % 2
            pp = pA if pb_ == 0 else pB
            if evA[pb_]:
                b.tensor.wait(*evA[pb_])
            for k in range(NFC):
                b.tensor.op("matmul", pp[:, :],
                            FC_[:, k * D + oc * 128: k * D + oc * 128 + 128],
                            UK[:, k * TCH:(k + 1) * TCH],
                            start=(k == 0), stop=False)
            for k in range(NFC):
                tv = b.tensor.op("matmul", pp[:, :],
                                 FS_[:, k * D + oc * 128: k * D + oc * 128 + 128],
                                 UV[:, k * TCH:(k + 1) * TCH],
                                 start=False, stop=(k == NFC - 1),
                                 inc=(b.ts if k == NFC - 1 else None))
            pbi[0] += 1
            xpp = xp.begin(x1dram[oc * 128:(oc + 1) * 128, t0:t0 + TCH])
            b.scalar.wait(b.ts, tv)
            osl = oc % 2
            if ot_cons[osl]:
                b.scalar.wait(*ot_cons[osl])
            ot_ap = OT[:, osl * TCH:(osl + 1) * TCH]
            ev = b.scalar.op("copy", ot_ap, pp[:, :], inc=b.ss)
            evA[pb_] = (b.ss, ev)
            b.vector.wait(b.ss, ev)
            b.vector.wait(b.ts, tvb)
            vvo = b.vector.op("tensor_tensor", CON["tmp_f32"][:, 0:TCH], ot_ap,
                              pbc[:, :], op=ALU.mult, inc=b.vs)
            ot_cons[osl] = (b.vs, vvo)
            xp.wait_ld(xpp)
            xp.wait_out_free(xpp)
            vv = b.vector.op("tensor_tensor", xp.out_ap(xpp),
                             xp.ld_ap(xpp), CON["tmp_f32"][:, 0:TCH],
                             op=ALU.add, inc=b.vs)
            xp.end(xpp, xodram[oc * 128:(oc + 1) * 128, t0:t0 + TCH], vv)


def emit_l1_ffn(b, xdram, x1dram, XN, CON, P, w, AR, CWT8, has_shared=False,
                gelu=False):
    T, NTC = b.T, b.NTC
    TH = 1024 if T >= 1024 else T
    NSUB = TH // TCH
    NTH = T // TH
    AR.reset()
    HB = AR.take_bf(16 * TH)
    GT = AR.take_bf(NSUB * TCH)
    BCW = AR.take_bf(NSUB * TCH)
    ACC = AR.take_f32(NFC * TH)
    wgS = WStream(b, "l1f_g", NFC, 128, arena=AR)
    wuS = WStream(b, "l1f_u", NFC, 128, arena=AR) if not gelu else None
    wdS = WStream(b, "l1f_d", 16, 128, arena=AR)
    xp = XPipe(b, CON, "f1")
    pY = [P[4], P[5]]
    pbc = P[6]
    evH = [None] * 4
    evG = [None, None]
    evU = [None, None]
    evY = [None, None]
    evB = None
    bcw_cons = [None]
    n_exp = w["n_experts"]

    for tc in range(NTH):
        t0 = tc * TH
        acc_ev = [None] * NFC
        exps = (["sh"] if has_shared else []) + list(range(n_exp))
        for e in exps:
            shared = (e == "sh")
            if not shared:
                for ts_ in range(NSUB):
                    if evB:
                        b.tensor.wait(*evB)
                    tvb = b.tensor.op("matmul", pbc[:, :],
                                      w["oh8"][:, e * 128:(e + 1) * 128],
                                      CWT8[0:8, t0 + ts_ * TCH: t0 + (ts_ + 1) * TCH],
                                      start=True, stop=True, inc=b.ts)
                    b.scalar.wait(b.ts, tvb)
                    if bcw_cons[0]:
                        b.scalar.wait(*bcw_cons[0])
                    evb2 = b.scalar.op("copy", BCW[:, ts_ * TCH:(ts_ + 1) * TCH],
                                       pbc[:, :], inc=b.ss)
                    evB = (b.ss, evb2)
            for half in range(2):
                hbase = half * 16
                for oc in range(16):
                    oca = hbase + oc
                    slot = oc % 2
                    if gelu:
                        wmat = w["sh_w1"] if shared else w["sp_w1"][e]
                        bias = w["sh_b1c"] if shared else w["sp_b1c"][e]
                        tiles, wv_ = wgS.load_block(wmat, oca * 128)
                        b.tensor.wait(*wv_)
                        for ts_ in range(NSUB):
                            bk = P[2 * slot + ts_]
                            if evH[2 * slot + ts_]:
                                b.tensor.wait(*evH[2 * slot + ts_])
                            tv = emit_xw_to_psum(
                                b, tiles,
                                lambda k, ts_=ts_: XN[:, k * T + t0 + ts_ * TCH:
                                                      k * T + t0 + (ts_ + 1) * TCH],
                                bk[:, :], NFC, inc=b.ts)
                        wgS.consumed(tv)
                        b.scalar.wait(b.ts, tv)
                        for ts_ in range(NSUB):
                            bk = P[2 * slot + ts_]
                            ev = b.scalar.op(
                                "activation",
                                HB[:, oc * TH + ts_ * TCH: oc * TH + (ts_ + 1) * TCH],
                                bk[:, :], AF.Gelu_apprx_tanh,
                                bias=bias[:, oca:oca + 1], inc=b.ss)
                            evH[2 * slot + ts_] = (b.ss, ev)
                    else:
                        tiles, wv_ = wgS.load_block(w["wg"][e], oca * 128)
                        b.tensor.wait(*wv_)
                        for ts_ in range(NSUB):
                            if evG[ts_]:
                                b.tensor.wait(*evG[ts_])
                            tv = emit_xw_to_psum(
                                b, tiles,
                                lambda k, ts_=ts_: XN[:, k * T + t0 + ts_ * TCH:
                                                      k * T + t0 + (ts_ + 1) * TCH],
                                P[ts_][:, :], NFC, inc=b.ts)
                        wgS.consumed(tv)
                        tvg = tv
                        tiles2, wv2 = wuS.load_block(w["wu"][e], oca * 128)
                        b.tensor.wait(*wv2)
                        for ts_ in range(NSUB):
                            if evU[ts_]:
                                b.tensor.wait(*evU[ts_])
                            tv2 = emit_xw_to_psum(
                                b, tiles2,
                                lambda k, ts_=ts_: XN[:, k * T + t0 + ts_ * TCH:
                                                      k * T + t0 + (ts_ + 1) * TCH],
                                P[2 + ts_][:, :], NFC, inc=b.ts)
                        wuS.consumed(tv2)
                        b.scalar.wait(b.ts, tvg)
                        for ts_ in range(NSUB):
                            sv = b.scalar.op("activation",
                                             GT[:, ts_ * TCH:(ts_ + 1) * TCH],
                                             P[ts_][:, :], AF.Silu, inc=b.ss)
                            evG[ts_] = (b.ss, sv)
                        b.vector.wait(b.ss, sv)
                        b.vector.wait(b.ts, tv2)
                        for ts_ in range(NSUB):
                            ev = b.vector.op(
                                "tensor_tensor",
                                HB[:, oc * TH + ts_ * TCH: oc * TH + (ts_ + 1) * TCH],
                                GT[:, ts_ * TCH:(ts_ + 1) * TCH],
                                P[2 + ts_][:, :], op=ALU.mult, inc=b.vs)
                            evU[ts_] = (b.vs, ev)
                for le in (evH + evG + evU):
                    if le:
                        b.tensor.wait(*le)
                for zc in range(NFC):
                    wmat = (w["sh_w2"] if shared else w["sp_w2"][e]) if gelu \
                        else w["wd"][e]
                    tiles, wv_ = wdS.load_block(wmat, zc * 128, k0=hbase)
                    b.tensor.wait(*wv_)
                    for ts_ in range(NSUB):
                        pb_ = (zc * NSUB + ts_) % 2
                        if evY[pb_]:
                            b.tensor.wait(*evY[pb_])
                        tv = emit_xw_to_psum(
                            b, tiles,
                            lambda k, ts_=ts_: HB[:, k * TH + ts_ * TCH:
                                                  k * TH + (ts_ + 1) * TCH],
                            pY[pb_][:, :], 16, inc=b.ts)
                        b.vector.wait(b.ts, tv)
                        acc_ap = ACC[:, zc * TH + ts_ * TCH: zc * TH + (ts_ + 1) * TCH]
                        first = (e == exps[0]) and half == 0
                        tmp = CON["tmp_f32"][:, 0:TCH]
                        py_ap = pY[pb_][:, :]
                        add_bias = (half == 1)
                        if shared:
                            if add_bias:
                                b.vector.op("tensor_scalar", tmp, py_ap,
                                            w["sh_b2c"][:, zc:zc + 1], None,
                                            op0=ALU.add)
                                src_ap = tmp
                            else:
                                src_ap = py_ap
                            if first:
                                ev = b.vector.op("tensor_copy", acc_ap, src_ap,
                                                 inc=b.vs)
                            else:
                                ev = b.vector.op("tensor_tensor", acc_ap, acc_ap,
                                                 src_ap, op=ALU.add, inc=b.vs)
                        else:
                            b.vector.wait(*evB)
                            if gelu and add_bias:
                                b.vector.op("tensor_scalar", tmp, py_ap,
                                            w["sp_b2c"][e][:, zc:zc + 1], None,
                                            op0=ALU.add)
                                ev0 = b.vector.op(
                                    "tensor_tensor", tmp, tmp,
                                    BCW[:, ts_ * TCH:(ts_ + 1) * TCH],
                                    op=ALU.mult, inc=b.vs)
                            else:
                                ev0 = b.vector.op(
                                    "tensor_tensor", tmp, py_ap,
                                    BCW[:, ts_ * TCH:(ts_ + 1) * TCH],
                                    op=ALU.mult, inc=b.vs)
                            bcw_cons[0] = (b.vs, ev0)
                            if first:
                                ev = b.vector.op("tensor_copy", acc_ap, tmp,
                                                 inc=b.vs)
                            else:
                                ev = b.vector.op("tensor_tensor", acc_ap, acc_ap,
                                                 tmp, op=ALU.add, inc=b.vs)
                        evY[pb_] = (b.vs, ev)
                        acc_ev[zc] = (b.vs, ev)
                    wdS.consumed(tv)
        for zc in range(NFC):
            for ts_ in range(NSUB):
                tt0 = t0 + ts_ * TCH
                xpp = xp.begin(xdram[zc * 128:(zc + 1) * 128, tt0:tt0 + TCH])
                xp.wait_ld(xpp)
                xp.wait_out_free(xpp)
                if acc_ev[zc]:
                    b.vector.wait(*acc_ev[zc])
                vv = b.vector.op("tensor_tensor", xp.out_ap(xpp), xp.ld_ap(xpp),
                                 ACC[:, zc * TH + ts_ * TCH:
                                     zc * TH + (ts_ + 1) * TCH],
                                 op=ALU.add, inc=b.vs)
                xp.end(xpp, x1dram[zc * 128:(zc + 1) * 128, tt0:tt0 + TCH], vv)


def emit_block1(b, CON, P, PT, xin, xout):
    T = b.T
    x1 = b.dram("x1scr1", [D, T], F32, kind="Internal")
    w = {
        "wqT": b.dram("L1_wqT", [D, D], F32),
        "wkT": b.dram("L1_wkT", [D, D], F32),
        "wvT": b.dram("L1_wvT", [D, D], F32),
        "fftc": b.dram("L1_fftc", [D, D], BF16),
        "ffts": b.dram("L1_ffts", [D, D], BF16),
        "n_experts": 8,
    }
    w["wq_bf"] = TW(b, "L1_wq_bf", D, D).view()
    w["wk_bf"] = TW(b, "L1_wk_bf", D, D).view()
    w["wv_bf"] = TW(b, "L1_wv_bf", D, D).view()
    w["bkc"] = load_colvec(b, "l1bkc", NFC, b.dram("L1_bkc", [128, NFC], F32))
    w["bvc"] = load_colvec(b, "l1bvc", NFC, b.dram("L1_bvc", [128, NFC], F32))
    w["bqc"] = load_colvec(b, "l1bqc", NFC, b.dram("L1_bqc", [128, NFC], F32))
    wg_f = b.dram("L1_wg", [8 * D, 4096], F32)
    wu_f = b.dram("L1_wu", [8 * D, 4096], F32)
    wd_f = b.dram("L1_wd", [8 * 4096, D], F32)
    wg_bf = TW(b, "L1_wg_bf", D, 4096, nmat=8)
    wu_bf = TW(b, "L1_wu_bf", D, 4096, nmat=8)
    wd_bf = TW(b, "L1_wd_bf", 4096, D, nmat=8)
    w["wg"] = [wg_bf[e] for e in range(8)]
    w["wu"] = [wu_bf[e] for e in range(8)]
    w["wd"] = [wd_bf[e] for e in range(8)]
    rwT = b.dram("L1_rwT", [D, 8], F32)
    w["oh8"] = load_oh8(b, CON)
    jobs = [
        (w["wqT"], None, w["wq_bf"].tw, D, D),
        (w["wkT"], None, w["wk_bf"].tw, D, D),
        (w["wvT"], None, w["wv_bf"].tw, D, D),
        (wg_f, None, wg_bf, 8 * D, 4096),
        (wu_f, None, wu_bf, 8 * D, 4096),
        (wd_f, None, wd_bf, 8 * 4096, D),
    ]
    if getattr(b, "XN", None) is None:
        b.XN = b.sb("XN", [128, NFC * b.T], BF16)
    XN = b.XN
    emit_precast(b, CON, XN[:, 0:2048], jobs)
    ARB = get_arb(b)
    rwT_sb = ARB.take_f32(NFC * 8)
    srw = b.dma_sem("ini_l1rw")
    for fc in range(NFC):
        b.load(rwT_sb[:, fc * 8:(fc + 1) * 8], rwT[fc * 128:(fc + 1) * 128, :], srw)
    csb = ARB.take_f32(8)
    b.tensor.wait(srw, srw.v)
    tvc = None
    for fc in range(NFC):
        tvc = b.tensor.op("matmul", P[6][0:1, 0:8], CON["ones_col_f32"][:],
                          rwT_sb[:, fc * 8:(fc + 1) * 8],
                          start=(fc == 0), stop=(fc == NFC - 1), inc=b.ts)
    b.vector.wait(b.ts, tvc)
    vvc = b.vector.op("tensor_copy", CON["statv"][0:1, 0:8], P[6][0:1, 0:8],
                      inc=b.vs)
    b.tensor.wait(b.vs, vvc)
    tvc = b.tensor.op("matmul", P[5][0:128, 0:8], CON["ones_row_f32"][0:1, :],
                      CON["statv"][0:1, 0:8], start=True, stop=True, inc=b.ts)
    b.vector.wait(b.ts, tvc)
    b.vector.op("tensor_copy", csb[:], P[5][0:128, 0:8], inc=b.vs)
    CWT8 = ARB.take_bf(T)[0:8, :]
    hook = make_router_hook(b, CON, P, w, CWT8, rwT_sb, csb)
    AR = b.AR if getattr(b, "AR", None) is not None else Arena(b, "AR", 44032)
    b.AR = AR
    emit_ln(b, xin, XN, CON, P, tc_hook=hook)
    emit_barrier(b)
    emit_l1_ffn(b, xin, x1, XN, CON, P, w, AR, CWT8)
    emit_barrier(b)
    emit_l1_attn(b, x1, xout, XN, CON, P, w, AR)


def load_oh8(b, CON):
    if getattr(b, "_oh8", None) is not None:
        return b._oh8
    t = b.sb("oh8", [8, 8 * 128], BF16)
    s = b.dma_sem("ini_oh8")
    b.load(t[:], b.dram("oh8_in", [8, 8 * 128], BF16), s)
    b.tensor.wait(s, s.v)
    b._oh8 = t
    return t


def alloc_common(b):
    CON = {}
    CON["ones_col_bf"] = b.sb("ones_col", [128, 1], BF16)
    CON["ones_col_f32"] = b.sb("ones_col32", [128, 1], F32)
    CON["rtmp"] = b.sb("rtmp", [128, 40], F32)
    CON["ones_row_bf"] = b.sb("ones_row", [1, 128], BF16)
    CON["ones_row_f32"] = b.sb("ones_row32", [1, 128], F32)
    CON["ident_bf"] = b.sb("ident_bf", [128, 128], BF16)
    CON["scratch_bf"] = b.sb("scratch_bf", [128, 2 * TCH], BF16)
    CON["scratch_bf2"] = b.sb("scratch_bf2", [128, 2 * TCH], BF16)
    CON["statv"] = b.sb("statv", [1, 4 * TCH], F32)
    CON["statv_bf"] = b.sb("statv_bf", [1, 2 * TCH], BF16)
    CON["tmp_f32"] = b.sb("tmp_f32", [128, 2 * TCH], F32)
    CON["xtile"] = b.sb("xtile", [128, 4 * TCH], F32)
    CON["lnx"] = b.sb("lnx", [128, NFC * TCH], F32)
    CON["rv"] = b.sb("con_rv", [1, TCH], F32)
    ident = b.dram("ident_in", [128, 128], F32)
    sini = b.dma_sem("ini_id")
    b.load(CON["xtile"][:, 0:128], ident, sini)
    b.vector.wait(sini, sini.v)
    b.vector.op("tensor_copy", CON["ident_bf"][:], CON["xtile"][:, 0:128])
    b.vector.op("memset", CON["ones_col_bf"][:], 1.0)
    b.vector.op("memset", CON["ones_col_f32"][:], 1.0)
    b.vector.op("memset", CON["ones_row_bf"][:], 1.0)
    vv = b.vector.op("memset", CON["ones_row_f32"][:], 1.0, inc=b.vs)
    b.tensor.wait(b.vs, vv)
    b.scalar.wait(b.vs, vv)
    return CON


def get_arb(b):
    if getattr(b, "ARB", None) is None:
        b.ARB = Arena(b, "ARB", 4992)
    b.ARB.reset()
    return b.ARB


def load_colvec_arb(b, arb, n_chunks, dram_ap, sem_name):
    t = arb.take_f32(n_chunks)
    s = b.dma_sem(sem_name)
    b.load(t, dram_ap, s)
    b.scalar.wait(s, s.v)
    b.vector.wait(s, s.v)
    return t


def load_colvec(b, name, n_chunks, dram_ap):
    t = b.sb(name, [128, n_chunks], F32)
    sini = b.dma_sem("ini_" + name)
    b.load(t[:], dram_ap, sini)
    b.scalar.wait(sini, sini.v)
    b.vector.wait(sini, sini.v)
    return t


def emit_block0(b, CON, P, PT, xin, xout):
    T = b.T
    x1 = b.dram("x1scr", [D, T], F32, kind="Internal")
    abuf = b.dram("abuf0", [D, T], BF16, kind="Internal")
    w = {
        "wqT": b.dram("L0_wqT", [D, D], F32),
        "wkT": b.dram("L0_wkT", [D, D], F32),
        "wvT": b.dram("L0_wvT", [D, D], F32),
        "woT": b.dram("L0_woT", [D, D], F32),
        "fc1_q": b.dram("L0_fc1_q", [D, 4096], I32),
        "fc1_se": b.dram("L0_fc1_se", [D, 4096], F32),
        "fc2_q": b.dram("L0_fc2_q", [4096, D], I32),
        "fc2_se": b.dram("L0_fc2_se", [4096, D], F32),
    }
    wq_bf = TW(b, "L0_wq_bf", D, D)
    wk_bf = TW(b, "L0_wk_bf", D, D)
    wv_bf = TW(b, "L0_wv_bf", D, D)
    wo_bf = TW(b, "L0_wo_bf", D, D)
    w1_bf = TW(b, "L0_w1_bf", D, 4096)
    w2_bf = TW(b, "L0_w2_bf", 4096, D)
    w["w1bf"] = w1_bf.view()
    w["w2bf"] = w2_bf.view()
    w["g1c"] = load_colvec(b, "g1c", NFC, b.dram("L0_g1c", [128, NFC], F32))
    w["b1c"] = load_colvec(b, "b1c", NFC, b.dram("L0_b1c", [128, NFC], F32))
    w["fb1c"] = load_colvec(b, "fb1c", 32, b.dram("L0_fb1c", [128, 32], F32))
    w["fb2c"] = load_colvec(b, "fb2c", NFC, b.dram("L0_fb2c", [128, NFC], F32))
    w["boc"] = load_colvec(b, "boc", NFC, b.dram("L0_boc", [128, NFC], F32))
    if getattr(b, "XN", None) is None:
        b.XN = b.sb("XN", [128, NFC * b.T], BF16)
    XN = b.XN
    AR = b.AR if getattr(b, "AR", None) is not None else Arena(b, "AR", 44032)
    b.AR = AR
    emit_precast(b, CON, XN[:, 0:2048], [
        (w["fc1_q"], w["fc1_se"], w1_bf, D, 4096),
        (w["fc2_q"], w["fc2_se"], w2_bf, 4096, D),
        (w["wqT"], None, wq_bf, D, D),
        (w["wkT"], None, wk_bf, D, D),
        (w["wvT"], None, wv_bf, D, D),
        (w["woT"], None, wo_bf, D, D),
    ])
    emit_ln(b, xin, XN, CON, P)
    emit_barrier(b)
    emit_l0_ffn(b, xin, x1, XN, CON, P, w, AR)
    emit_barrier(b)
    emit_mha(b, x1, xout, XN, CON, P, PT, w, wq_bf.view(), wk_bf.view(),
             wv_bf.view(), wo_bf.view(), w["boc"], abuf, AR)


def host_prep_l0(inputs):
    C = np.ascontiguousarray
    def colvec(v):
        return C(np.asarray(v).reshape(-1, 128).T)
    return {
        "L0_wqT": C(np.asarray(inputs["L0_attn_Wq"]).T),
        "L0_wkT": C(np.asarray(inputs["L0_attn_Wk"]).T),
        "L0_wvT": C(np.asarray(inputs["L0_attn_Wv"]).T),
        "L0_woT": C(np.asarray(inputs["L0_attn_Wo"]).T),
        "L0_boc": colvec(inputs["L0_attn_bo"]),
        "L0_g1c": colvec(inputs["L0_ffn_ln_gamma"]),
        "L0_b1c": colvec(inputs["L0_ffn_ln_beta"]),
        "L0_fc1_q": C(np.asarray(inputs["L0_ffn_fc1_q"])),
        "L0_fc1_se": C(np.repeat(np.asarray(inputs["L0_ffn_fc1_scales"]), 128, axis=0)),
        "L0_fb1c": colvec(inputs["L0_ffn_fc1_bias"]),
        "L0_fc2_q": C(np.asarray(inputs["L0_ffn_fc2_q"])),
        "L0_fc2_se": C(np.repeat(np.asarray(inputs["L0_ffn_fc2_scales"]), 128, axis=0)),
        "L0_fb2c": colvec(inputs["L0_ffn_fc2_bias"]),
        "ident_in": np.eye(128, dtype=np.float32),
    }


def build_block0(T=4096):
    nc = bass.Bass("TRN2", target_bir_lowering=False, debug=False)
    es = ExitStack()
    es.__enter__()
    b = Builder(nc, es, T)
    xin = b.dram("xin", [D, T], F32)
    xout = b.dram("xout", [D, T], F32, kind="ExternalOutput")
    P = [b.ps(f"P{i}", [128, TCH], F32) for i in range(7)]
    PT = b.ps("PT", [128, 1024], BF16)
    b.PT_BF = PT
    CON = alloc_common(b)
    emit_block0(b, CON, P, PT, xin, xout)
    b.finish()
    return nc, es



def emit_l2_qprep(b, XN, CON, P, w, qn_dram, AR):
    T, NTC = b.T, b.NTC
    AR.reset()
    QTC = AR.take_bf(NFC * TCH)
    wstrq = WStream(b, "l2q_w", NFC, 128, arena=AR)
    pq = [P[4], P[5]]
    ab, cb = P[2], P[3]
    evq = [None, None]
    sqs = b.dma_sem("l2_qn_st")
    for tc in range(NTC):
        t0 = tc * TCH
        b.vector.wait(sqs, sqs.v)
        for oc in range(NFC):
            tiles, wv_ = wstrq.load_block(w["wq_bf"], oc * 128)
            b.tensor.wait(*wv_)
            pb = oc % 2
            if evq[pb]:
                b.tensor.wait(*evq[pb])
            tv = emit_xw_to_psum(
                b, tiles, lambda k: XN[:, k * T + t0: k * T + t0 + TCH],
                pq[pb][:, :], NFC, inc=b.ts)
            wstrq.consumed(tv)
            b.vector.wait(b.ts, tv)
            ev = b.vector.op("tensor_scalar", QTC[:, oc * TCH:(oc + 1) * TCH],
                             pq[pb][:, :], w["bqc"][:, oc:oc + 1], None,
                             op0=ALU.add, inc=b.vs)
            evq[pb] = (b.vs, ev)
        ms_v = b.vector.milestone(b.vs)
        b.tensor.wait(*ms_v)
        b.scalar.wait(*ms_v)
        tv = emit_stats(b, CON, P,
                        lambda fc: QTC[:, fc * TCH:(fc + 1) * TCH],
                        NFC, 1e-5)
        b.vector.wait(b.ts, tv)
        b.sync.wait(sqs, sqs.v)
        for fc in range(NFC):
            tmp = CON["tmp_f32"]
            b.vector.op("tensor_tensor", tmp[:, 0:TCH],
                        QTC[:, fc * TCH:(fc + 1) * TCH], ab[:, :], op=ALU.mult)
            b.vector.op("tensor_tensor", tmp[:, TCH:2 * TCH], tmp[:, 0:TCH],
                        cb[:, :], op=ALU.add)
            vv = b.vector.op("tensor_scalar", QTC[:, fc * TCH:(fc + 1) * TCH],
                             tmp[:, TCH:2 * TCH], w["lngc"][:, fc:fc + 1],
                             w["lnbc"][:, fc:fc + 1], op0=ALU.mult, op1=ALU.add,
                             inc=b.vs)
            b.sync.wait(b.vs, vv)
            b.store(qn_dram[fc * 128:(fc + 1) * 128, t0:t0 + TCH],
                    QTC[:, fc * TCH:(fc + 1) * TCH], sqs)
        ms2 = b.vector.milestone(b.vs)
        b.tensor.wait(*ms2)


def emit_block2(b, CON, P, PT, xin, xout):
    T = b.T
    x1 = b.dram("x1scr2", [D, T], F32, kind="Internal")
    abuf = b.dram("abuf2", [D, T], BF16, kind="Internal")
    qn_dram = b.dram("l2_qn", [D, T], BF16, kind="Internal")
    w = {}
    names = ["Wq", "Wk", "Wv", "Wo"]
    f32_in = {}
    for n in names:
        f32_in[n] = b.dram("L2_" + n, [D, D], F32)
        w[n.lower() + "_bf"] = TW(b, "L2_" + n + "_bf", D, D).view()
    w["bqc"] = load_colvec(b, "l2bqc", NFC, b.dram("L2_bqc", [128, NFC], F32))
    w["bkc"] = load_colvec(b, "l2bkc", NFC, b.dram("L2_bkc", [128, NFC], F32))
    w["bvc"] = load_colvec(b, "l2bvc", NFC, b.dram("L2_bvc", [128, NFC], F32))
    w["boc"] = load_colvec(b, "l2boc", NFC, b.dram("L2_boc", [128, NFC], F32))
    w["lngc"] = load_colvec(b, "l2lngc", NFC, b.dram("L2_lngc", [128, NFC], F32))
    w["lnbc"] = load_colvec(b, "l2lnbc", NFC, b.dram("L2_lnbc", [128, NFC], F32))
    shw1_f = b.dram("L2_shw1T", [D, 4096], F32)
    shw2_f = b.dram("L2_shw2T", [4096, D], F32)
    spw1_f = b.dram("L2_spw1T", [8 * D, 4096], F32)
    spw2_f = b.dram("L2_spw2T", [8 * 4096, D], F32)
    shw1_bf = TW(b, "L2_shw1_bf", D, 4096)
    shw2_bf = TW(b, "L2_shw2_bf", 4096, D)
    spw1_bf = TW(b, "L2_spw1_bf", D, 4096, nmat=8)
    spw2_bf = TW(b, "L2_spw2_bf", 4096, D, nmat=8)
    w["sh_w1"] = shw1_bf.view()
    w["sh_w2"] = shw2_bf.view()
    w["sp_w1"] = [spw1_bf[e] for e in range(8)]
    w["sp_w2"] = [spw2_bf[e] for e in range(8)]
    w["sh_b1c"] = load_colvec(b, "l2shb1", 32, b.dram("L2_shb1c", [128, 32], F32))
    w["sh_b2c"] = load_colvec(b, "l2shb2", NFC, b.dram("L2_shb2c", [128, NFC], F32))
    ARB = get_arb(b)
    spb1 = ARB.take_f32(8 * 32)
    spb2 = ARB.take_f32(8 * NFC)
    scv = b.dma_sem("ini_l2spb")
    b.load(spb1, b.dram("L2_spb1c", [128, 8 * 32], F32), scv)
    b.load(spb2, b.dram("L2_spb2c", [128, 8 * NFC], F32), scv)
    b.scalar.wait(scv, scv.v)
    b.vector.wait(scv, scv.v)
    w["sp_b1c"] = [spb1[:, e * 32:(e + 1) * 32] for e in range(8)]
    w["sp_b2c"] = [spb2[:, e * NFC:(e + 1) * NFC] for e in range(8)]
    rwT = b.dram("L2_rwT", [D, 8], F32)
    rbb = ARB.take_f32(8)
    srb = b.dma_sem("ini_l2rb")
    b.load(rbb[:], b.dram("L2_rbb", [128, 8], F32), srb)
    b.vector.wait(srb, srb.v)
    w["oh8"] = load_oh8(b, CON)
    w["n_experts"] = 8
    jobs = [(f32_in[n], None, w[n.lower() + "_bf"].tw, D, D) for n in names]
    jobs += [
        (shw1_f, None, shw1_bf, D, 4096),
        (shw2_f, None, shw2_bf, 4096, D),
        (spw1_f, None, spw1_bf, 8 * D, 4096),
        (spw2_f, None, spw2_bf, 8 * 4096, D),
    ]
    if getattr(b, "XN", None) is None:
        b.XN = b.sb("XN", [128, NFC * b.T], BF16)
    XN = b.XN
    AR = b.AR if getattr(b, "AR", None) is not None else Arena(b, "AR", 44032)
    b.AR = AR
    emit_precast(b, CON, XN[:, 0:2048], jobs)
    rwT_sb = ARB.take_f32(NFC * 8)
    srw = b.dma_sem("ini_l2rw")
    for fc in range(NFC):
        b.load(rwT_sb[:, fc * 8:(fc + 1) * 8], rwT[fc * 128:(fc + 1) * 128, :], srw)
    csb = ARB.take_f32(8)
    b.tensor.wait(srw, srw.v)
    tvc = None
    for fc in range(NFC):
        tvc = b.tensor.op("matmul", P[6][0:1, 0:8], CON["ones_col_f32"][:],
                          rwT_sb[:, fc * 8:(fc + 1) * 8],
                          start=(fc == 0), stop=(fc == NFC - 1), inc=b.ts)
    b.vector.wait(b.ts, tvc)
    vvc = b.vector.op("tensor_copy", CON["statv"][0:1, 0:8], P[6][0:1, 0:8],
                      inc=b.vs)
    b.tensor.wait(b.vs, vvc)
    tvc = b.tensor.op("matmul", P[5][0:128, 0:8], CON["ones_row_f32"][0:1, :],
                      CON["statv"][0:1, 0:8], start=True, stop=True, inc=b.ts)
    b.vector.wait(b.ts, tvc)
    b.vector.op("tensor_copy", csb[:], P[5][0:128, 0:8], inc=b.vs)
    CWT8 = ARB.take_bf(T)[0:8, :]
    hook = make_router_hook(b, CON, P, w, CWT8, rwT_sb, csb, rb_b=rbb)
    emit_ln(b, xin, XN, CON, P, tc_hook=hook)
    emit_barrier(b)
    emit_l2_qprep(b, XN, CON, P, w, qn_dram, AR)
    emit_barrier(b)
    emit_l1_ffn(b, xin, x1, XN, CON, P, w, AR, CWT8, has_shared=True, gelu=True)
    emit_barrier(b)
    emit_mha(b, x1, xout, XN, CON, P, PT, w, w["wq_bf"], w["wk_bf"], w["wv_bf"],
             w["wo_bf"], w["boc"], abuf, AR, add_q=True, bkc=w["bkc"],
             bvc=w["bvc"], q_src=qn_dram)


def build_block2(T=4096):
    nc = bass.Bass("TRN2", target_bir_lowering=False, debug=False)
    es = ExitStack()
    es.__enter__()
    b = Builder(nc, es, T)
    xin = b.dram("xin", [D, T], F32)
    xout = b.dram("xout", [D, T], F32, kind="ExternalOutput")
    P = [b.ps(f"P{i}", [128, TCH], F32) for i in range(7)]
    PT = b.ps("PT", [128, 1024], BF16)
    b.PT_BF = PT
    CON = alloc_common(b)
    emit_block2(b, CON, P, PT, xin, xout)
    b.finish()
    return nc, es


def host_prep_l2(inputs):
    C = np.ascontiguousarray
    import ml_dtypes

    def colvec(v):
        return C(np.asarray(v).reshape(-1, 128).T)
    oh8 = np.zeros((8, 8 * 128), dtype=ml_dtypes.bfloat16)
    for e in range(8):
        oh8[e, e * 128:(e + 1) * 128] = 1.0
    sw1 = np.asarray(inputs["L2_ffn_spec_w1"])
    sw2 = np.asarray(inputs["L2_ffn_spec_w2"])
    return {
        "L2_Wq": C(np.asarray(inputs["L2_attn_Wq"])),
        "L2_Wk": C(np.asarray(inputs["L2_attn_Wk"])),
        "L2_Wv": C(np.asarray(inputs["L2_attn_Wv"])),
        "L2_Wo": C(np.asarray(inputs["L2_attn_Wo"])),
        "L2_bqc": colvec(inputs["L2_attn_bq"]),
        "L2_bkc": colvec(inputs["L2_attn_bk"]),
        "L2_bvc": colvec(inputs["L2_attn_bv"]),
        "L2_boc": colvec(inputs["L2_attn_bo"]),
        "L2_lngc": colvec(inputs["L2_attn_ln_g"]),
        "L2_lnbc": colvec(inputs["L2_attn_ln_b"]),
        "L2_shw1T": C(np.asarray(inputs["L2_ffn_shared_w1"]).T),
        "L2_shw2T": C(np.asarray(inputs["L2_ffn_shared_w2"]).T),
        "L2_spw1T": C(sw1.transpose(0, 2, 1).reshape(8 * D, 4096)),
        "L2_spw2T": C(sw2.transpose(0, 2, 1).reshape(8 * 4096, D)),
        "L2_shb1c": colvec(inputs["L2_ffn_shared_b1"]),
        "L2_shb2c": colvec(inputs["L2_ffn_shared_b2"]),
        "L2_spb1c": C(np.concatenate(
            [np.asarray(inputs["L2_ffn_spec_b1"])[e].reshape(32, 128).T
             for e in range(8)], axis=1)),
        "L2_spb2c": C(np.concatenate(
            [np.asarray(inputs["L2_ffn_spec_b2"])[e].reshape(NFC, 128).T
             for e in range(8)], axis=1)),
        "L2_rwT": C(np.asarray(inputs["L2_ffn_router_w"]).T),
        "L2_rbb": C(np.broadcast_to(np.asarray(inputs["L2_ffn_router_b"]),
                                    (128, 8)).copy()),
        "oh8_in": oh8,
        "ident_in": np.eye(128, dtype=np.float32),
    }



def emit_l3_ffn(b, xdram, x1dram, XN, CON, P, w, AR):
    T, NTC = b.T, b.NTC
    AR.reset()
    PB = AR.take_bf(2 * TCH)
    HBUF = AR.take_bf(32 * TCH)
    QB = AR.take_bf(2 * TCH)
    u1S = WStream(b, "l3_u1", NFC, 128, arena=AR)
    v1S = WStream(b, "l3_v1", 2, 128, arena=AR)
    u2S = WStream(b, "l3_u2", 32, 128, arena=AR)
    v2S = WStream(b, "l3_v2", 2, 128, arena=AR)
    xp = XPipe(b, CON, "f3")
    pA, pB_ = P[4], P[5]
    ev = [None, None]
    for tc in range(NTC):
        t0 = tc * TCH
        for oc in range(2):
            tiles, wv_ = u1S.load_block(w["u1_bf"], oc * 128)
            b.tensor.wait(*wv_)
            pb = oc % 2
            if ev[pb]:
                b.tensor.wait(*ev[pb])
            tv = emit_xw_to_psum(b, tiles,
                                 lambda k: XN[:, k * T + t0: k * T + t0 + TCH],
                                 (pA if pb == 0 else pB_)[:, :], NFC, inc=b.ts)
            u1S.consumed(tv)
            b.vector.wait(b.ts, tv)
            e_ = b.vector.op("tensor_copy", PB[:, oc * TCH:(oc + 1) * TCH],
                             (pA if pb == 0 else pB_)[:, :], inc=b.vs)
            ev[pb] = (b.vs, e_)
        msv = b.vector.milestone(b.vs)
        b.tensor.wait(*msv)
        for oc in range(32):
            tiles, wv_ = v1S.load_block(w["v1_bf"], oc * 128)
            b.tensor.wait(*wv_)
            pb = oc % 2
            if ev[pb]:
                b.tensor.wait(*ev[pb])
            tv = emit_xw_to_psum(b, tiles,
                                 lambda k: PB[:, k * TCH:(k + 1) * TCH],
                                 (pA if pb == 0 else pB_)[:, :], 2, inc=b.ts)
            v1S.consumed(tv)
            b.scalar.wait(b.ts, tv)
            e_ = b.scalar.op("activation", HBUF[:, oc * TCH:(oc + 1) * TCH],
                             (pA if pb == 0 else pB_)[:, :], AF.Gelu_apprx_tanh,
                             bias=w["b1c"][:, oc:oc + 1], inc=b.ss)
            ev[pb] = (b.ss, e_)
        for le in ev:
            if le:
                b.tensor.wait(*le)
        for oc in range(2):
            tiles, wv_ = u2S.load_block(w["u2_bf"], oc * 128)
            b.tensor.wait(*wv_)
            pb = oc % 2
            if ev[pb]:
                b.tensor.wait(*ev[pb])
            tv = emit_xw_to_psum(b, tiles,
                                 lambda k: HBUF[:, k * TCH:(k + 1) * TCH],
                                 (pA if pb == 0 else pB_)[:, :], 32, inc=b.ts)
            u2S.consumed(tv)
            b.vector.wait(b.ts, tv)
            e_ = b.vector.op("tensor_copy", QB[:, oc * TCH:(oc + 1) * TCH],
                             (pA if pb == 0 else pB_)[:, :], inc=b.vs)
            ev[pb] = (b.vs, e_)
        msv = b.vector.milestone(b.vs)
        b.tensor.wait(*msv)
        for oc in range(NFC):
            tiles, wv_ = v2S.load_block(w["v2_bf"], oc * 128)
            b.tensor.wait(*wv_)
            pb = oc % 2
            if ev[pb]:
                b.tensor.wait(*ev[pb])
            pp = pA if pb == 0 else pB_
            tv = emit_xw_to_psum(b, tiles,
                                 lambda k: QB[:, k * TCH:(k + 1) * TCH],
                                 pp[:, :], 2, inc=b.ts)
            v2S.consumed(tv)
            xpp = xp.begin(xdram[oc * 128:(oc + 1) * 128, t0:t0 + TCH])
            b.vector.wait(b.ts, tv)
            ev1 = b.vector.op("tensor_scalar", CON["tmp_f32"][:, 0:TCH], pp[:, :],
                              w["b2c"][:, oc:oc + 1], None, op0=ALU.add, inc=b.vs)
            ev[pb] = (b.vs, ev1)
            xp.wait_ld(xpp)
            xp.wait_out_free(xpp)
            vv = b.vector.op("tensor_tensor", xp.out_ap(xpp), xp.ld_ap(xpp),
                             CON["tmp_f32"][:, 0:TCH], op=ALU.add, inc=b.vs)
            xp.end(xpp, x1dram[oc * 128:(oc + 1) * 128, t0:t0 + TCH], vv)


def emit_l3_attn(b, x1dram, xodram, XN, CON, P, w, AR):
    T, NTC = b.T, b.NTC
    DI = 2048
    NDC = DI // 128
    xs_dram = w["xs_scr"]
    xsc_dram = w["xsc_scr"]
    res_dram = w["res_scr"]
    dt_dram = w["dt_scr"]
    u_dram = w["u_scr"]
    y_dram = w["y_scr"]
    weff_tw = w["weff_tw"]

    AR.reset()
    RNB = AR.take_bf(NFC * TCH)
    winS = WStream(b, "l3_win", NFC, 128, arena=AR)
    ab = P[2]
    pA, pB_ = P[4], P[5]
    ev = [None, None]
    sxr = b.dma_sem("l3_xr")
    for tc in range(NTC):
        t0 = tc * TCH
        b.scalar.wait(sxr, sxr.v)
        b.vector.wait(sxr, sxr.v)
        tv = emit_rms_stats(b, CON, P,
                            lambda fc: XN[:, fc * T + t0: fc * T + t0 + TCH],
                            NFC, 1e-5)
        b.vector.wait(b.ts, tv)
        for fc in range(NFC):
            tmp = CON["tmp_f32"]
            b.vector.op("tensor_tensor", tmp[:, 0:TCH],
                        XN[:, fc * T + t0: fc * T + t0 + TCH], ab[:, :],
                        op=ALU.mult)
            b.vector.op("tensor_scalar", RNB[:, fc * TCH:(fc + 1) * TCH],
                        tmp[:, 0:TCH], w["rmsc"][:, fc:fc + 1], None,
                        op0=ALU.mult)
        msv = b.vector.milestone(b.vs)
        b.tensor.wait(*msv)
        for oc in range(32):
            tiles, wv_ = winS.load_block(w["win_bf"], oc * 128)
            b.tensor.wait(*wv_)
            pb = oc % 2
            if ev[pb]:
                b.tensor.wait(*ev[pb])
            pp = pA if pb == 0 else pB_
            tv = emit_xw_to_psum(b, tiles,
                                 lambda k: RNB[:, k * TCH:(k + 1) * TCH],
                                 pp[:, :], NFC, inc=b.ts)
            winS.consumed(tv)
            b.vector.wait(b.ts, tv)
            b.vector.wait(sxr, sxr.v)
            e_ = b.vector.op("tensor_copy", CON["scratch_bf"][:, (pb) * TCH:(pb + 1) * TCH],
                             pp[:, :], inc=b.vs)
            ev[pb] = (b.vs, e_)
            b.sync.wait(b.vs, e_)
            dstd = xs_dram if oc < 16 else res_dram
            row0 = (oc % 16) * 128
            b.store(dstd[row0:row0 + 128, t0:t0 + TCH],
                    CON["scratch_bf"][:, pb * TCH:(pb + 1) * TCH], sxr)

    emit_barrier(b)

    AR.reset()
    CT = AR.take_bf(2 * (TCH + 4))
    CTB = AR.take_bf(2 * TCH)
    sconv = b.dma_sem("l3_cv")
    sconv_st = b.dma_sem("l3_cvs")
    conv_cons = [None, None]
    for dc in range(NDC):
        for tc in range(NTC):
            t0 = tc * TCH
            sl = (dc * NTC + tc) % 2
            ctile = CT[:, sl * (TCH + 4): sl * (TCH + 4) + TCH + 3]
            cbt = CTB[:, sl * TCH:(sl + 1) * TCH]
            if conv_cons[sl]:
                b.sync.wait(*conv_cons[sl])
            ld = b.dma_sem("l3_cvl%d" % sl)
            if tc == 0:
                b.load(ctile[:, 0:3], w["zeros3"][0:128, 0:3], ld)
                b.load(ctile[:, 3:3 + TCH],
                       xs_dram[dc * 128:(dc + 1) * 128, t0:t0 + TCH], ld)
            else:
                b.load(ctile[:, 0:3 + TCH],
                       xs_dram[dc * 128:(dc + 1) * 128, t0 - 3:t0 + TCH], ld)
            b.vector.wait(ld, ld.v)
            cv = ctile
            t1_ = CON["tmp_f32"][:, 0:TCH]
            b.vector.op("tensor_scalar", t1_, cv[:, 0:TCH],
                        w["cwc"][:, dc * 4:dc * 4 + 1], None, op0=ALU.mult)
            b.vector.op("scalar_tensor_tensor", t1_, cv[:, 1:1 + TCH],
                        w["cwc"][:, dc * 4 + 1:dc * 4 + 2], t1_,
                        op0=ALU.mult, op1=ALU.add)
            b.vector.op("scalar_tensor_tensor", t1_, cv[:, 2:2 + TCH],
                        w["cwc"][:, dc * 4 + 2:dc * 4 + 3], t1_,
                        op0=ALU.mult, op1=ALU.add)
            vv = b.vector.op("scalar_tensor_tensor", t1_, cv[:, 3:3 + TCH],
                            w["cwc"][:, dc * 4 + 3:dc * 4 + 4], t1_,
                            op0=ALU.mult, op1=ALU.add, inc=b.vs)
            conv_cons[sl] = (b.vs, vv)
            b.scalar.wait(b.vs, vv)
            b.scalar.wait(sconv_st, sconv_st.v)
            sv = b.scalar.op("activation", cbt, t1_, AF.Silu,
                             bias=w["cbc"][:, dc:dc + 1], inc=b.ss)
            b.sync.wait(b.ss, sv)
            b.store(xsc_dram[dc * 128:(dc + 1) * 128, t0:t0 + TCH], cbt, sconv_st)

    emit_barrier(b)

    AR.reset()
    wxS = WStream(b, "l3_wx", 16, 128, arena=AR)
    wdtS = WStream(b, "l3_wdt", 16, TCH, arena=AR)
    WEC = AR.take_bf(TCH)
    evw = [None, None]
    for ic in range(16):
        tiles, wv_ = wxS.load_block(w["wxT_bf"], ic * 128)
        b.tensor.wait(*wv_)
        for jc in range(4):
            rtiles, wv2 = wdtS.load_block(w["wdt_bf"], jc * TCH, ncols=TCH)
            b.tensor.wait(*wv2)
            pb = jc % 2
            if evw[pb]:
                b.tensor.wait(*evw[pb])
            pp = pA if pb == 0 else pB_
            tv = emit_xw_to_psum(b, tiles,
                                 lambda k: rtiles[k],
                                 pp[:, :], 16, inc=b.ts)
            wdtS.consumed(tv)
            b.vector.wait(b.ts, tv)
            e_ = b.vector.op("tensor_copy", WEC[:, 0:TCH], pp[:, :], inc=b.vs)
            evw[pb] = (b.vs, e_)
            b.sync.wait(b.vs, e_)
            b.store(weff_tw.dst(0, ic, jc * TCH, TCH), WEC[:, 0:TCH],
                    b.dma_sem("l3_wes"))
            b.vector.wait(b.dma_sem("l3_wes"), b.dma_sem("l3_wes").v)
        wxS.consumed(b.ts.v)

    emit_barrier(b)

    AR.reset()
    XSS = AR.take_bf(16 * TCH)
    weffS = WStream(b, "l3_we", 16, 128, arena=AR)
    wx2S = WStream(b, "l3_wx2", 16, 64, arena=AR)
    DTT = AR.take_bf(2 * TCH)
    BCB = b.ARB.take_bf(b.T)[0:64, :]
    sxss = b.dma_sem("l3_xss")
    sdt = b.dma_sem("l3_dts")
    evd = [None, None]
    tiles2, wv2 = wx2S.load_block(w["wx2_bf"], 0, ncols=64)
    for tc in range(NTC):
        t0 = tc * TCH
        b.sync.wait(b.ts, b.ts.v)
        for k in range(16):
            b.load(XSS[:, k * TCH:(k + 1) * TCH],
                   xsc_dram[k * 128:(k + 1) * 128, t0:t0 + TCH], sxss)
        b.tensor.wait(sxss, sxss.v)
        for oc in range(16):
            tiles, wv_ = weffS.load_block(weff_tw.view(), oc * 128)
            b.tensor.wait(*wv_)
            pb = oc % 2
            if evd[pb]:
                b.tensor.wait(*evd[pb])
            pp = pA if pb == 0 else pB_
            tv = emit_xw_to_psum(b, tiles,
                                 lambda k: XSS[:, k * TCH:(k + 1) * TCH],
                                 pp[:, :], 16, inc=b.ts)
            weffS.consumed(tv)
            b.scalar.wait(b.ts, tv)
            t1_ = CON["tmp_f32"][:, 0:TCH]
            t2_ = CON["tmp_f32"][:, TCH:2 * TCH]
            b.scalar.op("activation", t1_, pp[:, :], AF.Abs)
            b.scalar.op("activation", t1_, t1_, AF.Exp, scale=-1.0)
            b.scalar.op("activation", t1_, t1_, AF.Ln, bias=1.0)
            sv = b.scalar.op("activation", t2_, pp[:, :], AF.Relu, inc=b.ss)
            b.vector.wait(b.ss, sv)
            b.vector.op("tensor_tensor", t2_, t2_, t1_, op=ALU.add)
            b.vector.wait(sdt, sdt.v)
            e_ = b.vector.op("tensor_copy", DTT[:, 0:TCH], t2_, inc=b.vs)
            evd[pb] = (b.vs, e_)
            b.sync.wait(b.vs, e_)
            b.store(dt_dram[oc * 128:(oc + 1) * 128, t0:t0 + TCH], DTT[:, 0:TCH],
                    sdt)
            e2 = b.vector.op("tensor_tensor", DTT[:, TCH:2 * TCH], DTT[:, 0:TCH],
                             XSS[:, oc * TCH:(oc + 1) * TCH], op=ALU.mult,
                             inc=b.vs)
            evd[pb] = (b.vs, e2)
            b.sync.wait(b.vs, e2)
            b.store(u_dram[oc * 128:(oc + 1) * 128, t0:t0 + TCH],
                    DTT[:, TCH:2 * TCH], sdt)
        b.tensor.wait(*wv2)
        tvbc = emit_xw_to_psum(b, tiles2,
                               lambda k: XSS[:, k * TCH:(k + 1) * TCH],
                               P[6][0:64, :], 16, inc=b.ts)
        b.vector.wait(b.ts, tvbc)
        b.vector.op("tensor_copy", BCB[0:64, t0:t0 + TCH], P[6][0:64, :],
                    inc=b.vs)

    emit_barrier(b)

    AR.reset()
    _TH2 = b.T // max(1, b.T // 2048)
    BREP = AR.take_bf(b.T)
    CREP = AR.take_bf(b.T)
    APL = AR.take_f32(_TH2)
    DTRP = AR.take_bf(_TH2)
    URP = AR.take_bf(_TH2)
    BPL = AR.take_f32(_TH2)
    HPL = AR.take_f32(_TH2)
    PPL = AR.take_bf(_TH2)
    RESS = AR.take_bf(b.T)
    RSIL = AR.take_bf(b.T)
    SMAT_sb = AR.take_bf(16 * 128)
    QMAT_sb = AR.take_bf(16 * 128)
    RBS_sb = b.ARB.take_bf(128)[0:64, :]
    SCAR = b.sb("l3_scar", [128, 16], F32)
    YT = AR.take_bf(TCH)
    scm = b.dma_sem("ini_l3sc")
    b.load(RBS_sb[:], w["rbs_in"], scm)
    for i in range(16):
        b.load(SMAT_sb[:, i * 128:(i + 1) * 128],
               w["smat_in"][i * 128:(i + 1) * 128, :], scm)
        b.load(QMAT_sb[:, i * 128:(i + 1) * 128],
               w["qmat_in"][i * 128:(i + 1) * 128, :], scm)
    b.tensor.wait(scm, scm.v)
    ASTATE = b.ARB.take_f32(256)
    sas = b.dma_sem("ini_l3as")
    b.load(ASTATE[:], w["alog_in"], sas)
    b.scalar.wait(sas, sas.v)
    sv = b.scalar.op("activation", ASTATE[:], ASTATE[:], AF.Exp, inc=b.ss)
    b.vector.wait(b.ss, sv)
    vv = b.vector.op("tensor_scalar", ASTATE[:], ASTATE[:], -1.0, None,
                     op0=ALU.mult, inc=b.vs)
    b.scalar.wait(b.vs, vv)
    vrep = 0
    for (base, dst) in ((0, BREP), (32, CREP)):
        for tch in range(NTC):
            b.tensor.wait(b.vs, vrep)
            tv = b.tensor.op("matmul", P[6][0:128, :],
                             RBS_sb[base:base + 16, :],
                             BCB[base:base + 16, tch * TCH:(tch + 1) * TCH],
                             start=True, stop=True, inc=b.ts)
            b.vector.wait(b.ts, tv)
            vrep = b.vector.op("tensor_copy", dst[:, tch * TCH:(tch + 1) * TCH],
                               P[6][0:128, :], inc=b.vs)
    msv = b.vector.milestone(b.vs)
    b.tensor.wait(*msv)
    b.scalar.wait(*msv)

    NHALVES = max(1, b.T // 2048)
    TH2 = b.T // NHALVES
    NTH2 = TH2 // TCH
    sdtr = b.dma_sem("l3_dtr")
    sur = b.dma_sem("l3_ur")
    sres = b.dma_sem("l3_res")
    syst = b.dma_sem("l3_yst")
    vv = b.vector.op("memset", SCAR[:], 0.0, inc=b.vs)
    dt_h = w["dt_scr_h"]
    u_h = w["u_scr_h"]
    rep_cons_v = 0
    rep_cons_s = 0
    res_cons_s = 0
    rsil_cons_v = 0
    scan_prev = 0
    for dc in range(NDC):
        b.sync.wait(b.ss, res_cons_s)
        b.sync.wait(b.vs, rsil_cons_v)
        b.load(RESS[:, :], res_dram[dc * 128:(dc + 1) * 128, :], sres)
        b.scalar.wait(sres, sres.v)
        svr = b.scalar.op("activation", RSIL[:, :], RESS[:, :], AF.Silu, inc=b.ss)
        res_cons_s = svr
        for th in range(NHALVES):
            tb = th * TH2
            for scj in range(16):
                sc = dc * 16 + scj
                r0 = dc * 128 + scj * 8
                b.sync.wait(b.vs, rep_cons_v)
                b.sync.wait(b.ss, rep_cons_s)
                b.load(DTRP[:, :], bass.AP(dt_h, (r0 * b.T + tb),
                                           [[b.T, 8], [0, 16], [1, TH2]]), sdtr)
                b.load(URP[:, :], bass.AP(u_h, (r0 * b.T + tb),
                                          [[b.T, 8], [0, 16], [1, TH2]]), sur)
                b.scalar.wait(sdtr, sdtr.v)
                b.scalar.wait(b.vs, scan_prev)
                acol = ASTATE[:, sc:sc + 1]
                sva = b.scalar.op("activation", APL[:, 0:TH2], DTRP[:, :], AF.Exp,
                                  scale=acol, inc=b.ss)
                rep_cons_s = sva
                b.vector.wait(sur, sur.v)
                vvb = b.vector.op("tensor_tensor", BPL[:, :], URP[:, :],
                                  BREP[:, tb:tb + TH2], op=ALU.mult, inc=b.vs)
                rep_cons_v = vvb
                b.vector.wait(b.ss, sva)
                init = 0.0 if th == 0 else SCAR[:, scj:scj + 1]
                scan_prev = b.vector.op("tensor_tensor_scan", HPL[:, :],
                                        APL[:, 0:TH2], BPL[:, :], init,
                                        op0=ALU.mult, op1=ALU.add, inc=b.vs)
                if th < NHALVES - 1:
                    b.vector.op("tensor_copy", SCAR[:, scj:scj + 1],
                                HPL[:, TH2 - 1:TH2])
                msq = b.tensor.milestone(b.ts)
                b.vector.wait(*msq)
                vvp = b.vector.op("tensor_tensor", PPL[:, :], HPL[:, :],
                                  CREP[:, tb:tb + TH2], op=ALU.mult, inc=b.vs)
                b.tensor.wait(b.vs, vvp)
                for tch in range(NTH2):
                    b.tensor.op("matmul", P[tch][:, :],
                                QMAT_sb[:, scj * 128:(scj + 1) * 128],
                                PPL[:, tch * TCH:(tch + 1) * TCH],
                                start=(scj == 0), stop=(scj == 15),
                                inc=(b.ts if scj == 15 else None))
            tvy = b.ts.v
            b.vector.wait(b.ts, tvy)
            b.vector.wait(b.ss, svr)
            for tch in range(NTH2):
                b.vector.wait(syst, syst.v)
                vvy = b.vector.op("tensor_tensor", YT[:, 0:TCH], P[tch][:, :],
                                  RSIL[:, tb + tch * TCH: tb + (tch + 1) * TCH],
                                  op=ALU.mult, inc=b.vs)
                rsil_cons_v = vvy
                b.sync.wait(b.vs, vvy)
                b.store(y_dram[dc * 128:(dc + 1) * 128,
                               tb + tch * TCH: tb + (tch + 1) * TCH],
                        YT[:, 0:TCH], syst)
            b.vector.wait(syst, syst.v)
            ms2 = b.vector.milestone(b.vs)
            b.tensor.wait(*ms2)

    emit_barrier(b)

    AR.reset()
    YS = AR.take_bf(16 * TCH)
    woS = WStream(b, "l3_wo", 16, 128, arena=AR)
    xp = XPipe(b, CON, "a3")
    sys_ = b.dma_sem("l3_ys")
    evo = [None, None]
    for tc in range(NTC):
        t0 = tc * TCH
        b.sync.wait(b.ts, b.ts.v)
        for k in range(16):
            b.load(YS[:, k * TCH:(k + 1) * TCH],
                   y_dram[k * 128:(k + 1) * 128, t0:t0 + TCH], sys_)
        b.tensor.wait(sys_, sys_.v)
        for oc in range(NFC):
            tiles, wv_ = woS.load_block(w["wout_bf"], oc * 128)
            b.tensor.wait(*wv_)
            pb = oc % 2
            if evo[pb]:
                b.tensor.wait(*evo[pb])
            pp = pA if pb == 0 else pB_
            tv = emit_xw_to_psum(b, tiles,
                                 lambda k: YS[:, k * TCH:(k + 1) * TCH],
                                 pp[:, :], 16, inc=b.ts)
            woS.consumed(tv)
            xpp = xp.begin(x1dram[oc * 128:(oc + 1) * 128, t0:t0 + TCH])
            b.vector.wait(b.ts, tv)
            ev1 = b.vector.op("tensor_tensor", CON["tmp_f32"][:, 0:TCH],
                              XN[:, oc * T + t0: oc * T + t0 + TCH], pp[:, :],
                              op=ALU.add, inc=b.vs)
            evo[pb] = (b.vs, ev1)
            xp.wait_ld(xpp)
            xp.wait_out_free(xpp)
            vv = b.vector.op("tensor_tensor", xp.out_ap(xpp), xp.ld_ap(xpp),
                             CON["tmp_f32"][:, 0:TCH], op=ALU.add, inc=b.vs)
            xp.end(xpp, xodram[oc * 128:(oc + 1) * 128, t0:t0 + TCH], vv)


def emit_block3(b, CON, P, PT, xin, xout):
    T = b.T
    x1 = b.dram("x1scr3", [D, T], F32, kind="Internal")
    DI = 2048
    w = {
        "win_f": b.dram("L3_Win", [D, 4096], F32),
        "wxT_f": b.dram("L3_WxT", [2080, DI], F32),
        "wx2_f": b.dram("L3_Wx2p", [DI, 64], F32),
        "wdt_f": b.dram("L3_Wdt", [DI, DI], F32),
        "wout_f": b.dram("L3_Wout", [DI, D], F32),
        "u1_f": b.dram("L3_U1", [D, 256], F32),
        "v1_f": b.dram("L3_V1", [256, 4096], F32),
        "u2_f": b.dram("L3_U2", [4096, 256], F32),
        "v2_f": b.dram("L3_V2", [256, D], F32),
        "smat_in": b.dram("L3_smat", [16 * 128, 128], BF16),
        "qmat_in": b.dram("L3_qmat", [16 * 128, 128], BF16),
        "rbs_in": b.dram("L3_rbs", [64, 128], BF16),
        "alog_in": b.dram("L3_alog", [128, 256], F32),
        "zeros3": b.dram("L3_zeros3", [128, 4], BF16),
    }
    for (nm, K_, N_) in [("win_bf", D, 4096), ("wxT_bf", 2048, DI),
                         ("wdt_bf", DI, DI), ("wout_bf", DI, D),
                         ("u1_bf", D, 256), ("v1_bf", 256, 4096),
                         ("u2_bf", 4096, 256), ("v2_bf", 256, D)]:
        w[nm] = TW(b, "L3_" + nm, K_, N_).view()
    w["wx2_bf"] = b.dram("L3_wx2_bf", [DI, 64], BF16, kind="Internal")
    for (nm, shp, dt_) in [("xs_scr", [DI, T], BF16), ("xsc_scr", [DI, T], BF16),
                           ("res_scr", [DI, T], BF16), ("dt_scr", [DI, T], BF16),
                           ("u_scr", [DI, T], BF16), ("y_scr", [DI, T], BF16)]:
        h = b.nc.dram_tensor("L3_" + nm, shp, dt_, kind="Internal")
        w[nm] = h.ap()
        if nm in ("dt_scr", "u_scr"):
            w[nm + "_h"] = h
    w["weff_tw"] = TW(b, "L3_weff", DI, DI)
    ARB = get_arb(b)
    w["rmsc"] = load_colvec_arb(b, ARB, NFC, b.dram("L3_rmsc", [128, NFC], F32), "ini_l3cv")
    w["cwc"] = load_colvec_arb(b, ARB, 64, b.dram("L3_cwc", [128, 64], F32), "ini_l3cv2")
    w["cbc"] = load_colvec_arb(b, ARB, 16, b.dram("L3_cbc", [128, 16], F32), "ini_l3cv3")
    w["b1c"] = load_colvec_arb(b, ARB, 32, b.dram("L3_b1c", [128, 32], F32), "ini_l3cv4")
    w["b2c"] = load_colvec_arb(b, ARB, NFC, b.dram("L3_b2c", [128, NFC], F32), "ini_l3cv5")
    if getattr(b, "XN", None) is None:
        b.XN = b.sb("XN", [128, NFC * b.T], BF16)
    XN = b.XN
    AR = b.AR if getattr(b, "AR", None) is not None else Arena(b, "AR", 44032)
    b.AR = AR
    jobs = [
        (w["win_f"], None, w["win_bf"].tw, D, 4096),
        (w["wxT_f"], None, w["wxT_bf"].tw, 2048, DI),
        (w["wx2_f"], None, w["wx2_bf"], DI, 64),
        (w["wdt_f"], None, w["wdt_bf"].tw, DI, DI),
        (w["wout_f"], None, w["wout_bf"].tw, DI, D),
        (w["u1_f"], None, w["u1_bf"].tw, D, 256),
        (w["v1_f"], None, w["v1_bf"].tw, 256, 4096),
        (w["u2_f"], None, w["u2_bf"].tw, 4096, 256),
        (w["v2_f"], None, w["v2_bf"].tw, 256, D),
    ]
    emit_precast(b, CON, XN[:, 0:2048], jobs)
    emit_ln(b, xin, XN, CON, P)
    emit_barrier(b)
    emit_l3_ffn(b, xin, x1, XN, CON, P, w, AR)
    emit_barrier(b)
    emit_l3_attn(b, x1, xout, XN, CON, P, w, AR)


def build_block3(T=4096):
    nc = bass.Bass("TRN2", target_bir_lowering=False, debug=False)
    es = ExitStack()
    es.__enter__()
    b = Builder(nc, es, T)
    xin = b.dram("xin", [D, T], F32)
    xout = b.dram("xout", [D, T], F32, kind="ExternalOutput")
    P = [b.ps(f"P{i}", [128, TCH], F32) for i in range(7)]
    PT = b.ps("PT", [128, 1024], BF16)
    b.PT_BF = PT
    CON = alloc_common(b)
    emit_block3(b, CON, P, PT, xin, xout)
    b.finish()
    return nc, es


def host_prep_l3(inputs):
    C = np.ascontiguousarray
    import ml_dtypes
    DI = 2048

    def colvec(v):
        return C(np.asarray(v).reshape(-1, 128).T)
    wx = np.asarray(inputs["L3_attn_W_x"])
    wx2p = np.zeros((DI, 64), np.float32)
    wx2p[:, 0:16] = wx[:, 2048:2064]
    wx2p[:, 32:48] = wx[:, 2064:2080]
    smat = np.zeros((16 * 128, 128), dtype=ml_dtypes.bfloat16)
    qmat = np.zeros((16 * 128, 128), dtype=ml_dtypes.bfloat16)
    for j in range(16):
        r0 = 8 * j
        for i in range(128):
            smat[j * 128 + r0 + i // 16, i] = 1.0
            qmat[j * 128 + i, r0 + i // 16] = 1.0
    rbs = np.zeros((64, 128), dtype=ml_dtypes.bfloat16)
    for n in range(16):
        for i in range(128):
            if i % 16 == n:
                rbs[n, i] = 1.0
                rbs[32 + n, i] = 1.0
    alog = np.asarray(inputs["L3_attn_A_log"]).astype(np.float32)
    alog_state = C(alog.reshape(-1)[None, :].reshape(256, 128).T)
    return {
        "L3_Win": C(np.asarray(inputs["L3_attn_W_in"])),
        "L3_WxT": C(wx.T),
        "L3_Wx2p": wx2p,
        "L3_Wdt": C(np.asarray(inputs["L3_attn_W_dt"])),
        "L3_Wout": C(np.asarray(inputs["L3_attn_W_out"])),
        "L3_U1": C(np.asarray(inputs["L3_ffn_U1"])),
        "L3_V1": C(np.asarray(inputs["L3_ffn_V1"])),
        "L3_U2": C(np.asarray(inputs["L3_ffn_U2"])),
        "L3_V2": C(np.asarray(inputs["L3_ffn_V2"])),
        "L3_smat": smat,
        "L3_qmat": qmat,
        "L3_rbs": rbs,
        "L3_alog": alog_state,
        "L3_zeros3": np.zeros((128, 4), dtype=ml_dtypes.bfloat16),
        "L3_rmsc": colvec(inputs["L3_attn_rms_w"]),
        "L3_cwc": colvec(np.asarray(inputs["L3_attn_conv_w"]).reshape(DI, 4)
                         .reshape(16, 128, 4).transpose(1, 0, 2).reshape(128, 64)
                         ) if False else C(
            np.asarray(inputs["L3_attn_conv_w"]).reshape(DI, 4)
            .reshape(16, 128, 4).transpose(1, 0, 2).reshape(128, 64)),
        "L3_cbc": colvec(inputs["L3_attn_conv_b"]),
        "L3_b1c": colvec(inputs["L3_ffn_b1"]),
        "L3_b2c": colvec(inputs["L3_ffn_b2"]),
        "ident_in": np.eye(128, dtype=np.float32),
    }


def build_block1(T=4096):
    nc = bass.Bass("TRN2", target_bir_lowering=False, debug=False)
    es = ExitStack()
    es.__enter__()
    b = Builder(nc, es, T)
    xin = b.dram("xin", [D, T], F32)
    xout = b.dram("xout", [D, T], F32, kind="ExternalOutput")
    P = [b.ps(f"P{i}", [128, TCH], F32) for i in range(7)]
    PT = b.ps("PT", [128, 1024], BF16)
    b.PT_BF = PT
    CON = alloc_common(b)
    emit_block1(b, CON, P, PT, xin, xout)
    b.finish()
    return nc, es


def host_prep_l1(inputs):
    C = np.ascontiguousarray
    import ml_dtypes

    def colvec(v):
        return C(np.asarray(v).reshape(-1, 128).T)
    d = np.arange(D)
    th = 2.0 * np.pi * np.outer(d, d) / D
    fftc = np.cos(th).astype(ml_dtypes.bfloat16)
    ffts = (-np.sin(th)).astype(ml_dtypes.bfloat16)
    oh8 = np.zeros((8, 8 * 128), dtype=ml_dtypes.bfloat16)
    for e in range(8):
        oh8[e, e * 128:(e + 1) * 128] = 1.0
    return {
        "L1_wqT": C(np.asarray(inputs["L1_attn_Wq"]).T),
        "L1_wkT": C(np.asarray(inputs["L1_attn_Wk"]).T),
        "L1_wvT": C(np.asarray(inputs["L1_attn_Wv"]).T),
        "L1_bqc": colvec(inputs["L1_attn_bq"]),
        "L1_bkc": colvec(inputs["L1_attn_bk"]),
        "L1_bvc": colvec(inputs["L1_attn_bv"]),
        "L1_fftc": fftc,
        "L1_ffts": ffts,
        "L1_wg": C(np.asarray(inputs["L1_ffn_w_gate"]).reshape(8 * D, 4096)),
        "L1_wu": C(np.asarray(inputs["L1_ffn_w_up"]).reshape(8 * D, 4096)),
        "L1_wd": C(np.asarray(inputs["L1_ffn_w_down"]).reshape(8 * 4096, D)),
        "L1_rwT": C(np.asarray(inputs["L1_ffn_router_w"]).T),
        "oh8_in": oh8,
        "ident_in": np.eye(128, dtype=np.float32),
    }


def build_full(T=4096):
    nc = bass.Bass("TRN2", target_bir_lowering=False, debug=False)
    es = ExitStack()
    es.__enter__()
    b = Builder(nc, es, T)
    xin = b.dram("xin", [D, T], F32)
    xout = b.dram("xout", [D, T], F32, kind="ExternalOutput")
    xc1 = b.dram("xc1", [D, T], F32, kind="Internal")
    xc2 = b.dram("xc2", [D, T], F32, kind="Internal")
    xc3 = b.dram("xc3", [D, T], F32, kind="Internal")
    P = [b.ps(f"P{i}", [128, TCH], F32) for i in range(7)]
    PT = b.ps("PT", [128, 1024], BF16)
    b.PT_BF = PT
    CON = alloc_common(b)
    b.AR = Arena(b, "AR", 44032)
    emit_block0(b, CON, P, PT, xin, xc1)
    emit_barrier(b)
    emit_block1(b, CON, P, PT, xc1, xc2)
    emit_barrier(b)
    emit_block2(b, CON, P, PT, xc2, xc3)
    emit_barrier(b)
    emit_block3(b, CON, P, PT, xc3, xout)
    b.finish()
    n_sems = len(b.sems)
    assert n_sems <= 100, f"too many semaphores: {n_sems}"
    return nc, es


def host_prep_all(inputs):
    m = {}
    m.update(host_prep_l0(inputs))
    m.update(host_prep_l1(inputs))
    m.update(host_prep_l2(inputs))
    m.update(host_prep_l3(inputs))
    return m


FUSED = True


def kernel(**inputs):
    T = 4096
    x = np.asarray(inputs["L0_attn_x"])
    if FUSED:
        nc, es = build_full(T)
        wmap = host_prep_all(inputs)
        in_maps = [dict(wmap, xin=np.ascontiguousarray(x[c].T)) for c in range(8)]
        res = run_bass_kernel_spmd(nc, in_maps, core_ids=list(range(8)))
        es.__exit__(None, None, None)
        return np.stack([np.ascontiguousarray(r["xout"].T) for r in res.results])
    cur = [np.ascontiguousarray(x[c].T) for c in range(8)]
    blocks = [(build_block0, host_prep_l0), (build_block1, host_prep_l1),
              (build_block2, host_prep_l2), (build_block3, host_prep_l3)]
    for build, prep in blocks:
        nc, es = build(T)
        wmap = prep(inputs)
        in_maps = [dict(wmap, xin=cur[c]) for c in range(8)]
        res = run_bass_kernel_spmd(nc, in_maps, core_ids=list(range(8)))
        es.__exit__(None, None, None)
        cur = [np.ascontiguousarray(r["xout"]) for r in res.results]
    return np.stack([np.ascontiguousarray(c.T) for c in cur])
